# version 26
# baseline (speedup 1.0000x reference)
import numpy as np
from contextlib import ExitStack

import concourse.bass as bass
import concourse.mybir as mybir
from concourse.ap import AP
from concourse.bass_utils import run_bass_kernel_spmd

F32 = mybir.dt.float32
BF16 = mybir.dt.bfloat16
AF = mybir.ActivationFunctionType
ALU = mybir.AluOpType
AX = mybir.AxisListType

T = 8192
C = 1024
H = 16
QEXT = 128
QOWN = 2048
QR = QEXT + QOWN
NSC = 64
QBS = [512, 512, 512, 512, 128]
QB0 = [0, 512, 1024, 1536, 2048]
ETW = 10368
BUFW = 10240
EPS = 1e-5
DEBUG_TAPS = 0


class Builder:
    def __init__(self, nc, ctx):
        self.nc = nc
        self.ctx = ctx
        self.prog = {k: [] for k in ("sync", "tensor", "scalar", "vector",
                                     "gpsimd")}
        self.sems = {}
        self.sv = {}

    def sem(self, *names):
        for n in names:
            self.sems[n] = self.ctx.enter_context(self.nc.semaphore(n))
            self.sv[n] = 0

    def w(self, eng, sname, val=None):
        v = self.sv[sname] if val is None else val
        if v <= 0:
            return
        self.prog[eng].append(
            lambda e, s=self.sems[sname], v=v: e.wait_ge(s, v))

    def op(self, eng, fn, *incs):
        hs = []
        for (s, n) in incs:
            self.sv[s] += n
            hs.append((self.sems[s], n))

        def run(e, fn=fn, hs=hs):
            ins = fn(e)
            for h, n in hs:
                ins = ins.then_inc(h, n)
        self.prog[eng].append(run)

    def vfence(self):
        f = self.fence_sb
        self.op("vector", lambda e, f=f: e.tensor_copy(
            f[0:1, 0:1], f[0:1, 1:2]), ("s_ve", 1))

    def sfence(self):
        f = self.fence_sb
        self.op("scalar", lambda e, f=f: e.activation(
            f[0:1, 2:3], f[0:1, 3:4], mybir.ActivationFunctionType.Copy),
            ("s_ae", 1))

    def dma(self, eng, sname, out, in_, transpose=False, slow=False):
        if slow:
            self.op(eng, lambda e, o=out, i=in_:
                    e.dma_start(out=o, in_=i,
                                allow_slow_non_contiguous=True), (sname, 16))
        else:
            self.op(eng, lambda e, o=out, i=in_, t=transpose:
                    e.dma_start(out=o, in_=i, transpose=t), (sname, 16))


def build_nc():
    nc = bass.Bass("TRN2", target_bir_lowering=False)
    din = {}

    def mk_in(name, shape, dt=F32):
        h = nc.dram_tensor(name, list(shape), dt, kind="ExternalInput")
        din[name] = h
        return h

    for name, shape in (
            ("xf", [T, C]), ("xo", [QR, C]), ("wqkv", [C, 3 * C]),
            ("bqkv", [1, 3 * C]), ("wout", [C, C]), ("bout", [1, C]),
            ("wr", [C, C]), ("wk", [C, C]), ("wv", [C, C]),
            ("mur", [1, C]), ("muk", [1, C]), ("w1r", [1, 32]),
            ("b1r", [1, 32]), ("w2r", [32, H]), ("b2r", [1, H]),
            ("gsl", [32, ETW]), ("gt4", [8, H]), ("mask0", [128, 1])):
        mk_in(name, shape)

    y = nc.dram_tensor("y", [QOWN, C], F32, kind="ExternalOutput").ap()

    xbf_scr = nc.dram_tensor("xbf_scr", [T, C], BF16)
    xhi_scr = nc.dram_tensor("xhi_scr", [QR, C], BF16)
    xlo_scr = nc.dram_tensor("xlo_scr", [QR, C], BF16)
    k_scr = nc.dram_tensor("k_scr", [C, T], BF16)
    v_scr = nc.dram_tensor("v_scr", [T, C], BF16)
    etab_scr = nc.dram_tensor("etab_scr", [H, ETW], BF16)
    rep_scr = nc.dram_tensor("rep_scr", [128, H * BUFW], BF16)
    o_scr = nc.dram_tensor("o_scr", [C, QR], BF16)
    q_scr = nc.dram_tensor("q_scr", [C, QR], BF16)
    wT_scr = [nc.dram_tensor(f"w{n}T_scr", [C, C], BF16) for n in "rkv"]
    x2h_scr = nc.dram_tensor("x2h_scr", [C, QOWN], BF16)
    x2l_scr = nc.dram_tensor("x2l_scr", [C, QOWN], BF16)
    vec_scr = nc.dram_tensor("vec_scr", [4, 16], F32)

    with ExitStack() as ctx:
        E = ctx.enter_context
        B = Builder(nc, ctx)
        B.sem("s_ld", "s_tp", "s_sc", "s_pe", "s_ve", "s_ae", "s_g",
              "s_smm", "s_exp", "s_mul", "s_pv", "s_nrm", "s_bc",
              "s_g96", "s_ost", "s_y", "s_rep", "s_xa", "s_xb", "s_xo")
        w, op, dma = B.w, B.op, B.dma
        sv = B.sv

        onesl = E(nc.sbuf_tensor("onesl", [128, 1], BF16))
        B.fence_sb = E(nc.sbuf_tensor("fence_sb", [1, 4], F32))
        B.op("vector", lambda e: e.memset(B.fence_sb[:, :], 0.0), ("s_ve", 1))
        onesP = E(nc.sbuf_tensor("onesP", [128, 128], F32))
        onesPb = E(nc.sbuf_tensor("onesPb", [1, 128], BF16))
        onesc = E(nc.sbuf_tensor("onesc", [128, 1], F32))
        bias24 = E(nc.sbuf_tensor("bias24", [128, 24], F32))
        bo8 = E(nc.sbuf_tensor("bo8", [128, 8], F32))
        mur8 = E(nc.sbuf_tensor("mur8", [128, 8], F32))
        muk8 = E(nc.sbuf_tensor("muk8", [128, 8], F32))
        mur8i = E(nc.sbuf_tensor("mur8i", [128, 8], F32))
        muk8i = E(nc.sbuf_tensor("muk8i", [128, 8], F32))
        mask_sb = E(nc.sbuf_tensor("mask_sb", [128, 1], F32))
        bvrow = E(nc.sbuf_tensor("bvrow", [1, C], BF16))
        w1c = E(nc.sbuf_tensor("w1c", [32, 1], F32))
        b1c = E(nc.sbuf_tensor("b1c", [32, 1], F32))
        w2_sb = E(nc.sbuf_tensor("w2_sb", [32, H], F32))
        b2c = E(nc.sbuf_tensor("b2c", [16, 1], F32))
        gt4_sb = E(nc.sbuf_tensor("gt4_sb", [8, H], F32))
        qmax2 = E(nc.sbuf_tensor("qmax2", [1, H], F32))
        kmax2 = E(nc.sbuf_tensor("kmax2", [1, 1], F32))
        t11 = E(nc.sbuf_tensor("t11", [1, 16], F32))
        mh_row = E(nc.sbuf_tensor("mh_row", [1, H], F32))
        mhg_row = E(nc.sbuf_tensor("mhg_row", [1, 8], F32))
        nmhg_row = E(nc.sbuf_tensor("nmhg_row", [1, 8], F32))
        mh_col = E(nc.sbuf_tensor("mh_col", [16, 1], F32))
        mhg_col4 = E(nc.sbuf_tensor("mhg_col4", [8, 1], F32))
        dcol = E(nc.sbuf_tensor("dcol", [16, 1], F32))
        corr_col = E(nc.sbuf_tensor("corr_col", [16, 1], F32))
        corr_row = E(nc.sbuf_tensor("corr_row", [1, 16], F32))
        corrE = E(nc.sbuf_tensor("corrE", [128, 16], F32))
        mbg = E(nc.sbuf_tensor("mbg", [128, 8], F32))
        rl96 = E(nc.sbuf_tensor("rl96", [1, 512], F32))
        sqt = E(nc.sbuf_tensor("sqt", [128, 512], BF16))
        wtmp = E(nc.sbuf_tensor("wtmp", [128, 1024], BF16))

        psA = E(nc.psum_tensor("psA", [128, 2048], F32))
        psB = E(nc.psum_tensor("psB", [128, 512], F32))
        psC = E(nc.psum_tensor("psC", [128, 512], F32))
        psD = E(nc.psum_tensor("psD", [128, 512], F32))
        psE = E(nc.psum_tensor("psE", [128, 512], F32))

        for t_, v_ in ((onesl, 1.0), (onesP, 1.0), (onesPb, 1.0),
                       (onesc, 1.0), (qmax2, 0.0), (kmax2, 0.0)):
            op("vector", lambda e, t=t_, v=v_: e.memset(t[:, :], v), ("s_ve", 1))

        bv32 = E(nc.sbuf_tensor("bv32", [1, C], F32))
        dma("sync", "s_tp", bias24[:, :], AP(din["bqkv"], 0, [[1, 128], [128, 24]]), slow=True)
        dma("sync", "s_tp", bo8[:, :], AP(din["bout"], 0, [[1, 128], [128, 8]]), slow=True)
        dma("sync", "s_tp", mur8[:, :], AP(din["mur"], 0, [[1, 128], [128, 8]]), slow=True)
        dma("sync", "s_tp", muk8[:, :], AP(din["muk"], 0, [[1, 128], [128, 8]]), slow=True)
        dma("sync", "s_tp", mask_sb[:, :], din["mask0"].ap())
        dma("sync", "s_tp", bv32[:, :], din["bqkv"].ap()[0:1, 2048:3072])
        dma("sync", "s_tp", w1c[:, :], AP(din["w1r"], 0, [[1, 32], [1, 1]]))
        dma("sync", "s_tp", b1c[:, :], AP(din["b1r"], 0, [[1, 32], [1, 1]]))
        dma("sync", "s_tp", w2_sb[:, :], din["w2r"].ap())
        dma("sync", "s_tp", b2c[:, :], AP(din["b2r"], 0, [[1, 16], [1, 1]]))
        dma("sync", "s_tp", gt4_sb[:, :], din["gt4"].ap())
        ld_small = sv["s_tp"]

        w("vector", "s_tp", ld_small)
        op("vector", lambda e: e.tensor_scalar(bias24[:, 0:8], bias24[:, 0:8],
           0.125, None, ALU.mult), ("s_ve", 1))
        op("vector", lambda e: e.tensor_scalar(mur8i[:, :], mur8[:, :],
           -1.0, 1.0, ALU.mult, ALU.add), ("s_ve", 1))
        op("vector", lambda e: e.tensor_scalar(muk8i[:, :], muk8[:, :],
           -1.0, 1.0, ALU.mult, ALU.add), ("s_ve", 1))
        op("vector", lambda e: e.tensor_copy(bvrow[:, :], bv32[:, :]),
           ("s_ve", 1))

        with ExitStack() as ctx1:
            E1 = ctx1.enter_context
            wq_sb = E1(nc.sbuf_tensor("wq_sb", [128, 8 * 3072], BF16))

            for cc in range(8):
                dma("gpsimd", "s_ld", wq_sb[:, 3072 * cc:3072 * (cc + 1)],
                    din["wqkv"].ap()[128 * cc:128 * (cc + 1), :])
            ld_wq = sv["s_ld"]
            w("vector", "s_ld", ld_wq)
            for cc in range(8):
                op("vector", lambda e, cc=cc: e.tensor_scalar(
                    wq_sb[:, 3072 * cc:3072 * cc + 1024],
                    wq_sb[:, 3072 * cc:3072 * cc + 1024], 0.125, None,
                    ALU.mult), ("s_ve", 1))
            ve_wq = sv["s_ve"]

            xstg = [E1(nc.sbuf_tensor(f"xstg{i}", [128, 1024], BF16))
                    for i in range(4)]

            with ExitStack() as ctx1a:
                E1a = ctx1a.enter_context
                g_sb = E1a(nc.sbuf_tensor("g_sb", [32, ETW], F32))
                h_sb = E1a(nc.sbuf_tensor("h_sb", [32, ETW], F32))
                etab_sb = E1a(nc.sbuf_tensor("etab_sb", [16, ETW], BF16))
                dma("sync", "s_tp", g_sb[:, :], din["gsl"].ap())
                ld_g = sv["s_tp"]
                w("vector", "s_tp", ld_g)
                op("vector", lambda e, g_sb=g_sb, h_sb=h_sb: e.tensor_scalar(
                    h_sb[:, :], g_sb[:, :], w1c[:, 0:1], None, ALU.mult),
                    ("s_ve", 1))
                w("scalar", "s_ve", sv["s_ve"])
                op("scalar", lambda e, h_sb=h_sb: e.activation(
                    h_sb[:, :], h_sb[:, :], AF.Relu, bias=b1c[:, 0:1],
                    scale=1.0), ("s_ae", 1))
                w("tensor", "s_ae", sv["s_ae"])
                w("tensor", "s_tp", ld_small)
                nch = (ETW + 511) // 512
                for chi in range(nch):
                    c0 = 512 * chi
                    cw = min(512, ETW - c0)
                    bank = psD if chi % 2 == 0 else psE
                    if chi >= 2:
                        w("tensor", "s_ae", sv["s_ae"] - 1)
                    op("tensor", lambda e, c0=c0, cw=cw, h_sb=h_sb, bank=bank:
                       e.matmul(bank[0:16, 0:cw], w2_sb[:, :],
                                h_sb[:, c0:c0 + cw], start=True, stop=True),
                       ("s_pe", 1))
                    w("scalar", "s_pe", sv["s_pe"])
                    op("scalar", lambda e, c0=c0, cw=cw, g_sb=g_sb, bank=bank:
                       e.activation(g_sb[0:16, c0:c0 + cw], bank[0:16, 0:cw],
                                    AF.Exp, bias=b2c[:, 0:1], scale=1.0),
                       ("s_ae", 1))
                w("vector", "s_ae", sv["s_ae"])
                op("vector", lambda e, g_sb=g_sb, etab_sb=etab_sb:
                   e.tensor_copy(etab_sb[:, :], g_sb[0:16, :]), ("s_ve", 1))
                w("vector", "s_ve")
                op("vector", lambda e, etab_sb=etab_sb: e.tensor_copy(
                    etab_sb[0:1, 0:1], etab_sb[0:1, 0:1]), ("s_ve", 1))
                ve_etab = sv["s_ve"]
                if DEBUG_TAPS >= 1:
                    tap_h = nc.dram_tensor("tap_h", [32, ETW], F32,
                                           kind="ExternalOutput").ap()
                if DEBUG_TAPS in (2, 3):
                    w("sync", "s_ae", sv["s_ae"])
                    dma("sync", "s_y", tap_h, h_sb[:, :])

            xo32 = E1(nc.sbuf_tensor("xo32", [128, 1024], F32))
            xlo32 = E1(nc.sbuf_tensor("xlo32", [128, 1024], F32))
            xhib = [E1(nc.sbuf_tensor(f"xhib{i}", [128, 1024], BF16))
                    for i in range(2)]
            xlob = [E1(nc.sbuf_tensor(f"xlob{i}", [128, 1024], BF16))
                    for i in range(2)]
            xT = [[E1(nc.sbuf_tensor(f"xT{b_}_{cc}", [128, 2048], BF16))
                   for cc in range(8)] for b_ in range(2)]
            ktile = [E1(nc.sbuf_tensor(f"ktile{i}", [128, 512], BF16))
                     for i in range(2)]
            vtile = [E1(nc.sbuf_tensor(f"vtile{i}", [128, 1024], BF16))
                     for i in range(2)]
            qstg = [E1(nc.sbuf_tensor(f"qstg{i}", [128, 512], BF16))
                    for i in range(2)]

            xb_st = []
            for t_ in range(64):
                if t_ >= 4:
                    w("gpsimd", "s_xb", xb_st[t_ - 4])
                dma("gpsimd", "s_xa", xstg[t_ % 4][:, :],
                    din["xf"].ap()[128 * t_:128 * (t_ + 1), :])
                w("sync", "s_xa", sv["s_xa"])
                dma("sync", "s_xb",
                    xbf_scr.ap()[128 * t_:128 * (t_ + 1), :],
                    xstg[t_ % 4][:, :])
                xb_st.append(sv["s_xb"])
            xb_mark = {tg: ("s_xb", xb_st[16 * (tg + 1) - 1])
                       for tg in range(4)}

            w("scalar", "s_ve", ve_etab)
            dma("scalar", "s_sc", etab_scr.ap(), etab_sb[:, :])
            sc_etab = sv["s_sc"]
            w("scalar", "s_sc", sc_etab)
            for p_ in range(128):
                dma("scalar", "s_rep", rep_scr.ap()[p_:p_ + 1, :],
                    AP(etab_scr, 127 - p_, [[ETW, H], [1, BUFW]]))
            rep_done = sv["s_rep"]

            kevK = []
            kdma = []
            vevV = []
            vdma = []
            ki = 0
            vi = 0
            pe_end_tg = []

            def emit_xt(tg):
                tb = tg % 2
                w("sync", xb_mark[tg][0], xb_mark[tg][1])
                if tg < 2:
                    w("sync", "s_sc", sc_etab)
                if tg >= 2:
                    w("sync", "s_pe", pe_end_tg[tg - 2])
                for cc in range(8):
                    dma("sync", "s_tp", xT[tb][cc][:, :],
                        xbf_scr.ap()[2048 * tg:2048 * (tg + 1),
                                     128 * cc:128 * (cc + 1)], transpose=True)
                return sv["s_tp"]

            tp_list = [emit_xt(0), emit_xt(1)]

            xo_done = []
            w("sync", "s_sc", sc_etab)
            for t_ in range(17):
                b_ = t_ % 2
                if t_ >= 1:
                    w("sync", "s_ve", xo_done[t_ - 1][0])
                dma("sync", "s_tp", xo32[:, :],
                    din["xo"].ap()[128 * t_:128 * (t_ + 1), :])
                tp_t = sv["s_tp"]
                if t_ == 0:
                    w("gpsimd", "s_sc", sc_etab)
                if t_ >= 1:
                    w("gpsimd", "s_ve", xo_done[t_ - 1][0])
                if t_ >= 2:
                    w("gpsimd", "s_xo", xo_done[t_ - 2][1])
                dma("gpsimd", "s_xa", xhib[b_][:, :],
                    din["xo"].ap()[128 * t_:128 * (t_ + 1), :])
                ld_t = sv["s_xa"]
                w("vector", "s_tp", tp_t)
                w("vector", "s_xa", ld_t)
                if t_ >= 2:
                    w("vector", "s_xo", xo_done[t_ - 2][1])
                op("vector", lambda e, b_=b_: e.tensor_sub(
                    xlo32[:, :], xo32[:, :], xhib[b_][:, :]), ("s_ve", 1))
                w("vector", "s_ve")
                op("vector", lambda e, b_=b_: e.tensor_copy(
                    xlob[b_][:, :], xlo32[:, :]), ("s_ve", 1))
                B.vfence()
                ve_t = sv["s_ve"]
                w("sync", "s_xa", ld_t)
                dma("sync", "s_xo",
                    xhi_scr.ap()[128 * t_:128 * (t_ + 1), :], xhib[b_][:, :])
                w("sync", "s_ve", ve_t)
                dma("sync", "s_xo",
                    xlo_scr.ap()[128 * t_:128 * (t_ + 1), :], xlob[b_][:, :])
                xo_done.append((ve_t, sv["s_xo"]))
            xo_all = sv["s_xo"]

            for tg in range(4):
                tb = tg % 2
                tp_xt = tp_list[tg]
                w("tensor", "s_tp", tp_xt)
                if tg == 0:
                    w("tensor", "s_ve", ve_wq)
                    w("tensor", "s_ld", ld_wq)
                if vevV:
                    w("tensor", "s_ve", vevV[-1])
                for nb in range(4):
                    for mt in range(8):
                        hb = (ki % 2)
                        bank = psA[:, 1024 + 512 * hb:1536 + 512 * hb]
                        if ki >= 2:
                            w("tensor", "s_ve", kevK[ki - 2])
                        for cc in range(8):
                            op("tensor", lambda e, bank=bank, cc=cc, mt=mt,
                               nb=nb, tb=tb: e.matmul(
                                   bank,
                                   wq_sb[:, 3072 * cc + 1024 + 128 * mt:
                                         3072 * cc + 1024 + 128 * (mt + 1)],
                                   xT[tb][cc][:, 512 * nb:512 * (nb + 1)],
                                   start=(cc == 0), stop=(cc == 7)),
                               *([("s_pe", 1)] if cc == 7 else []))
                        pe_k = sv["s_pe"]
                        kb = ktile[ki % 2]
                        w("vector", "s_pe", pe_k)
                        if ki >= 2:
                            w("vector", "s_sc", kdma[ki - 2])
                        op("vector", lambda e, kb=kb, bank=bank, mt=mt:
                           e.tensor_scalar(kb[:, :], bank,
                                           bias24[:, 8 + mt:9 + mt],
                                           None, ALU.add), ("s_ve", 1))
                        if ki >= 1:
                            w("vector", "s_pe", sv["s_pe"])
                        w("vector", "s_ve")
                        op("vector", lambda e, kb=kb: e.tensor_mul(
                            sqt[:, :], kb[:, :], kb[:, :]), ("s_ve", 1))
                        kevK.append(sv["s_ve"])
                        B.vfence()
                        w("tensor", "s_ve", sv["s_ve"])
                        op("tensor", lambda e, mt=mt: e.matmul(
                            psD[0:1, 0:512], onesl[:, 0:1], sqt[:, :],
                            start=(mt == 0), stop=(mt == 7)), ("s_pe", 1))
                        w("gpsimd", "s_ve")
                        dma("gpsimd", "s_sc",
                            k_scr.ap()[128 * mt:128 * (mt + 1),
                                       2048 * tg + 512 * nb:
                                       2048 * tg + 512 * (nb + 1)], kb[:, :])
                        kdma.append(sv["s_sc"])
                        ki += 1
                    pe_kn = sv["s_pe"]
                    w("vector", "s_pe", pe_kn)
                    op("vector", lambda e: e.reduce_max(
                        t11[0:1, 0:1], psD[0:1, 0:512], axis=AX.X), ("s_ve", 1))
                    w("vector", "s_ve")
                    op("vector", lambda e: e.tensor_max(
                        kmax2[:, :], kmax2[:, :], t11[0:1, 0:1]), ("s_ve", 1))
                    w("tensor", "s_ve", sv["s_ve"])
                for st in range(16):
                    vb = vtile[vi % 2]
                    vbase = 1024 * (vi % 2)
                    if vi == 1:
                        w("tensor", "s_ve", kevK[-1])
                    if vi >= 2:
                        w("tensor", "s_ve", vevV[vi - 2])
                    for half in range(2):
                        op("tensor", lambda e, half=half, vbase=vbase:
                           e.matmul(
                               psA[:, vbase + 512 * half:
                                   vbase + 512 * (half + 1)],
                               onesPb[0:1, 0:128],
                               bvrow[0:1, 512 * half:512 * (half + 1)],
                               start=True, stop=True))
                    for cc in range(8):
                        for half in range(2):
                            op("tensor", lambda e, cc=cc, st=st, half=half,
                               tb=tb, vbase=vbase: e.matmul(
                                   psA[:, vbase + 512 * half:
                                       vbase + 512 * (half + 1)],
                                   xT[tb][cc][:, 128 * st:128 * (st + 1)],
                                   wq_sb[:, 3072 * cc + 2048 + 512 * half:
                                         3072 * cc + 2048 + 512 * (half + 1)],
                                   start=False, stop=(cc == 7)),
                               *([("s_pe", 1)] if cc == 7 and half == 1
                                 else []))
                    pe_v = sv["s_pe"]
                    w("vector", "s_pe", pe_v)
                    if vi >= 2:
                        w("vector", "s_sc", vdma[vi - 2])
                    op("vector", lambda e, vb=vb, vbase=vbase: e.tensor_copy(
                        vb[:, :], psA[:, vbase:vbase + 1024]), ("s_ve", 1))
                    vevV.append(sv["s_ve"])
                    B.vfence()
                    w("gpsimd", "s_ve")
                    dma("gpsimd", "s_sc",
                        v_scr.ap()[2048 * tg + 128 * st:
                                   2048 * tg + 128 * (st + 1), :], vb[:, :])
                    vdma.append(sv["s_sc"])
                    vi += 1
                pe_end_tg.append(sv["s_pe"])
                if tg + 2 < 4:
                    tp_list.append(emit_xt(tg + 2))
            sc_kv = sv["s_sc"]

            xoT = [E1(nc.sbuf_tensor(f"xoT{cc}", [128, QR], BF16))
                   for cc in range(8)]
            w("sync", "s_xo", xo_all)
            w("sync", "s_pe", sv["s_pe"])
            for cc in range(8):
                dma("sync", "s_tp", xoT[cc][:, :],
                    xhi_scr.ap()[:, 128 * cc:128 * (cc + 1)], transpose=True)
            tp_xo = sv["s_tp"]
            w("tensor", "s_tp", tp_xo)
            qev = []
            qdma = []
            qi = 0
            for mt in range(8):
                for nb in range(5):
                    q0, qw = QB0[nb], QBS[nb]
                    hb = qi % 2
                    qs = qstg[qi % 2]
                    if qi >= 2:
                        w("tensor", "s_ve", qev[qi - 2])
                    for cc in range(8):
                        op("tensor", lambda e, hb=hb, cc=cc, mt=mt, q0=q0,
                           qw=qw: e.matmul(
                               psA[:, 1024 + 512 * hb:1024 + 512 * hb + qw],
                               wq_sb[:, 3072 * cc + 128 * mt:
                                     3072 * cc + 128 * (mt + 1)],
                               xoT[cc][:, q0:q0 + qw],
                               start=(cc == 0), stop=(cc == 7)),
                           *([("s_pe", 1)] if cc == 7 else []))
                    pe_q = sv["s_pe"]
                    w("vector", "s_pe", pe_q)
                    if qi >= 2:
                        w("vector", "s_sc", qdma[qi - 2])
                    op("vector", lambda e, hb=hb, qs=qs, mt=mt, qw=qw:
                       e.tensor_scalar(
                           qs[:, 0:qw],
                           psA[:, 1024 + 512 * hb:1024 + 512 * hb + qw],
                           bias24[:, mt:mt + 1], None, ALU.add), ("s_ve", 1))
                    B.vfence()
                    ve_qe = sv["s_ve"]
                    w("gpsimd", "s_ve", ve_qe)
                    dma("gpsimd", "s_sc",
                        q_scr.ap()[128 * mt:128 * (mt + 1), q0:q0 + qw],
                        qs[:, 0:qw])
                    qdma.append(sv["s_sc"])
                    w("vector", "s_pe", sv["s_pe"])
                    w("vector", "s_ve")
                    op("vector", lambda e, qs=qs, qw=qw: e.tensor_mul(
                        sqt[:, 0:qw], qs[:, 0:qw], qs[:, 0:qw]), ("s_ve", 1))
                    qev.append(sv["s_ve"])
                    w("tensor", "s_ve", sv["s_ve"])
                    op("tensor", lambda e, qw=qw: e.matmul(
                        psD[0:1, 0:qw], onesl[0:64, 0:1], sqt[0:64, 0:qw],
                        start=True, stop=True), ("s_pe", 1))
                    op("tensor", lambda e, qw=qw: e.matmul(
                        psE[0:1, 0:qw], onesl[64:128, 0:1], sqt[64:128, 0:qw],
                        start=True, stop=True), ("s_pe", 1))
                    pe_qn = sv["s_pe"]
                    w("vector", "s_pe", pe_qn)
                    op("vector", lambda e, qw=qw: e.reduce_max(
                        t11[0:1, 0:1], psD[0:1, 0:qw], axis=AX.X), ("s_ve", 1))
                    op("vector", lambda e, qw=qw: e.reduce_max(
                        t11[0:1, 1:2], psE[0:1, 0:qw], axis=AX.X), ("s_ve", 1))
                    w("vector", "s_ve")
                    op("vector", lambda e, mt=mt: e.tensor_max(
                        qmax2[0:1, 2 * mt:2 * mt + 1],
                        qmax2[0:1, 2 * mt:2 * mt + 1], t11[0:1, 0:1]),
                        ("s_ve", 1))
                    op("vector", lambda e, mt=mt: e.tensor_max(
                        qmax2[0:1, 2 * mt + 1:2 * mt + 2],
                        qmax2[0:1, 2 * mt + 1:2 * mt + 2], t11[0:1, 1:2]),
                        ("s_ve", 1))
                    qi += 1

            w("scalar", "s_ve", sv["s_ve"])
            op("scalar", lambda e: e.activation(mh_row[:, :], qmax2[:, :],
               AF.Sqrt), ("s_ae", 1))
            op("scalar", lambda e: e.activation(t11[0:1, 0:1], kmax2[:, :],
               AF.Sqrt), ("s_ae", 1))
            w("vector", "s_ae", sv["s_ae"])
            op("vector", lambda e: e.tensor_scalar(mh_row[:, :], mh_row[:, :],
               t11[0:1, 0:1], 0.5, ALU.mult, ALU.add), ("s_ve", 1))
            w("vector", "s_ve")
            op("vector", lambda e: e.reduce_max(
                mhg_row[:, :],
                mh_row[0:1, 0:16].rearrange("p (g h) -> p g h", g=8),
                axis=AX.X), ("s_ve", 1))
            w("vector", "s_ve")
            op("vector", lambda e: e.tensor_scalar(nmhg_row[:, :],
               mhg_row[:, :], -1.0, None, ALU.mult), ("s_ve", 1))
            B.vfence()
            ve_m = sv["s_ve"]
            w("gpsimd", "s_ve", ve_m)
            dma("gpsimd", "s_g", vec_scr.ap()[0:1, 0:16], mh_row[0:1, :])
            dma("gpsimd", "s_g", vec_scr.ap()[1:2, 0:8], mhg_row[0:1, :])
            w("gpsimd", "s_g", sv["s_g"])
            dma("gpsimd", "s_g", mh_col[:, :],
                AP(vec_scr, 0, [[1, 16], [1, 1]]))
            dma("gpsimd", "s_g", mhg_col4[:, :],
                AP(vec_scr, 16, [[1, 8], [1, 1]]))
            w("tensor", "s_g", sv["s_g"])
            op("tensor", lambda e: e.matmul(psD[0:16, 0:1], gt4_sb[:, :],
               mhg_col4[:, :], start=True, stop=True), ("s_pe", 1))
            w("vector", "s_pe", sv["s_pe"])
            op("vector", lambda e: e.tensor_sub(dcol[:, :], psD[0:16, 0:1],
               mh_col[:, :]), ("s_ve", 1))
            w("scalar", "s_ve", sv["s_ve"])
            op("scalar", lambda e: e.activation(corr_col[:, :], dcol[:, :],
               AF.Exp), ("s_ae", 1))
            B.sfence()
            w("gpsimd", "s_ae", sv["s_ae"])
            dma("gpsimd", "s_g", vec_scr.ap()[2:3, 0:16],
                corr_col[:, 0:1])
            w("gpsimd", "s_g", sv["s_g"])
            dma("gpsimd", "s_g", corr_row[0:1, :], vec_scr.ap()[2:3, 0:16])
            g_c = sv["s_g"]
            w("tensor", "s_ve", ve_m)
            for g_ in range(8):
                op("tensor", lambda e, g_=g_: e.matmul(
                    psE[:, g_:g_ + 1], onesP[0:1, 0:128],
                    nmhg_row[0:1, g_:g_ + 1], start=True, stop=True),
                    *([("s_pe", 1)] if g_ == 7 else []))
            w("tensor", "s_g", g_c)
            for hh in range(16):
                op("tensor", lambda e, hh=hh: e.matmul(
                    psD[:, hh:hh + 1], onesP[0:1, 0:128],
                    corr_row[0:1, hh:hh + 1], start=True, stop=True),
                    *([("s_pe", 1)] if hh == 15 else []))
            w("vector", "s_pe", sv["s_pe"])
            op("vector", lambda e: e.tensor_copy(mbg[:, :], psE[:, 0:8]),
               ("s_ve", 1))
            op("vector", lambda e: e.tensor_copy(corrE[:, :], psD[:, 0:16]),
               ("s_ve", 1))
            ve_mset = sv["s_ve"]

            for wsrc, wscr in ((din["wr"], wT_scr[0]), (din["wk"], wT_scr[1]),
                               (din["wv"], wT_scr[2])):
                for rr in range(8):
                    dma("gpsimd", "s_ld", wtmp[:, :],
                        wsrc.ap()[128 * rr:128 * (rr + 1), :])
                    w("gpsimd", "s_ld", sv["s_ld"])
                    dma("gpsimd", "s_sc",
                        wscr.ap()[128 * rr:128 * (rr + 1), :], wtmp[:, :])
                    w("gpsimd", "s_sc", sv["s_sc"])
            ffn_bounce_done = sv["s_sc"]

        with ExitStack() as ctx2:
            E2 = ctx2.enter_context
            kTp = E2(nc.sbuf_tensor("kTp", [128, 2 * T], BF16))
            qTh = E2(nc.sbuf_tensor("qTh", [128, 2 * QR], BF16))
            vS = E2(nc.sbuf_tensor("vS", [128, 2 * NSC * 128], BF16))
            bufs = E2(nc.sbuf_tensor("bufs", [128, 4 * BUFW], BF16))
            pT = [E2(nc.sbuf_tensor(f"pT{i}", [128, 1024], BF16))
                  for i in range(3)]
            pm = [E2(nc.sbuf_tensor(f"pm{i}", [128, 1024], BF16))
                  for i in range(3)]
            rlb = E2(nc.sbuf_tensor("rlb", [128, 512], F32))
            opv = E2(nc.sbuf_tensor("opv", [128, 512], F32))
            otst = [E2(nc.sbuf_tensor(f"otst{i}", [128, 512], BF16))
                    for i in range(2)]

            git = 0
            ost_list = []
            pend_pv = []
            pair_end = {}

            def flush_one():
                fn, wv = pend_pv.pop(0)
                w("tensor", "s_mul", wv)
                fn()

            def flush_pv(keep=0):
                while len(pend_pv) > keep:
                    flush_one()

            def stream_pair(pr):
                sl = pr % 2
                if pr >= 2:
                    smm_e, pv_e, mul_e = pair_end[pr - 2]
                    w("sync", "s_smm", smm_e)
                    w("sync", "s_pv", pv_e)
                    w("sync", "s_mul", mul_e)
                else:
                    w("sync", "s_sc")
                    w("sync", "s_rep", rep_done)
                    w("sync", "s_ve", ve_mset)
                dma("sync", "s_tp", kTp[:, T * sl:T * (sl + 1)],
                    k_scr.ap()[128 * pr:128 * (pr + 1), :])
                dma("sync", "s_tp", qTh[:, QR * sl:QR * (sl + 1)],
                    q_scr.ap()[128 * pr:128 * (pr + 1), :])
                dma("sync", "s_tp",
                    vS[:, 8192 * sl:8192 * (sl + 1)].rearrange(
                        "p (n m) -> p n m", n=NSC),
                    AP(v_scr, 128 * pr, [[1024, 128], [131072, NSC],
                                         [1, 128]]))
                for h2 in range(2):
                    dma("sync", "s_tp",
                        bufs[:, BUFW * (2 * sl + h2):
                             BUFW * (2 * sl + h2 + 1)],
                        rep_scr.ap()[:, BUFW * (2 * pr + h2):
                                     BUFW * (2 * pr + h2 + 1)])
                return sv["s_tp"]

            tp_pair = {0: stream_pair(0), 1: stream_pair(1)}
            for pr in range(8):
                sl = pr % 2
                tp_hg = tp_pair[pr]
                w("vector", "s_tp", tp_hg)
                if pr == 0:
                    w("vector", "s_ve", ve_mset)
                for h2 in range(2):
                    op("vector", lambda e, h2=h2, sl=sl, pr=pr:
                       e.tensor_scalar(
                           bufs[:, BUFW * (2 * sl + h2):
                                BUFW * (2 * sl + h2 + 1)],
                           bufs[:, BUFW * (2 * sl + h2):
                                BUFW * (2 * sl + h2 + 1)],
                           corrE[:, 2 * pr + h2:2 * pr + h2 + 1], None,
                           ALU.mult))
                for qb in range(5):
                    q0, qw = QB0[qb], QBS[qb]
                    for sc in range(NSC):
                        j = git
                        jb = j % 2
                        if j >= 2:
                            w("tensor", "s_exp", j - 1)
                        if sc == 0:
                            w("tensor", "s_tp", tp_hg)
                            if qb == 0 and pr == 0:
                                w("tensor", "s_ve", ve_mset)
                            if qb > 0 or pr > 0:
                                w("tensor", "s_nrm", sv["s_nrm"])
                        for h2 in range(2):
                            r0 = 64 * h2
                            op("tensor", lambda e, h2=h2, r0=r0, sl=sl,
                               sc=sc, q0=q0, qw=qw, jb=jb: e.matmul(
                                   psA[:, 1024 * jb + 512 * h2:
                                       1024 * jb + 512 * h2 + qw],
                                   kTp[r0:r0 + 64, T * sl + 128 * sc:
                                       T * sl + 128 * (sc + 1)],
                                   qTh[r0:r0 + 64, QR * sl + q0:
                                       QR * sl + q0 + qw],
                                   start=True, stop=True,
                                   tile_position=(r0, 0)),
                               *([("s_smm", 1)] if h2 == 1 else []))
                        flush_pv(keep=1)
                        if j == 0:
                            w("scalar", "s_ve", ve_mset)
                        w("scalar", "s_smm", j + 1)
                        if j >= 3:
                            w("scalar", "s_mul", j - 2)
                        op("scalar", lambda e, j=j, qw=qw, pr=pr, jb=jb:
                           e.activation(
                               pT[j % 3][:, :].rearrange(
                                   "p (h q) -> p h q", h=2)[:, :, 0:qw],
                               psA[:, 1024 * jb:1024 * (jb + 1)].rearrange(
                                   "p (h q) -> p h q", h=2)[:, :, 0:qw],
                               AF.Exp, bias=mbg[:, pr:pr + 1], scale=1.0),
                           ("s_exp", 1))
                        w("vector", "s_exp", j + 1)
                        if j >= 3:
                            w("vector", "s_pv", j - 2)
                        j0 = q0 - 128 * sc + 8064
                        op("vector", lambda e, j=j, j0=j0, qw=qw, sl=sl:
                           e.tensor_mul(
                               pm[j % 3][:, :].rearrange(
                                   "p (h q) -> p h q", h=2)[:, :, 0:qw],
                               bufs[:, :].rearrange(
                                   "p (h q) -> p h q",
                                   h=4)[:, 2 * sl:2 * sl + 2, j0:j0 + qw],
                               pT[j % 3][:, :].rearrange(
                                   "p (h q) -> p h q", h=2)[:, :, 0:qw]),
                           ("s_mul", 1))
                        mul_target = sv["s_mul"]

                        def mk_pv(sc=sc, qw=qw, g=git, sl=sl):
                            def go():
                                for h2 in range(2):
                                    op("tensor", lambda e, h2=h2, sc=sc,
                                       qw=qw, g=g, sl=sl: e.matmul(
                                           psB[64 * h2:64 * (h2 + 1), 0:qw],
                                           vS[:, 8192 * sl + 128 * sc +
                                              64 * h2:8192 * sl + 128 * sc +
                                              64 * (h2 + 1)],
                                           pm[g % 3][:, 512 * h2:
                                                     512 * h2 + qw],
                                           start=(sc == 0),
                                           stop=(sc == NSC - 1),
                                           tile_position=(0, 64 * h2)))
                                for h2 in range(2):
                                    op("tensor", lambda e, h2=h2, sc=sc,
                                       qw=qw, g=g: e.matmul(
                                           psD[32 * h2:32 * h2 + 1, 0:qw],
                                           onesl[:, 0:1],
                                           pm[g % 3][:, 512 * h2:
                                                     512 * h2 + qw],
                                           start=(sc == 0),
                                           stop=(sc == NSC - 1),
                                           tile_position=(0, 32 * h2)),
                                       *([("s_pv", 1)] if h2 == 1 else []))
                            return go
                        pend_pv.append((mk_pv(), mul_target))
                        git += 1
                    flush_pv()
                    w("vector", "s_pv", sv["s_pv"])
                    w("vector", "s_bc", sv["s_bc"])
                    op("vector", lambda e, qw=qw: e.reciprocal(
                        rlb[:, 0:qw], psD[:, 0:qw]), ("s_nrm", 1))
                    op("vector", lambda e: e.tensor_copy(
                        B.fence_sb[0:1, 0:1], B.fence_sb[0:1, 1:2]),
                        ("s_nrm", 1))
                    w("tensor", "s_nrm", sv["s_nrm"])
                    for h2 in range(2):
                        trow = 32 * h2
                        op("tensor", lambda e, trow=trow, h2=h2, qw=qw:
                           e.matmul(
                               psA[64 * h2:64 * (h2 + 1), 0:qw],
                               onesP[trow:trow + 1, 0:64],
                               rlb[trow:trow + 1, 0:qw],
                               start=True, stop=True,
                               tile_position=(trow, 64 * h2)),
                           *([("s_bc", 1)] if h2 == 1 else []))
                    ot = otst[len(ost_list) % 2]
                    w("vector", "s_bc", sv["s_bc"])
                    oi = len(ost_list)
                    if oi >= 2:
                        w("vector", "s_ost", ost_list[oi - 2])
                    op("vector", lambda e, qw=qw: e.tensor_copy(
                        opv[:, 0:qw], psB[:, 0:qw]), ("s_nrm", 1))
                    w("vector", "s_nrm")
                    op("vector", lambda e, ot=ot, qw=qw: e.tensor_mul(
                        ot[:, 0:qw], opv[:, 0:qw], psA[:, 0:qw]),
                        ("s_nrm", 1))
                    op("vector", lambda e: e.tensor_copy(
                        B.fence_sb[0:1, 0:1], B.fence_sb[0:1, 1:2]),
                        ("s_nrm", 1))
                    w("sync", "s_nrm", sv["s_nrm"])
                    dma("sync", "s_ost",
                        o_scr.ap()[128 * pr:128 * (pr + 1), q0:q0 + qw],
                        ot[:, 0:qw])
                    ost_list.append(sv["s_ost"])
                pair_end[pr] = (sv["s_smm"], sv["s_pv"], sv["s_mul"])
                if pr + 2 < 8:
                    tp_pair[pr + 2] = stream_pair(pr + 2)
            last_ost = sv["s_ost"]

        x1hi = E(nc.sbuf_tensor("x1hi", [128, 8 * QR], BF16))
        x1lo = E(nc.sbuf_tensor("x1lo", [128, 8 * QR], BF16))
        stat = E(nc.sbuf_tensor("stat", [1, 2048], F32))
        sq32 = E(nc.sbuf_tensor("sq32", [128, 512], F32))
        x1f = E(nc.sbuf_tensor("x1f", [128, 512], F32))

        def emit_ln(slabs, qw, sink):
            w("tensor", "s_ve")
            w("tensor", "s_nrm")
            for cc in range(8):
                th, c0 = slabs[cc]
                op("tensor", lambda e, th=th, c0=c0, cc=cc, qw=qw: e.matmul(
                    psD[0:1, 0:qw], onesc[:, 0:1], th[:, c0:c0 + qw],
                    start=(cc == 0), stop=(cc == 7)),
                    *([("s_pe", 1)] if cc == 7 else []))
            for cc in range(8):
                th, c0 = slabs[cc]
                if cc >= 1:
                    w("vector", "s_pe", sv["s_pe"])
                op("vector", lambda e, th=th, c0=c0, qw=qw: e.tensor_mul(
                    sq32[:, 0:qw], th[:, c0:c0 + qw], th[:, c0:c0 + qw]),
                    ("s_ve", 1))
                w("tensor", "s_ve", sv["s_ve"])
                op("tensor", lambda e, cc=cc, qw=qw: e.matmul(
                    psE[0:1, 0:qw], onesc[:, 0:1], sq32[:, 0:qw],
                    start=(cc == 0), stop=(cc == 7)), ("s_pe", 1))
            w("vector", "s_pe", sv["s_pe"])
            op("vector", lambda e, qw=qw: e.tensor_scalar(
                stat[0:1, 0:qw], psD[0:1, 0:qw], 1.0 / C, None, ALU.mult),
                ("s_ve", 1))
            op("vector", lambda e, qw=qw: e.tensor_scalar(
                stat[0:1, 512:512 + qw], psE[0:1, 0:qw], 1.0 / C, None,
                ALU.mult), ("s_ve", 1))
            w("vector", "s_ve")
            op("vector", lambda e, qw=qw: e.tensor_mul(
                stat[0:1, 1024:1024 + qw], stat[0:1, 0:qw], stat[0:1, 0:qw]),
                ("s_ve", 1))
            w("vector", "s_ve")
            op("vector", lambda e, qw=qw: e.tensor_sub(
                stat[0:1, 512:512 + qw], stat[0:1, 512:512 + qw],
                stat[0:1, 1024:1024 + qw]), ("s_ve", 1))
            w("vector", "s_ve")
            op("vector", lambda e, qw=qw: e.tensor_scalar(
                stat[0:1, 512:512 + qw], stat[0:1, 512:512 + qw], 1.0, EPS,
                ALU.mult, ALU.add), ("s_ve", 1))
            w("scalar", "s_ve", sv["s_ve"])
            op("scalar", lambda e, qw=qw: e.activation(
                stat[0:1, 1024:1024 + qw], stat[0:1, 512:512 + qw], AF.Ln),
                ("s_ae", 1))
            w("scalar", "s_ae")
            op("scalar", lambda e, qw=qw: e.activation(
                stat[0:1, 1536:1536 + qw], stat[0:1, 1024:1024 + qw], AF.Exp,
                bias=0.0, scale=-0.5), ("s_ae", 1))
            w("tensor", "s_ve", sv["s_ve"])
            w("tensor", "s_ae", sv["s_ae"])
            op("tensor", lambda e, qw=qw: e.matmul(
                psD[:, 0:qw], onesP[0:1, 0:128], stat[0:1, 0:qw],
                start=True, stop=True), ("s_pe", 1))
            op("tensor", lambda e, qw=qw: e.matmul(
                psE[:, 0:qw], onesP[0:1, 0:128], stat[0:1, 1536:1536 + qw],
                start=True, stop=True), ("s_pe", 1))
            w("vector", "s_pe", sv["s_pe"])
            for cc in range(8):
                th, c0 = slabs[cc]
                op("vector", lambda e, th=th, c0=c0, qw=qw: e.tensor_sub(
                    x1f[:, 0:qw], th[:, c0:c0 + qw], psD[:, 0:qw]),
                    ("s_ve", 1))
                w("vector", "s_ve")
                op("vector", lambda e, qw=qw: e.tensor_mul(
                    x1f[:, 0:qw], x1f[:, 0:qw], psE[:, 0:qw]), ("s_ve", 1))
                w("vector", "s_ve")
                sink(cc)

        with ExitStack() as ctx3:
            E3 = ctx3.enter_context
            wo_sb = E3(nc.sbuf_tensor("wo_sb", [128, 8 * C], BF16))
            oT = [E3(nc.sbuf_tensor(f"oT{i}", [128, 8 * 512], BF16))
                  for i in range(2)]
            xhiT = [E3(nc.sbuf_tensor(f"xhiT{i}", [128, 8 * 512], BF16))
                    for i in range(2)]
            xloT = [E3(nc.sbuf_tensor(f"xloT{i}", [128, 8 * 512], BF16))
                    for i in range(2)]
            r32 = E3(nc.sbuf_tensor("r32", [128, 8 * 512], F32))

            w("gpsimd", "s_mul")
            w("gpsimd", "s_pv")
            w("gpsimd", "s_nrm")
            w("gpsimd", "s_smm")
            w("gpsimd", "s_exp")
            for cc in range(8):
                dma("gpsimd", "s_ld", wo_sb[:, C * cc:C * (cc + 1)],
                    din["wout"].ap()[128 * cc:128 * (cc + 1), :])
            ld_wo = sv["s_ld"]

            tb_pe = []
            tb_tp = []
            tb_vedone = [0] * 5

            def p3_loads(tb):
                q0, qw = QB0[tb], QBS[tb]
                bb = tb % 2
                if tb == 0:
                    w("sync", "s_ost", last_ost)
                if tb >= 2:
                    w("sync", "s_pe", tb_pe[tb - 2])
                    w("sync", "s_ve", tb_vedone[tb - 2])
                for cc in range(8):
                    dma("sync", "s_tp", oT[bb][:, 512 * cc:512 * cc + qw],
                        o_scr.ap()[128 * cc:128 * (cc + 1), q0:q0 + qw])
                    dma("sync", "s_tp", xhiT[bb][:, 512 * cc:512 * cc + qw],
                        xhi_scr.ap()[q0:q0 + qw, 128 * cc:128 * (cc + 1)],
                        transpose=True)
                    dma("sync", "s_tp", xloT[bb][:, 512 * cc:512 * cc + qw],
                        xlo_scr.ap()[q0:q0 + qw, 128 * cc:128 * (cc + 1)],
                        transpose=True)
                return sv["s_tp"]

            tb_tp = [p3_loads(0), p3_loads(1)]
            for tb in range(5):
                q0, qw = QB0[tb], QBS[tb]
                bb = tb % 2
                tp_tb = tb_tp[tb]
                w("tensor", "s_tp", tp_tb)
                if tb == 0:
                    w("tensor", "s_ld", ld_wo)
                oev = []
                for mt in range(8):
                    hb = mt % 2
                    if mt >= 2:
                        w("tensor", "s_ve", oev[mt - 2])
                    for cc in range(8):
                        op("tensor", lambda e, hb=hb, cc=cc, mt=mt, qw=qw,
                           bb=bb: e.matmul(
                               psA[:, 1024 + 512 * hb:1024 + 512 * hb + qw],
                               wo_sb[:, C * cc + 128 * mt:
                                     C * cc + 128 * (mt + 1)],
                               oT[bb][:, 512 * cc:512 * cc + qw],
                               start=(cc == 0), stop=(cc == 7)),
                           *([("s_pe", 1)] if cc == 7 else []))
                    w("vector", "s_pe", sv["s_pe"])
                    op("vector", lambda e, hb=hb, mt=mt, qw=qw:
                       e.tensor_scalar(
                           r32[:, 512 * mt:512 * mt + qw],
                           psA[:, 1024 + 512 * hb:1024 + 512 * hb + qw],
                           bo8[:, mt:mt + 1], None, ALU.add), ("s_ve", 1))
                    w("vector", "s_ve")
                    op("vector", lambda e, mt=mt, qw=qw, bb=bb: e.tensor_add(
                        r32[:, 512 * mt:512 * mt + qw],
                        r32[:, 512 * mt:512 * mt + qw],
                        xhiT[bb][:, 512 * mt:512 * mt + qw]), ("s_ve", 1))
                    w("vector", "s_ve")
                    op("vector", lambda e, mt=mt, qw=qw, bb=bb: e.tensor_add(
                        r32[:, 512 * mt:512 * mt + qw],
                        r32[:, 512 * mt:512 * mt + qw],
                        xloT[bb][:, 512 * mt:512 * mt + qw]), ("s_ve", 1))
                    oev.append(sv["s_ve"])
                def sink_x1(cc, q0=q0, qw=qw):
                    op("vector", lambda e, cc=cc, qw=qw, q0=q0: e.tensor_copy(
                        x1hi[:, QR * cc + q0:QR * cc + q0 + qw],
                        x1f[:, 0:qw]), ("s_ve", 1))
                    w("vector", "s_ve")
                    op("vector", lambda e, cc=cc, qw=qw, q0=q0: e.tensor_sub(
                        sq32[:, 0:qw], x1f[:, 0:qw],
                        x1hi[:, QR * cc + q0:QR * cc + q0 + qw]), ("s_ve", 1))
                    w("vector", "s_ve")
                    op("vector", lambda e, cc=cc, qw=qw, q0=q0: e.tensor_copy(
                        x1lo[:, QR * cc + q0:QR * cc + q0 + qw],
                        sq32[:, 0:qw]), ("s_ve", 1))
                emit_ln([(r32, 512 * cc) for cc in range(8)], qw, sink_x1)
                tb_pe.append(sv["s_pe"])
                tb_vedone[tb] = sv["s_ve"]
                if tb + 2 < 5:
                    tb_tp.append(p3_loads(tb + 2))
            for cc in range(8):
                for t_ in (x1hi, x1lo):
                    op("vector", lambda e, t_=t_, cc=cc: e.tensor_scalar(
                        t_[:, QR * cc + 127:QR * cc + 128],
                        t_[:, QR * cc + 127:QR * cc + 128],
                        mask_sb[:, 0:1], None, ALU.mult), ("s_ve", 1))
            ve_x1m = sv["s_ve"]
            p3_pe = sv["s_pe"]

        with ExitStack() as ctx4:
            E4 = ctx4.enter_context
            wrT = E4(nc.sbuf_tensor("wrT", [128, 8 * C], BF16))
            wkT = E4(nc.sbuf_tensor("wkT", [128, 8 * C], BF16))
            wvT = E4(nc.sbuf_tensor("wvT", [128, 8 * C], BF16))
            xr_sb = E4(nc.sbuf_tensor("xr_sb", [128, 8 * 512], BF16))
            xk_sb = E4(nc.sbuf_tensor("xk_sb", [128, 8 * 512], BF16))
            rb_sb = E4(nc.sbuf_tensor("rb_sb", [128, 8 * 512], BF16))
            ka_sb = E4(nc.sbuf_tensor("ka_sb", [128, 8 * 512], BF16))
            km32 = E4(nc.sbuf_tensor("km32", [128, 512], F32))
            ff32 = E4(nc.sbuf_tensor("ff32", [128, 8 * 512], F32))
            x2hi = [E4(nc.sbuf_tensor(f"x2hi{i}", [128, 512], BF16))
                    for i in range(2)]
            x2lo = [E4(nc.sbuf_tensor(f"x2lo{i}", [128, 512], BF16))
                    for i in range(2)]

            w("sync", "s_pe", p3_pe)
            w("sync", "s_ve", sv["s_ve"])
            w("sync", "s_sc", ffn_bounce_done)
            for wscr, wsb in ((wT_scr[0], wrT), (wT_scr[1], wkT),
                              (wT_scr[2], wvT)):
                for cc in range(8):
                    dma("sync", "s_tp", wsb[:, C * cc:C * (cc + 1)],
                        wscr.ap()[:, 128 * cc:128 * (cc + 1)], transpose=True)
            tp_w = sv["s_tp"]

            x2_done = []
            x2i = 0
            for fb in range(4):
                f0 = QEXT + 512 * fb
                fw = 512
                if fb == 0:
                    w("vector", "s_ve", ve_x1m)
                for cc in range(8):
                    for (dst, mu, mui) in ((xr_sb, mur8, mur8i),
                                           (xk_sb, muk8, muk8i)):
                        op("vector", lambda e, dst=dst, mu=mu, cc=cc, f0=f0,
                           fw=fw: e.tensor_scalar(
                               dst[:, 512 * cc:512 * cc + fw],
                               x1hi[:, QR * cc + f0:QR * cc + f0 + fw],
                               mu[:, cc:cc + 1], None, ALU.mult), ("s_ve", 1))
                        w("vector", "s_ve")
                        op("vector", lambda e, dst=dst, mui=mui, cc=cc, f0=f0,
                           fw=fw: e.scalar_tensor_tensor(
                               dst[:, 512 * cc:512 * cc + fw],
                               x1hi[:, QR * cc + f0 - 1:
                                    QR * cc + f0 - 1 + fw],
                               mui[:, cc:cc + 1],
                               dst[:, 512 * cc:512 * cc + fw],
                               ALU.mult, ALU.add), ("s_ve", 1))
                ve_bl = sv["s_ve"]
                w("tensor", "s_ve", ve_bl)
                if fb == 0:
                    w("tensor", "s_tp", tp_w)
                sig_ae = []
                for mt in range(8):
                    hb = mt % 2
                    if mt >= 2:
                        w("tensor", "s_ae", sig_ae[mt - 2])
                    for cc in range(8):
                        op("tensor", lambda e, hb=hb, cc=cc, mt=mt, fw=fw:
                           e.matmul(
                               psA[:, 1024 + 512 * hb:1024 + 512 * hb + fw],
                               wrT[:, C * cc + 128 * mt:
                                   C * cc + 128 * (mt + 1)],
                               xr_sb[:, 512 * cc:512 * cc + fw],
                               start=(cc == 0), stop=(cc == 7)),
                           *([("s_pe", 1)] if cc == 7 else []))
                    w("scalar", "s_pe", sv["s_pe"])
                    op("scalar", lambda e, hb=hb, mt=mt, fw=fw: e.activation(
                        rb_sb[:, 512 * mt:512 * mt + fw],
                        psA[:, 1024 + 512 * hb:1024 + 512 * hb + fw],
                        AF.Sigmoid), ("s_ae", 1))
                    sig_ae.append(sv["s_ae"])
                kp_ve = []
                for mt in range(8):
                    hb = mt % 2
                    if mt == 0:
                        w("tensor", "s_ae", sig_ae[-1])
                    if mt >= 2:
                        w("tensor", "s_ve", kp_ve[mt - 2])
                    for cc in range(8):
                        op("tensor", lambda e, hb=hb, cc=cc, mt=mt, fw=fw:
                           e.matmul(
                               psA[:, 1024 + 512 * hb:1024 + 512 * hb + fw],
                               wkT[:, C * cc + 128 * mt:
                                   C * cc + 128 * (mt + 1)],
                               xk_sb[:, 512 * cc:512 * cc + fw],
                               start=(cc == 0), stop=(cc == 7)),
                           *([("s_pe", 1)] if cc == 7 else []))
                    w("vector", "s_pe", sv["s_pe"])
                    op("vector", lambda e, hb=hb, fw=fw: e.tensor_scalar(
                        km32[:, 0:fw],
                        psA[:, 1024 + 512 * hb:1024 + 512 * hb + fw],
                        0.0, None, ALU.max), ("s_ve", 1))
                    w("vector", "s_ve")
                    op("vector", lambda e, mt=mt, fw=fw: e.tensor_mul(
                        ka_sb[:, 512 * mt:512 * mt + fw], km32[:, 0:fw],
                        km32[:, 0:fw]), ("s_ve", 1))
                    kp_ve.append(sv["s_ve"])
                fv_ve = []
                for mt in range(8):
                    hb = mt % 2
                    if mt == 0:
                        w("tensor", "s_ve", kp_ve[-1])
                    if mt >= 2:
                        w("tensor", "s_ve", fv_ve[mt - 2])
                    for cc in range(8):
                        op("tensor", lambda e, hb=hb, cc=cc, mt=mt, fw=fw:
                           e.matmul(
                               psA[:, 1024 + 512 * hb:1024 + 512 * hb + fw],
                               wvT[:, C * cc + 128 * mt:
                                   C * cc + 128 * (mt + 1)],
                               ka_sb[:, 512 * cc:512 * cc + fw],
                               start=(cc == 0), stop=(cc == 7)),
                           *([("s_pe", 1)] if cc == 7 else []))
                    w("vector", "s_pe", sv["s_pe"])
                    op("vector", lambda e, hb=hb, mt=mt, fw=fw: e.tensor_mul(
                        ff32[:, 512 * mt:512 * mt + fw],
                        rb_sb[:, 512 * mt:512 * mt + fw],
                        psA[:, 1024 + 512 * hb:1024 + 512 * hb + fw]),
                        ("s_ve", 1))
                    w("vector", "s_ve")
                    op("vector", lambda e, mt=mt, f0=f0, fw=fw: e.tensor_add(
                        ff32[:, 512 * mt:512 * mt + fw],
                        ff32[:, 512 * mt:512 * mt + fw],
                        x1hi[:, QR * mt + f0:QR * mt + f0 + fw]), ("s_ve", 1))
                    w("vector", "s_ve")
                    op("vector", lambda e, mt=mt, f0=f0, fw=fw: e.tensor_add(
                        ff32[:, 512 * mt:512 * mt + fw],
                        ff32[:, 512 * mt:512 * mt + fw],
                        x1lo[:, QR * mt + f0:QR * mt + f0 + fw]), ("s_ve", 1))
                    fv_ve.append(sv["s_ve"])
                def sink_x2(cc, fb=fb, fw=fw):
                    bi = (fb * 8 + cc) % 2
                    if fb * 8 + cc >= 2:
                        w("vector", "s_sc", x2_done[fb * 8 + cc - 2])
                    op("vector", lambda e, bi=bi, fw=fw: e.tensor_copy(
                        x2hi[bi][:, 0:fw], x1f[:, 0:fw]), ("s_ve", 1))
                    w("vector", "s_ve")
                    op("vector", lambda e, bi=bi, fw=fw: e.tensor_sub(
                        sq32[:, 0:fw], x1f[:, 0:fw], x2hi[bi][:, 0:fw]),
                        ("s_ve", 1))
                    w("vector", "s_ve")
                    op("vector", lambda e, bi=bi, fw=fw: e.tensor_copy(
                        x2lo[bi][:, 0:fw], sq32[:, 0:fw]), ("s_ve", 1))
                    B.vfence()
                    w("gpsimd", "s_ve", sv["s_ve"])
                    dma("gpsimd", "s_sc",
                        x2h_scr.ap()[128 * cc:128 * (cc + 1),
                                     512 * fb:512 * (fb + 1)],
                        x2hi[bi][:, 0:fw])
                    dma("gpsimd", "s_sc",
                        x2l_scr.ap()[128 * cc:128 * (cc + 1),
                                     512 * fb:512 * (fb + 1)],
                        x2lo[bi][:, 0:fw])
                    x2_done.append(sv["s_sc"])
                emit_ln([(ff32, 512 * cc) for cc in range(8)], fw, sink_x2)
            sc_x2 = sv["s_sc"]

        yh = [E(nc.sbuf_tensor(f"yh{i}", [128, C], BF16)) for i in range(2)]
        yl = [E(nc.sbuf_tensor(f"yl{i}", [128, C], BF16)) for i in range(2)]
        y32 = [E(nc.sbuf_tensor(f"y32{i}", [128, C], F32)) for i in range(2)]
        w("sync", "s_sc", sc_x2)
        ydma = []
        for ob in range(16):
            b_ = ob % 2
            if ob >= 2:
                w("sync", "s_y", ydma[ob - 2])
            dma("sync", "s_tp", yh[b_][:, :],
                x2h_scr.ap()[:, 128 * ob:128 * (ob + 1)], transpose=True)
            dma("sync", "s_tp", yl[b_][:, :],
                x2l_scr.ap()[:, 128 * ob:128 * (ob + 1)], transpose=True)
            tp_y = sv["s_tp"]
            w("vector", "s_tp", tp_y)
            if ob >= 2:
                w("vector", "s_y", ydma[ob - 2])
            op("vector", lambda e, b_=b_: e.tensor_add(
                y32[b_][:, :], yh[b_][:, :], yl[b_][:, :]), ("s_ve", 1))
            B.vfence()
            w("sync", "s_ve", sv["s_ve"])
            dma("sync", "s_y", y[128 * ob:128 * (ob + 1), :], y32[b_][:, :])
            ydma.append(sv["s_y"])
        if DEBUG_TAPS >= 1:
            tap = {}
            for nm, shape, dt in (
                    ("mbg", [128, 8], F32), ("corrE", [128, 16], F32),
                    ("stats", [1, 64], F32), ("etab", [16, ETW], BF16),
                    ("q", [128, QR], BF16), ("k", [128, 2048], BF16),
                    ("v", [128, 1024], BF16), ("o", [128, QR], BF16),
                    ("x1", [128, QR], BF16), ("xhi", [128, 1024], BF16),
                    ("rep", [128, 2048], BF16), ("x2h", [128, 2048], BF16),
                    ("x2l", [128, 2048], BF16), ("x1b", [128, QR], BF16),
                    ("x1lo", [128, QR], BF16)):
                tap[nm] = nc.dram_tensor("tap_" + nm, shape, dt,
                                         kind="ExternalOutput").ap()
        if DEBUG_TAPS in (3, 4):
            w("sync", "s_y", sv["s_y"])
            dma("sync", "s_y", tap["mbg"], mbg[:, :])
            dma("sync", "s_y", tap["corrE"], corrE[:, :])
            dma("sync", "s_y", tap["stats"][0:1, 0:16], qmax2[:, :])
            dma("sync", "s_y", tap["stats"][0:1, 16:17], kmax2[:, :])
            dma("sync", "s_y", tap["stats"][0:1, 17:33], mh_row[:, :])
            dma("sync", "s_y", tap["stats"][0:1, 33:41], mhg_row[:, :])
            dma("sync", "s_y", tap["stats"][0:1, 41:57], corr_row[:, :])
            dma("sync", "s_y", tap["etab"], etab_scr.ap())
            dma("sync", "s_y", tap["q"], q_scr.ap()[0:128, :])
            dma("sync", "s_y", tap["k"], k_scr.ap()[0:128, 0:2048])
            dma("sync", "s_y", tap["v"], v_scr.ap()[0:128, :])
            dma("sync", "s_y", tap["o"], o_scr.ap()[0:128, :])
            dma("sync", "s_y", tap["x1"], x1hi[:, 0:QR])
            dma("sync", "s_y", tap["xhi"], xhi_scr.ap()[0:128, :])
            dma("sync", "s_y", tap["rep"], rep_scr.ap()[0:128, 0:2048])
            dma("sync", "s_y", tap["x2h"], x2h_scr.ap()[0:128, :])
            dma("sync", "s_y", tap["x2l"], x2l_scr.ap()[0:128, :])
            dma("sync", "s_y", tap["x1b"], x1hi[:, 3 * QR:4 * QR])
            dma("sync", "s_y", tap["x1lo"], x1lo[:, 0:QR])
        w("sync", "s_y", sv["s_y"])
        w("gpsimd", "s_sc", sv["s_sc"])
        w("gpsimd", "s_g96", sv["s_g96"])
        w("scalar", "s_rep", sv["s_rep"])

        with nc.Block() as block:
            @block.sync
            def _(e):
                for fn in B.prog["sync"]:
                    fn(e)

            @block.tensor
            def _(e):
                for fn in B.prog["tensor"]:
                    fn(e)

            @block.scalar
            def _(e):
                for fn in B.prog["scalar"]:
                    fn(e)

            @block.vector
            def _(e):
                for fn in B.prog["vector"]:
                    fn(e)

            @block.gpsimd
            def _(e):
                for fn in B.prog["gpsimd"]:
                    fn(e)

    return nc


_NC_CACHE = None


def kernel(**inputs):
    global _NC_CACHE
    f32 = np.float32
    L = {k: np.asarray(v) for k, v in inputs.items()}
    x = L["L0_attn_x"]

    def g_of(delta):
        d = delta.astype(np.float64)
        return (np.sign(d) * np.log1p(np.abs(d))).astype(f32)

    in_maps = []
    for c in range(8):
        b, ts = c // 4, c % 4
        t0 = 2048 * ts
        xb = np.ascontiguousarray(x[b]).astype(f32)
        idx = (np.arange(t0 - QEXT, t0 + QOWN) % T)
        xo = np.ascontiguousarray(xb[idx])
        delta0 = t0 - QEXT - (T - 1)
        g = g_of(np.arange(delta0, delta0 + ETW))
        gsl = np.ascontiguousarray(np.broadcast_to(g, (32, ETW)))
        gt4 = np.zeros((8, H), f32)
        for gg in range(8):
            gt4[gg, 2 * gg:2 * gg + 2] = 1.0
        m0 = np.full((128, 1), 0.0 if ts == 0 else 1.0, f32)
        in_maps.append(dict(
            xf=xb, xo=xo,
            wqkv=np.ascontiguousarray(L["L0_attn_w_qkv"], f32),
            bqkv=np.ascontiguousarray(L["L0_attn_b_qkv"], f32).reshape(1, -1),
            wout=np.ascontiguousarray(L["L0_attn_w_out"], f32),
            bout=np.ascontiguousarray(L["L0_attn_b_out"], f32).reshape(1, -1),
            wr=np.ascontiguousarray(L["L0_ffn_W_r"], f32),
            wk=np.ascontiguousarray(L["L0_ffn_W_k"], f32),
            wv=np.ascontiguousarray(L["L0_ffn_W_v"], f32),
            mur=np.ascontiguousarray(L["L0_ffn_mu_r"], f32).reshape(1, -1),
            muk=np.ascontiguousarray(L["L0_ffn_mu_k"], f32).reshape(1, -1),
            w1r=np.ascontiguousarray(L["L0_attn_w1_rpe"], f32),
            b1r=np.ascontiguousarray(L["L0_attn_b1_rpe"], f32).reshape(1, -1),
            w2r=np.ascontiguousarray(L["L0_attn_w2_rpe"], f32),
            b2r=np.ascontiguousarray(L["L0_attn_b2_rpe"], f32).reshape(1, -1),
            gsl=gsl, gt4=gt4, mask0=m0,
        ))

    if _NC_CACHE is None:
        _NC_CACHE = build_nc()
    res = run_bass_kernel_spmd(_NC_CACHE, in_maps, core_ids=list(range(8)))
    out = np.zeros((2, T, C), f32)
    for c in range(8):
        b, ts = c // 4, c % 4
        out[b, 2048 * ts:2048 * (ts + 1), :] = res.results[c]["y"]
    return out


# version 27
# speedup vs baseline: 1.0433x; 1.0433x over previous
import numpy as np
from contextlib import ExitStack

import concourse.bass as bass
import concourse.mybir as mybir
from concourse.ap import AP
from concourse.bass_utils import run_bass_kernel_spmd

F32 = mybir.dt.float32
BF16 = mybir.dt.bfloat16
AF = mybir.ActivationFunctionType
ALU = mybir.AluOpType
AX = mybir.AxisListType

T = 8192
C = 1024
H = 16
QEXT = 128
QOWN = 2048
QR = QEXT + QOWN
NSC = 64
QBS = [512, 512, 512, 512, 128]
QB0 = [0, 512, 1024, 1536, 2048]
ETW = 10368
BUFW = 10240
EPS = 1e-5
DEBUG_TAPS = 0


class Builder:
    def __init__(self, nc, ctx):
        self.nc = nc
        self.ctx = ctx
        self.prog = {k: [] for k in ("sync", "tensor", "scalar", "vector",
                                     "gpsimd")}
        self.sems = {}
        self.sv = {}

    def sem(self, *names):
        for n in names:
            self.sems[n] = self.ctx.enter_context(self.nc.semaphore(n))
            self.sv[n] = 0

    def w(self, eng, sname, val=None):
        v = self.sv[sname] if val is None else val
        if v <= 0:
            return
        self.prog[eng].append(
            lambda e, s=self.sems[sname], v=v: e.wait_ge(s, v))

    def op(self, eng, fn, *incs):
        hs = []
        for (s, n) in incs:
            self.sv[s] += n
            hs.append((self.sems[s], n))

        def run(e, fn=fn, hs=hs):
            ins = fn(e)
            for h, n in hs:
                ins = ins.then_inc(h, n)
        self.prog[eng].append(run)

    def vfence(self):
        f = self.fence_sb
        self.op("vector", lambda e, f=f: e.tensor_copy(
            f[0:1, 0:1], f[0:1, 1:2]), ("s_ve", 1))

    def sfence(self):
        f = self.fence_sb
        self.op("scalar", lambda e, f=f: e.activation(
            f[0:1, 2:3], f[0:1, 3:4], mybir.ActivationFunctionType.Copy),
            ("s_ae", 1))

    def dma(self, eng, sname, out, in_, transpose=False, slow=False):
        if slow:
            self.op(eng, lambda e, o=out, i=in_:
                    e.dma_start(out=o, in_=i,
                                allow_slow_non_contiguous=True), (sname, 16))
        else:
            self.op(eng, lambda e, o=out, i=in_, t=transpose:
                    e.dma_start(out=o, in_=i, transpose=t), (sname, 16))


def build_nc():
    nc = bass.Bass("TRN2", target_bir_lowering=False)
    din = {}

    def mk_in(name, shape, dt=F32):
        h = nc.dram_tensor(name, list(shape), dt, kind="ExternalInput")
        din[name] = h
        return h

    for name, shape in (
            ("xf", [T, C]), ("xo", [QR, C]), ("wqkv", [C, 3 * C]),
            ("bqkv", [1, 3 * C]), ("wout", [C, C]), ("bout", [1, C]),
            ("wr", [C, C]), ("wk", [C, C]), ("wv", [C, C]),
            ("mur", [1, C]), ("muk", [1, C]), ("w1r", [1, 32]),
            ("b1r", [1, 32]), ("w2r", [32, H]), ("b2r", [1, H]),
            ("gsl", [32, ETW]), ("gt4", [8, H]), ("mask0", [128, 1])):
        mk_in(name, shape)

    y = nc.dram_tensor("y", [QOWN, C], F32, kind="ExternalOutput").ap()

    xbf_scr = nc.dram_tensor("xbf_scr", [T, C], BF16)
    xhi_scr = nc.dram_tensor("xhi_scr", [QR, C], BF16)
    xlo_scr = nc.dram_tensor("xlo_scr", [QR, C], BF16)
    k_scr = nc.dram_tensor("k_scr", [C, T], BF16)
    v_scr = nc.dram_tensor("v_scr", [T, C], BF16)
    etab_scr = nc.dram_tensor("etab_scr", [H, ETW], BF16)
    rep_scr = nc.dram_tensor("rep_scr", [128, H * BUFW], BF16)
    o_scr = nc.dram_tensor("o_scr", [C, QR], BF16)
    q_scr = nc.dram_tensor("q_scr", [C, QR], BF16)
    wT_scr = [nc.dram_tensor(f"w{n}T_scr", [C, C], BF16) for n in "rkv"]
    x2h_scr = nc.dram_tensor("x2h_scr", [C, QOWN], BF16)
    x2l_scr = nc.dram_tensor("x2l_scr", [C, QOWN], BF16)
    vec_scr = nc.dram_tensor("vec_scr", [4, 16], F32)

    with ExitStack() as ctx:
        E = ctx.enter_context
        B = Builder(nc, ctx)
        B.sem("s_ld", "s_tp", "s_sc", "s_pe", "s_ve", "s_ae", "s_g",
              "s_smm", "s_exp", "s_mul", "s_pv", "s_nrm", "s_bc",
              "s_g96", "s_ost", "s_y", "s_rep", "s_xa", "s_xb", "s_xo")
        w, op, dma = B.w, B.op, B.dma
        sv = B.sv

        onesl = E(nc.sbuf_tensor("onesl", [128, 1], BF16))
        B.fence_sb = E(nc.sbuf_tensor("fence_sb", [1, 4], F32))
        B.op("vector", lambda e: e.memset(B.fence_sb[:, :], 0.0), ("s_ve", 1))
        onesP = E(nc.sbuf_tensor("onesP", [128, 128], F32))
        onesPb = E(nc.sbuf_tensor("onesPb", [1, 128], BF16))
        onesc = E(nc.sbuf_tensor("onesc", [128, 1], F32))
        bias24 = E(nc.sbuf_tensor("bias24", [128, 24], F32))
        bo8 = E(nc.sbuf_tensor("bo8", [128, 8], F32))
        mur8 = E(nc.sbuf_tensor("mur8", [128, 8], F32))
        muk8 = E(nc.sbuf_tensor("muk8", [128, 8], F32))
        mur8i = E(nc.sbuf_tensor("mur8i", [128, 8], F32))
        muk8i = E(nc.sbuf_tensor("muk8i", [128, 8], F32))
        mask_sb = E(nc.sbuf_tensor("mask_sb", [128, 1], F32))
        bvrow = E(nc.sbuf_tensor("bvrow", [1, C], BF16))
        w1c = E(nc.sbuf_tensor("w1c", [32, 1], F32))
        b1c = E(nc.sbuf_tensor("b1c", [32, 1], F32))
        w2_sb = E(nc.sbuf_tensor("w2_sb", [32, H], F32))
        b2c = E(nc.sbuf_tensor("b2c", [16, 1], F32))
        gt4_sb = E(nc.sbuf_tensor("gt4_sb", [8, H], F32))
        qmax2 = E(nc.sbuf_tensor("qmax2", [1, H], F32))
        kmax2 = E(nc.sbuf_tensor("kmax2", [1, 1], F32))
        t11 = E(nc.sbuf_tensor("t11", [1, 16], F32))
        mh_row = E(nc.sbuf_tensor("mh_row", [1, H], F32))
        mhg_row = E(nc.sbuf_tensor("mhg_row", [1, 8], F32))
        nmhg_row = E(nc.sbuf_tensor("nmhg_row", [1, 8], F32))
        mh_col = E(nc.sbuf_tensor("mh_col", [16, 1], F32))
        mhg_col4 = E(nc.sbuf_tensor("mhg_col4", [8, 1], F32))
        dcol = E(nc.sbuf_tensor("dcol", [16, 1], F32))
        corr_col = E(nc.sbuf_tensor("corr_col", [16, 1], F32))
        corr_row = E(nc.sbuf_tensor("corr_row", [1, 16], F32))
        corrE = E(nc.sbuf_tensor("corrE", [128, 16], F32))
        mbg = E(nc.sbuf_tensor("mbg", [128, 8], F32))
        rl96 = E(nc.sbuf_tensor("rl96", [1, 512], F32))
        sqt = E(nc.sbuf_tensor("sqt", [128, 512], BF16))
        wtmp = E(nc.sbuf_tensor("wtmp", [128, 1024], BF16))

        psA = E(nc.psum_tensor("psA", [128, 2048], F32))
        psB = E(nc.psum_tensor("psB", [128, 512], F32))
        psC = E(nc.psum_tensor("psC", [128, 512], F32))
        psD = E(nc.psum_tensor("psD", [128, 512], F32))
        psE = E(nc.psum_tensor("psE", [128, 512], F32))

        for t_, v_ in ((onesl, 1.0), (onesP, 1.0), (onesPb, 1.0),
                       (onesc, 1.0), (qmax2, 0.0), (kmax2, 0.0)):
            op("vector", lambda e, t=t_, v=v_: e.memset(t[:, :], v), ("s_ve", 1))

        bv32 = E(nc.sbuf_tensor("bv32", [1, C], F32))
        dma("sync", "s_tp", bias24[:, :], AP(din["bqkv"], 0, [[1, 128], [128, 24]]), slow=True)
        dma("sync", "s_tp", bo8[:, :], AP(din["bout"], 0, [[1, 128], [128, 8]]), slow=True)
        dma("sync", "s_tp", mur8[:, :], AP(din["mur"], 0, [[1, 128], [128, 8]]), slow=True)
        dma("sync", "s_tp", muk8[:, :], AP(din["muk"], 0, [[1, 128], [128, 8]]), slow=True)
        dma("sync", "s_tp", mask_sb[:, :], din["mask0"].ap())
        dma("sync", "s_tp", bv32[:, :], din["bqkv"].ap()[0:1, 2048:3072])
        dma("sync", "s_tp", w1c[:, :], AP(din["w1r"], 0, [[1, 32], [1, 1]]))
        dma("sync", "s_tp", b1c[:, :], AP(din["b1r"], 0, [[1, 32], [1, 1]]))
        dma("sync", "s_tp", w2_sb[:, :], din["w2r"].ap())
        dma("sync", "s_tp", b2c[:, :], AP(din["b2r"], 0, [[1, 16], [1, 1]]))
        dma("sync", "s_tp", gt4_sb[:, :], din["gt4"].ap())
        ld_small = sv["s_tp"]

        w("vector", "s_tp", ld_small)
        op("vector", lambda e: e.tensor_scalar(bias24[:, 0:8], bias24[:, 0:8],
           0.125, None, ALU.mult), ("s_ve", 1))
        op("vector", lambda e: e.tensor_scalar(mur8i[:, :], mur8[:, :],
           -1.0, 1.0, ALU.mult, ALU.add), ("s_ve", 1))
        op("vector", lambda e: e.tensor_scalar(muk8i[:, :], muk8[:, :],
           -1.0, 1.0, ALU.mult, ALU.add), ("s_ve", 1))
        op("vector", lambda e: e.tensor_copy(bvrow[:, :], bv32[:, :]),
           ("s_ve", 1))

        with ExitStack() as ctx1:
            E1 = ctx1.enter_context
            wq_sb = E1(nc.sbuf_tensor("wq_sb", [128, 8 * 3072], BF16))

            for cc in range(8):
                dma("gpsimd", "s_ld", wq_sb[:, 3072 * cc:3072 * (cc + 1)],
                    din["wqkv"].ap()[128 * cc:128 * (cc + 1), :])
            ld_wq = sv["s_ld"]
            w("vector", "s_ld", ld_wq)
            for cc in range(8):
                op("vector", lambda e, cc=cc: e.tensor_scalar(
                    wq_sb[:, 3072 * cc:3072 * cc + 1024],
                    wq_sb[:, 3072 * cc:3072 * cc + 1024], 0.125, None,
                    ALU.mult), ("s_ve", 1))
            ve_wq = sv["s_ve"]

            xstg = [E1(nc.sbuf_tensor(f"xstg{i}", [128, 1024], BF16))
                    for i in range(4)]

            with ExitStack() as ctx1a:
                E1a = ctx1a.enter_context
                g_sb = E1a(nc.sbuf_tensor("g_sb", [32, ETW], F32))
                h_sb = E1a(nc.sbuf_tensor("h_sb", [32, ETW], F32))
                etab_sb = E1a(nc.sbuf_tensor("etab_sb", [16, ETW], BF16))
                dma("sync", "s_tp", g_sb[:, :], din["gsl"].ap())
                ld_g = sv["s_tp"]
                w("vector", "s_tp", ld_g)
                op("vector", lambda e, g_sb=g_sb, h_sb=h_sb: e.tensor_scalar(
                    h_sb[:, :], g_sb[:, :], w1c[:, 0:1], None, ALU.mult),
                    ("s_ve", 1))
                w("scalar", "s_ve", sv["s_ve"])
                op("scalar", lambda e, h_sb=h_sb: e.activation(
                    h_sb[:, :], h_sb[:, :], AF.Relu, bias=b1c[:, 0:1],
                    scale=1.0), ("s_ae", 1))
                w("tensor", "s_ae", sv["s_ae"])
                w("tensor", "s_tp", ld_small)
                nch = (ETW + 511) // 512
                for chi in range(nch):
                    c0 = 512 * chi
                    cw = min(512, ETW - c0)
                    bank = psD if chi % 2 == 0 else psE
                    if chi >= 2:
                        w("tensor", "s_ae", sv["s_ae"] - 1)
                    op("tensor", lambda e, c0=c0, cw=cw, h_sb=h_sb, bank=bank:
                       e.matmul(bank[0:16, 0:cw], w2_sb[:, :],
                                h_sb[:, c0:c0 + cw], start=True, stop=True),
                       ("s_pe", 1))
                    w("scalar", "s_pe", sv["s_pe"])
                    op("scalar", lambda e, c0=c0, cw=cw, g_sb=g_sb, bank=bank:
                       e.activation(g_sb[0:16, c0:c0 + cw], bank[0:16, 0:cw],
                                    AF.Exp, bias=b2c[:, 0:1], scale=1.0),
                       ("s_ae", 1))
                w("vector", "s_ae", sv["s_ae"])
                op("vector", lambda e, g_sb=g_sb, etab_sb=etab_sb:
                   e.tensor_copy(etab_sb[:, :], g_sb[0:16, :]), ("s_ve", 1))
                w("vector", "s_ve")
                op("vector", lambda e, etab_sb=etab_sb: e.tensor_copy(
                    etab_sb[0:1, 0:1], etab_sb[0:1, 0:1]), ("s_ve", 1))
                ve_etab = sv["s_ve"]
                if DEBUG_TAPS >= 1:
                    tap_h = nc.dram_tensor("tap_h", [32, ETW], F32,
                                           kind="ExternalOutput").ap()
                if DEBUG_TAPS in (2, 3):
                    w("sync", "s_ae", sv["s_ae"])
                    dma("sync", "s_y", tap_h, h_sb[:, :])

            xo32 = E1(nc.sbuf_tensor("xo32", [128, 1024], F32))
            xlo32 = E1(nc.sbuf_tensor("xlo32", [128, 1024], F32))
            xhib = [E1(nc.sbuf_tensor(f"xhib{i}", [128, 1024], BF16))
                    for i in range(2)]
            xlob = [E1(nc.sbuf_tensor(f"xlob{i}", [128, 1024], BF16))
                    for i in range(2)]
            xT = [[E1(nc.sbuf_tensor(f"xT{b_}_{cc}", [128, 2048], BF16))
                   for cc in range(8)] for b_ in range(2)]
            ktile = [E1(nc.sbuf_tensor(f"ktile{i}", [128, 512], BF16))
                     for i in range(2)]
            vtile = [E1(nc.sbuf_tensor(f"vtile{i}", [128, 1024], BF16))
                     for i in range(2)]
            qstg = [E1(nc.sbuf_tensor(f"qstg{i}", [128, 512], BF16))
                    for i in range(2)]

            xb_st = []
            for t_ in range(64):
                if t_ >= 4:
                    w("gpsimd", "s_xb", xb_st[t_ - 4])
                dma("gpsimd", "s_xa", xstg[t_ % 4][:, :],
                    din["xf"].ap()[128 * t_:128 * (t_ + 1), :])
                w("sync", "s_xa", sv["s_xa"])
                dma("sync", "s_xb",
                    xbf_scr.ap()[128 * t_:128 * (t_ + 1), :],
                    xstg[t_ % 4][:, :])
                xb_st.append(sv["s_xb"])
            xb_mark = {tg: ("s_xb", xb_st[16 * (tg + 1) - 1])
                       for tg in range(4)}

            w("scalar", "s_ve", ve_etab)
            dma("scalar", "s_sc", etab_scr.ap(), etab_sb[:, :])
            sc_etab = sv["s_sc"]
            w("scalar", "s_sc", sc_etab)
            for p_ in range(128):
                dma("scalar", "s_rep", rep_scr.ap()[p_:p_ + 1, :],
                    AP(etab_scr, 127 - p_, [[ETW, H], [1, BUFW]]))
            rep_done = sv["s_rep"]

            kevK = []
            kdma = []
            vevV = []
            vdma = []
            ki = 0
            vi = 0
            pe_end_tg = []

            def emit_xt(tg):
                tb = tg % 2
                w("sync", xb_mark[tg][0], xb_mark[tg][1])
                if tg < 2:
                    w("sync", "s_sc", sc_etab)
                if tg >= 2:
                    w("sync", "s_pe", pe_end_tg[tg - 2])
                for cc in range(8):
                    dma("sync", "s_tp", xT[tb][cc][:, :],
                        xbf_scr.ap()[2048 * tg:2048 * (tg + 1),
                                     128 * cc:128 * (cc + 1)], transpose=True)
                return sv["s_tp"]

            tp_list = [emit_xt(0), emit_xt(1)]

            xo_done = []
            w("sync", "s_sc", sc_etab)
            for t_ in range(17):
                b_ = t_ % 2
                if t_ >= 1:
                    w("sync", "s_ve", xo_done[t_ - 1][0])
                dma("sync", "s_tp", xo32[:, :],
                    din["xo"].ap()[128 * t_:128 * (t_ + 1), :])
                tp_t = sv["s_tp"]
                if t_ == 0:
                    w("gpsimd", "s_sc", sc_etab)
                if t_ >= 1:
                    w("gpsimd", "s_ve", xo_done[t_ - 1][0])
                if t_ >= 2:
                    w("gpsimd", "s_xo", xo_done[t_ - 2][1])
                dma("gpsimd", "s_xa", xhib[b_][:, :],
                    din["xo"].ap()[128 * t_:128 * (t_ + 1), :])
                ld_t = sv["s_xa"]
                w("vector", "s_tp", tp_t)
                w("vector", "s_xa", ld_t)
                if t_ >= 2:
                    w("vector", "s_xo", xo_done[t_ - 2][1])
                op("vector", lambda e, b_=b_: e.tensor_sub(
                    xlo32[:, :], xo32[:, :], xhib[b_][:, :]), ("s_ve", 1))
                w("vector", "s_ve")
                op("vector", lambda e, b_=b_: e.tensor_copy(
                    xlob[b_][:, :], xlo32[:, :]), ("s_ve", 1))
                B.vfence()
                ve_t = sv["s_ve"]
                w("sync", "s_xa", ld_t)
                dma("sync", "s_xo",
                    xhi_scr.ap()[128 * t_:128 * (t_ + 1), :], xhib[b_][:, :])
                w("sync", "s_ve", ve_t)
                dma("sync", "s_xo",
                    xlo_scr.ap()[128 * t_:128 * (t_ + 1), :], xlob[b_][:, :])
                xo_done.append((ve_t, sv["s_xo"]))
            xo_all = sv["s_xo"]

            for tg in range(4):
                tb = tg % 2
                tp_xt = tp_list[tg]
                w("tensor", "s_tp", tp_xt)
                if tg == 0:
                    w("tensor", "s_ve", ve_wq)
                    w("tensor", "s_ld", ld_wq)
                if vevV:
                    w("tensor", "s_ve", vevV[-1])
                for nb in range(4):
                    for mt in range(8):
                        hb = (ki % 2)
                        bank = psA[:, 1024 + 512 * hb:1536 + 512 * hb]
                        if ki >= 2:
                            w("tensor", "s_ve", kevK[ki - 2])
                        for cc in range(8):
                            op("tensor", lambda e, bank=bank, cc=cc, mt=mt,
                               nb=nb, tb=tb: e.matmul(
                                   bank,
                                   wq_sb[:, 3072 * cc + 1024 + 128 * mt:
                                         3072 * cc + 1024 + 128 * (mt + 1)],
                                   xT[tb][cc][:, 512 * nb:512 * (nb + 1)],
                                   start=(cc == 0), stop=(cc == 7)),
                               *([("s_pe", 1)] if cc == 7 else []))
                        pe_k = sv["s_pe"]
                        kb = ktile[ki % 2]
                        w("vector", "s_pe", pe_k)
                        if ki >= 2:
                            w("vector", "s_sc", kdma[ki - 2])
                        op("vector", lambda e, kb=kb, bank=bank, mt=mt:
                           e.tensor_scalar(kb[:, :], bank,
                                           bias24[:, 8 + mt:9 + mt],
                                           None, ALU.add), ("s_ve", 1))
                        if ki >= 1:
                            w("vector", "s_pe", sv["s_pe"])
                        w("vector", "s_ve")
                        op("vector", lambda e, kb=kb: e.tensor_mul(
                            sqt[:, :], kb[:, :], kb[:, :]), ("s_ve", 1))
                        kevK.append(sv["s_ve"])
                        B.vfence()
                        w("tensor", "s_ve", sv["s_ve"])
                        op("tensor", lambda e, mt=mt: e.matmul(
                            psD[0:1, 0:512], onesl[:, 0:1], sqt[:, :],
                            start=(mt == 0), stop=(mt == 7)), ("s_pe", 1))
                        w("gpsimd", "s_ve")
                        dma("gpsimd", "s_sc",
                            k_scr.ap()[128 * mt:128 * (mt + 1),
                                       2048 * tg + 512 * nb:
                                       2048 * tg + 512 * (nb + 1)], kb[:, :])
                        kdma.append(sv["s_sc"])
                        ki += 1
                    pe_kn = sv["s_pe"]
                    w("vector", "s_pe", pe_kn)
                    op("vector", lambda e: e.reduce_max(
                        t11[0:1, 0:1], psD[0:1, 0:512], axis=AX.X), ("s_ve", 1))
                    w("vector", "s_ve")
                    op("vector", lambda e: e.tensor_max(
                        kmax2[:, :], kmax2[:, :], t11[0:1, 0:1]), ("s_ve", 1))
                    w("tensor", "s_ve", sv["s_ve"])
                for st in range(16):
                    vb = vtile[vi % 2]
                    vbase = 1024 * (vi % 2)
                    if vi == 1:
                        w("tensor", "s_ve", kevK[-1])
                    if vi >= 2:
                        w("tensor", "s_ve", vevV[vi - 2])
                    for half in range(2):
                        op("tensor", lambda e, half=half, vbase=vbase:
                           e.matmul(
                               psA[:, vbase + 512 * half:
                                   vbase + 512 * (half + 1)],
                               onesPb[0:1, 0:128],
                               bvrow[0:1, 512 * half:512 * (half + 1)],
                               start=True, stop=True))
                    for cc in range(8):
                        for half in range(2):
                            op("tensor", lambda e, cc=cc, st=st, half=half,
                               tb=tb, vbase=vbase: e.matmul(
                                   psA[:, vbase + 512 * half:
                                       vbase + 512 * (half + 1)],
                                   xT[tb][cc][:, 128 * st:128 * (st + 1)],
                                   wq_sb[:, 3072 * cc + 2048 + 512 * half:
                                         3072 * cc + 2048 + 512 * (half + 1)],
                                   start=False, stop=(cc == 7)),
                               *([("s_pe", 1)] if cc == 7 and half == 1
                                 else []))
                    pe_v = sv["s_pe"]
                    w("vector", "s_pe", pe_v)
                    if vi >= 2:
                        w("vector", "s_sc", vdma[vi - 2])
                    op("vector", lambda e, vb=vb, vbase=vbase: e.tensor_copy(
                        vb[:, :], psA[:, vbase:vbase + 1024]), ("s_ve", 1))
                    vevV.append(sv["s_ve"])
                    B.vfence()
                    w("gpsimd", "s_ve")
                    dma("gpsimd", "s_sc",
                        v_scr.ap()[2048 * tg + 128 * st:
                                   2048 * tg + 128 * (st + 1), :], vb[:, :])
                    vdma.append(sv["s_sc"])
                    vi += 1
                pe_end_tg.append(sv["s_pe"])
                if tg + 2 < 4:
                    tp_list.append(emit_xt(tg + 2))
            sc_kv = sv["s_sc"]

            xoT = [E1(nc.sbuf_tensor(f"xoT{cc}", [128, QR], BF16))
                   for cc in range(8)]
            w("sync", "s_xo", xo_all)
            w("sync", "s_pe", sv["s_pe"])
            for cc in range(8):
                dma("sync", "s_tp", xoT[cc][:, :],
                    xhi_scr.ap()[:, 128 * cc:128 * (cc + 1)], transpose=True)
            tp_xo = sv["s_tp"]
            w("tensor", "s_tp", tp_xo)
            qev = []
            qdma = []
            qi = 0
            for mt in range(8):
                for nb in range(5):
                    q0, qw = QB0[nb], QBS[nb]
                    hb = qi % 2
                    qs = qstg[qi % 2]
                    if qi >= 2:
                        w("tensor", "s_ve", qev[qi - 2])
                    for cc in range(8):
                        op("tensor", lambda e, hb=hb, cc=cc, mt=mt, q0=q0,
                           qw=qw: e.matmul(
                               psA[:, 1024 + 512 * hb:1024 + 512 * hb + qw],
                               wq_sb[:, 3072 * cc + 128 * mt:
                                     3072 * cc + 128 * (mt + 1)],
                               xoT[cc][:, q0:q0 + qw],
                               start=(cc == 0), stop=(cc == 7)),
                           *([("s_pe", 1)] if cc == 7 else []))
                    pe_q = sv["s_pe"]
                    w("vector", "s_pe", pe_q)
                    if qi >= 2:
                        w("vector", "s_sc", qdma[qi - 2])
                    op("vector", lambda e, hb=hb, qs=qs, mt=mt, qw=qw:
                       e.tensor_scalar(
                           qs[:, 0:qw],
                           psA[:, 1024 + 512 * hb:1024 + 512 * hb + qw],
                           bias24[:, mt:mt + 1], None, ALU.add), ("s_ve", 1))
                    B.vfence()
                    ve_qe = sv["s_ve"]
                    w("gpsimd", "s_ve", ve_qe)
                    dma("gpsimd", "s_sc",
                        q_scr.ap()[128 * mt:128 * (mt + 1), q0:q0 + qw],
                        qs[:, 0:qw])
                    qdma.append(sv["s_sc"])
                    w("vector", "s_pe", sv["s_pe"])
                    w("vector", "s_ve")
                    op("vector", lambda e, qs=qs, qw=qw: e.tensor_mul(
                        sqt[:, 0:qw], qs[:, 0:qw], qs[:, 0:qw]), ("s_ve", 1))
                    qev.append(sv["s_ve"])
                    w("tensor", "s_ve", sv["s_ve"])
                    op("tensor", lambda e, qw=qw: e.matmul(
                        psD[0:1, 0:qw], onesl[0:64, 0:1], sqt[0:64, 0:qw],
                        start=True, stop=True), ("s_pe", 1))
                    op("tensor", lambda e, qw=qw: e.matmul(
                        psE[0:1, 0:qw], onesl[64:128, 0:1], sqt[64:128, 0:qw],
                        start=True, stop=True), ("s_pe", 1))
                    pe_qn = sv["s_pe"]
                    w("vector", "s_pe", pe_qn)
                    op("vector", lambda e, qw=qw: e.reduce_max(
                        t11[0:1, 0:1], psD[0:1, 0:qw], axis=AX.X), ("s_ve", 1))
                    op("vector", lambda e, qw=qw: e.reduce_max(
                        t11[0:1, 1:2], psE[0:1, 0:qw], axis=AX.X), ("s_ve", 1))
                    w("vector", "s_ve")
                    op("vector", lambda e, mt=mt: e.tensor_max(
                        qmax2[0:1, 2 * mt:2 * mt + 1],
                        qmax2[0:1, 2 * mt:2 * mt + 1], t11[0:1, 0:1]),
                        ("s_ve", 1))
                    op("vector", lambda e, mt=mt: e.tensor_max(
                        qmax2[0:1, 2 * mt + 1:2 * mt + 2],
                        qmax2[0:1, 2 * mt + 1:2 * mt + 2], t11[0:1, 1:2]),
                        ("s_ve", 1))
                    qi += 1

            w("scalar", "s_ve", sv["s_ve"])
            op("scalar", lambda e: e.activation(mh_row[:, :], qmax2[:, :],
               AF.Sqrt), ("s_ae", 1))
            op("scalar", lambda e: e.activation(t11[0:1, 0:1], kmax2[:, :],
               AF.Sqrt), ("s_ae", 1))
            w("vector", "s_ae", sv["s_ae"])
            op("vector", lambda e: e.tensor_scalar(mh_row[:, :], mh_row[:, :],
               t11[0:1, 0:1], 0.5, ALU.mult, ALU.add), ("s_ve", 1))
            w("vector", "s_ve")
            op("vector", lambda e: e.reduce_max(
                mhg_row[:, :],
                mh_row[0:1, 0:16].rearrange("p (g h) -> p g h", g=8),
                axis=AX.X), ("s_ve", 1))
            w("vector", "s_ve")
            op("vector", lambda e: e.tensor_scalar(nmhg_row[:, :],
               mhg_row[:, :], -1.0, None, ALU.mult), ("s_ve", 1))
            B.vfence()
            ve_m = sv["s_ve"]
            w("gpsimd", "s_ve", ve_m)
            dma("gpsimd", "s_g", vec_scr.ap()[0:1, 0:16], mh_row[0:1, :])
            dma("gpsimd", "s_g", vec_scr.ap()[1:2, 0:8], mhg_row[0:1, :])
            w("gpsimd", "s_g", sv["s_g"])
            dma("gpsimd", "s_g", mh_col[:, :],
                AP(vec_scr, 0, [[1, 16], [1, 1]]))
            dma("gpsimd", "s_g", mhg_col4[:, :],
                AP(vec_scr, 16, [[1, 8], [1, 1]]))
            w("tensor", "s_g", sv["s_g"])
            op("tensor", lambda e: e.matmul(psD[0:16, 0:1], gt4_sb[:, :],
               mhg_col4[:, :], start=True, stop=True), ("s_pe", 1))
            w("vector", "s_pe", sv["s_pe"])
            op("vector", lambda e: e.tensor_sub(dcol[:, :], psD[0:16, 0:1],
               mh_col[:, :]), ("s_ve", 1))
            w("scalar", "s_ve", sv["s_ve"])
            op("scalar", lambda e: e.activation(corr_col[:, :], dcol[:, :],
               AF.Exp), ("s_ae", 1))
            B.sfence()
            w("gpsimd", "s_ae", sv["s_ae"])
            dma("gpsimd", "s_g", vec_scr.ap()[2:3, 0:16],
                corr_col[:, 0:1])
            w("gpsimd", "s_g", sv["s_g"])
            dma("gpsimd", "s_g", corr_row[0:1, :], vec_scr.ap()[2:3, 0:16])
            g_c = sv["s_g"]
            w("tensor", "s_ve", ve_m)
            for g_ in range(8):
                op("tensor", lambda e, g_=g_: e.matmul(
                    psE[:, g_:g_ + 1], onesP[0:1, 0:128],
                    nmhg_row[0:1, g_:g_ + 1], start=True, stop=True),
                    *([("s_pe", 1)] if g_ == 7 else []))
            w("tensor", "s_g", g_c)
            for hh in range(16):
                op("tensor", lambda e, hh=hh: e.matmul(
                    psD[:, hh:hh + 1], onesP[0:1, 0:128],
                    corr_row[0:1, hh:hh + 1], start=True, stop=True),
                    *([("s_pe", 1)] if hh == 15 else []))
            w("vector", "s_pe", sv["s_pe"])
            op("vector", lambda e: e.tensor_copy(mbg[:, :], psE[:, 0:8]),
               ("s_ve", 1))
            op("vector", lambda e: e.tensor_copy(corrE[:, :], psD[:, 0:16]),
               ("s_ve", 1))
            ve_mset = sv["s_ve"]

            p1_sc_done = sv["s_sc"]
            for wsrc, wscr in ((din["wr"], wT_scr[0]), (din["wk"], wT_scr[1]),
                               (din["wv"], wT_scr[2])):
                for rr in range(8):
                    dma("gpsimd", "s_ld", wtmp[:, :],
                        wsrc.ap()[128 * rr:128 * (rr + 1), :])
                    w("gpsimd", "s_ld", sv["s_ld"])
                    dma("gpsimd", "s_sc",
                        wscr.ap()[128 * rr:128 * (rr + 1), :], wtmp[:, :])
                    w("gpsimd", "s_sc", sv["s_sc"])
            ffn_bounce_done = sv["s_sc"]

        with ExitStack() as ctx2:
            E2 = ctx2.enter_context
            kTp = E2(nc.sbuf_tensor("kTp", [128, 2 * T], BF16))
            qTh = E2(nc.sbuf_tensor("qTh", [128, 2 * QR], BF16))
            vS = E2(nc.sbuf_tensor("vS", [128, 2 * NSC * 128], BF16))
            bufs = E2(nc.sbuf_tensor("bufs", [128, 4 * BUFW], BF16))
            pT = [E2(nc.sbuf_tensor(f"pT{i}", [128, 1024], BF16))
                  for i in range(3)]
            pm = [E2(nc.sbuf_tensor(f"pm{i}", [128, 1024], BF16))
                  for i in range(3)]
            rlb = E2(nc.sbuf_tensor("rlb", [128, 512], F32))
            opv = E2(nc.sbuf_tensor("opv", [128, 512], F32))
            otst = [E2(nc.sbuf_tensor(f"otst{i}", [128, 512], BF16))
                    for i in range(2)]

            git = 0
            ost_list = []
            pend_pv = []
            pair_end = {}

            def flush_one():
                fn, wv = pend_pv.pop(0)
                w("tensor", "s_mul", wv)
                fn()

            def flush_pv(keep=0):
                while len(pend_pv) > keep:
                    flush_one()

            def stream_pair(pr):
                sl = pr % 2
                if pr >= 2:
                    smm_e, pv_e, mul_e = pair_end[pr - 2]
                    w("sync", "s_smm", smm_e)
                    w("sync", "s_pv", pv_e)
                    w("sync", "s_mul", mul_e)
                else:
                    w("sync", "s_sc", p1_sc_done)
                    w("sync", "s_rep", rep_done)
                    w("sync", "s_ve", ve_mset)
                dma("sync", "s_tp", kTp[:, T * sl:T * (sl + 1)],
                    k_scr.ap()[128 * pr:128 * (pr + 1), :])
                dma("sync", "s_tp", qTh[:, QR * sl:QR * (sl + 1)],
                    q_scr.ap()[128 * pr:128 * (pr + 1), :])
                dma("sync", "s_tp",
                    vS[:, 8192 * sl:8192 * (sl + 1)].rearrange(
                        "p (n m) -> p n m", n=NSC),
                    AP(v_scr, 128 * pr, [[1024, 128], [131072, NSC],
                                         [1, 128]]))
                for h2 in range(2):
                    dma("sync", "s_tp",
                        bufs[:, BUFW * (2 * sl + h2):
                             BUFW * (2 * sl + h2 + 1)],
                        rep_scr.ap()[:, BUFW * (2 * pr + h2):
                                     BUFW * (2 * pr + h2 + 1)])
                return sv["s_tp"]

            tp_pair = {0: stream_pair(0), 1: stream_pair(1)}
            for pr in range(8):
                sl = pr % 2
                tp_hg = tp_pair[pr]
                w("vector", "s_tp", tp_hg)
                if pr == 0:
                    w("vector", "s_ve", ve_mset)
                for h2 in range(2):
                    op("vector", lambda e, h2=h2, sl=sl, pr=pr:
                       e.tensor_scalar(
                           bufs[:, BUFW * (2 * sl + h2):
                                BUFW * (2 * sl + h2 + 1)],
                           bufs[:, BUFW * (2 * sl + h2):
                                BUFW * (2 * sl + h2 + 1)],
                           corrE[:, 2 * pr + h2:2 * pr + h2 + 1], None,
                           ALU.mult))
                for qb in range(5):
                    q0, qw = QB0[qb], QBS[qb]
                    for sc in range(NSC):
                        j = git
                        jb = j % 2
                        if j >= 2:
                            w("tensor", "s_exp", j - 1)
                        if sc == 0:
                            w("tensor", "s_tp", tp_hg)
                            if qb == 0 and pr == 0:
                                w("tensor", "s_ve", ve_mset)
                            if qb > 0 or pr > 0:
                                w("tensor", "s_nrm", sv["s_nrm"])
                        for h2 in range(2):
                            r0 = 64 * h2
                            op("tensor", lambda e, h2=h2, r0=r0, sl=sl,
                               sc=sc, q0=q0, qw=qw, jb=jb: e.matmul(
                                   psA[:, 1024 * jb + 512 * h2:
                                       1024 * jb + 512 * h2 + qw],
                                   kTp[r0:r0 + 64, T * sl + 128 * sc:
                                       T * sl + 128 * (sc + 1)],
                                   qTh[r0:r0 + 64, QR * sl + q0:
                                       QR * sl + q0 + qw],
                                   start=True, stop=True,
                                   tile_position=(r0, 0)),
                               *([("s_smm", 1)] if h2 == 1 else []))
                        flush_pv(keep=1)
                        if j == 0:
                            w("scalar", "s_ve", ve_mset)
                        w("scalar", "s_smm", j + 1)
                        if j >= 3:
                            w("scalar", "s_mul", j - 2)
                        op("scalar", lambda e, j=j, qw=qw, pr=pr, jb=jb:
                           e.activation(
                               pT[j % 3][:, :].rearrange(
                                   "p (h q) -> p h q", h=2)[:, :, 0:qw],
                               psA[:, 1024 * jb:1024 * (jb + 1)].rearrange(
                                   "p (h q) -> p h q", h=2)[:, :, 0:qw],
                               AF.Exp, bias=mbg[:, pr:pr + 1], scale=1.0),
                           ("s_exp", 1))
                        w("vector", "s_exp", j + 1)
                        if j >= 3:
                            w("vector", "s_pv", j - 2)
                        j0 = q0 - 128 * sc + 8064
                        op("vector", lambda e, j=j, j0=j0, qw=qw, sl=sl:
                           e.tensor_mul(
                               pm[j % 3][:, :].rearrange(
                                   "p (h q) -> p h q", h=2)[:, :, 0:qw],
                               bufs[:, :].rearrange(
                                   "p (h q) -> p h q",
                                   h=4)[:, 2 * sl:2 * sl + 2, j0:j0 + qw],
                               pT[j % 3][:, :].rearrange(
                                   "p (h q) -> p h q", h=2)[:, :, 0:qw]),
                           ("s_mul", 1))
                        mul_target = sv["s_mul"]

                        def mk_pv(sc=sc, qw=qw, g=git, sl=sl):
                            def go():
                                for h2 in range(2):
                                    op("tensor", lambda e, h2=h2, sc=sc,
                                       qw=qw, g=g, sl=sl: e.matmul(
                                           psB[64 * h2:64 * (h2 + 1), 0:qw],
                                           vS[:, 8192 * sl + 128 * sc +
                                              64 * h2:8192 * sl + 128 * sc +
                                              64 * (h2 + 1)],
                                           pm[g % 3][:, 512 * h2:
                                                     512 * h2 + qw],
                                           start=(sc == 0),
                                           stop=(sc == NSC - 1),
                                           tile_position=(0, 64 * h2)))
                                for h2 in range(2):
                                    op("tensor", lambda e, h2=h2, sc=sc,
                                       qw=qw, g=g: e.matmul(
                                           psD[32 * h2:32 * h2 + 1, 0:qw],
                                           onesl[:, 0:1],
                                           pm[g % 3][:, 512 * h2:
                                                     512 * h2 + qw],
                                           start=(sc == 0),
                                           stop=(sc == NSC - 1),
                                           tile_position=(0, 32 * h2)),
                                       *([("s_pv", 1)] if h2 == 1 else []))
                            return go
                        pend_pv.append((mk_pv(), mul_target))
                        git += 1
                    flush_pv()
                    w("vector", "s_pv", sv["s_pv"])
                    w("vector", "s_bc", sv["s_bc"])
                    op("vector", lambda e, qw=qw: e.reciprocal(
                        rlb[:, 0:qw], psD[:, 0:qw]), ("s_nrm", 1))
                    op("vector", lambda e: e.tensor_copy(
                        B.fence_sb[0:1, 0:1], B.fence_sb[0:1, 1:2]),
                        ("s_nrm", 1))
                    w("tensor", "s_nrm", sv["s_nrm"])
                    for h2 in range(2):
                        trow = 32 * h2
                        op("tensor", lambda e, trow=trow, h2=h2, qw=qw:
                           e.matmul(
                               psA[64 * h2:64 * (h2 + 1), 0:qw],
                               onesP[trow:trow + 1, 0:64],
                               rlb[trow:trow + 1, 0:qw],
                               start=True, stop=True,
                               tile_position=(trow, 64 * h2)),
                           *([("s_bc", 1)] if h2 == 1 else []))
                    ot = otst[len(ost_list) % 2]
                    w("vector", "s_bc", sv["s_bc"])
                    oi = len(ost_list)
                    if oi >= 2:
                        w("vector", "s_ost", ost_list[oi - 2])
                    op("vector", lambda e, qw=qw: e.tensor_copy(
                        opv[:, 0:qw], psB[:, 0:qw]), ("s_nrm", 1))
                    w("vector", "s_nrm")
                    op("vector", lambda e, ot=ot, qw=qw: e.tensor_mul(
                        ot[:, 0:qw], opv[:, 0:qw], psA[:, 0:qw]),
                        ("s_nrm", 1))
                    op("vector", lambda e: e.tensor_copy(
                        B.fence_sb[0:1, 0:1], B.fence_sb[0:1, 1:2]),
                        ("s_nrm", 1))
                    w("sync", "s_nrm", sv["s_nrm"])
                    dma("sync", "s_ost",
                        o_scr.ap()[128 * pr:128 * (pr + 1), q0:q0 + qw],
                        ot[:, 0:qw])
                    ost_list.append(sv["s_ost"])
                pair_end[pr] = (sv["s_smm"], sv["s_pv"], sv["s_mul"])
                if pr + 2 < 8:
                    tp_pair[pr + 2] = stream_pair(pr + 2)
            last_ost = sv["s_ost"]

        x1hi = E(nc.sbuf_tensor("x1hi", [128, 8 * QR], BF16))
        x1lo = E(nc.sbuf_tensor("x1lo", [128, 8 * QR], BF16))
        stat = E(nc.sbuf_tensor("stat", [1, 2048], F32))
        sq32 = E(nc.sbuf_tensor("sq32", [128, 512], F32))
        x1f = E(nc.sbuf_tensor("x1f", [128, 512], F32))

        def emit_ln(slabs, qw, sink):
            w("tensor", "s_ve")
            w("tensor", "s_nrm")
            for cc in range(8):
                th, c0 = slabs[cc]
                op("tensor", lambda e, th=th, c0=c0, cc=cc, qw=qw: e.matmul(
                    psD[0:1, 0:qw], onesc[:, 0:1], th[:, c0:c0 + qw],
                    start=(cc == 0), stop=(cc == 7)),
                    *([("s_pe", 1)] if cc == 7 else []))
            for cc in range(8):
                th, c0 = slabs[cc]
                if cc >= 1:
                    w("vector", "s_pe", sv["s_pe"])
                op("vector", lambda e, th=th, c0=c0, qw=qw: e.tensor_mul(
                    sq32[:, 0:qw], th[:, c0:c0 + qw], th[:, c0:c0 + qw]),
                    ("s_ve", 1))
                w("tensor", "s_ve", sv["s_ve"])
                op("tensor", lambda e, cc=cc, qw=qw: e.matmul(
                    psE[0:1, 0:qw], onesc[:, 0:1], sq32[:, 0:qw],
                    start=(cc == 0), stop=(cc == 7)), ("s_pe", 1))
            w("vector", "s_pe", sv["s_pe"])
            op("vector", lambda e, qw=qw: e.tensor_scalar(
                stat[0:1, 0:qw], psD[0:1, 0:qw], 1.0 / C, None, ALU.mult),
                ("s_ve", 1))
            op("vector", lambda e, qw=qw: e.tensor_scalar(
                stat[0:1, 512:512 + qw], psE[0:1, 0:qw], 1.0 / C, None,
                ALU.mult), ("s_ve", 1))
            w("vector", "s_ve")
            op("vector", lambda e, qw=qw: e.tensor_mul(
                stat[0:1, 1024:1024 + qw], stat[0:1, 0:qw], stat[0:1, 0:qw]),
                ("s_ve", 1))
            w("vector", "s_ve")
            op("vector", lambda e, qw=qw: e.tensor_sub(
                stat[0:1, 512:512 + qw], stat[0:1, 512:512 + qw],
                stat[0:1, 1024:1024 + qw]), ("s_ve", 1))
            w("vector", "s_ve")
            op("vector", lambda e, qw=qw: e.tensor_scalar(
                stat[0:1, 512:512 + qw], stat[0:1, 512:512 + qw], 1.0, EPS,
                ALU.mult, ALU.add), ("s_ve", 1))
            w("scalar", "s_ve", sv["s_ve"])
            op("scalar", lambda e, qw=qw: e.activation(
                stat[0:1, 1024:1024 + qw], stat[0:1, 512:512 + qw], AF.Ln),
                ("s_ae", 1))
            w("scalar", "s_ae")
            op("scalar", lambda e, qw=qw: e.activation(
                stat[0:1, 1536:1536 + qw], stat[0:1, 1024:1024 + qw], AF.Exp,
                bias=0.0, scale=-0.5), ("s_ae", 1))
            w("tensor", "s_ve", sv["s_ve"])
            w("tensor", "s_ae", sv["s_ae"])
            op("tensor", lambda e, qw=qw: e.matmul(
                psD[:, 0:qw], onesP[0:1, 0:128], stat[0:1, 0:qw],
                start=True, stop=True), ("s_pe", 1))
            op("tensor", lambda e, qw=qw: e.matmul(
                psE[:, 0:qw], onesP[0:1, 0:128], stat[0:1, 1536:1536 + qw],
                start=True, stop=True), ("s_pe", 1))
            w("vector", "s_pe", sv["s_pe"])
            for cc in range(8):
                th, c0 = slabs[cc]
                op("vector", lambda e, th=th, c0=c0, qw=qw: e.tensor_sub(
                    x1f[:, 0:qw], th[:, c0:c0 + qw], psD[:, 0:qw]),
                    ("s_ve", 1))
                w("vector", "s_ve")
                op("vector", lambda e, qw=qw: e.tensor_mul(
                    x1f[:, 0:qw], x1f[:, 0:qw], psE[:, 0:qw]), ("s_ve", 1))
                w("vector", "s_ve")
                sink(cc)

        with ExitStack() as ctx3:
            E3 = ctx3.enter_context
            wo_sb = E3(nc.sbuf_tensor("wo_sb", [128, 8 * C], BF16))
            oT = [E3(nc.sbuf_tensor(f"oT{i}", [128, 8 * 512], BF16))
                  for i in range(2)]
            xhiT = [E3(nc.sbuf_tensor(f"xhiT{i}", [128, 8 * 512], BF16))
                    for i in range(2)]
            xloT = [E3(nc.sbuf_tensor(f"xloT{i}", [128, 8 * 512], BF16))
                    for i in range(2)]
            r32 = E3(nc.sbuf_tensor("r32", [128, 8 * 512], F32))

            w("gpsimd", "s_mul")
            w("gpsimd", "s_pv")
            w("gpsimd", "s_nrm")
            w("gpsimd", "s_smm")
            w("gpsimd", "s_exp")
            for cc in range(8):
                dma("gpsimd", "s_ld", wo_sb[:, C * cc:C * (cc + 1)],
                    din["wout"].ap()[128 * cc:128 * (cc + 1), :])
            ld_wo = sv["s_ld"]

            tb_pe = []
            tb_tp = []
            tb_vedone = [0] * 5

            def p3_loads(tb):
                q0, qw = QB0[tb], QBS[tb]
                bb = tb % 2
                if tb == 0:
                    w("sync", "s_ost", last_ost)
                if tb >= 2:
                    w("sync", "s_pe", tb_pe[tb - 2])
                    w("sync", "s_ve", tb_vedone[tb - 2])
                for cc in range(8):
                    dma("sync", "s_tp", oT[bb][:, 512 * cc:512 * cc + qw],
                        o_scr.ap()[128 * cc:128 * (cc + 1), q0:q0 + qw])
                    dma("sync", "s_tp", xhiT[bb][:, 512 * cc:512 * cc + qw],
                        xhi_scr.ap()[q0:q0 + qw, 128 * cc:128 * (cc + 1)],
                        transpose=True)
                    dma("sync", "s_tp", xloT[bb][:, 512 * cc:512 * cc + qw],
                        xlo_scr.ap()[q0:q0 + qw, 128 * cc:128 * (cc + 1)],
                        transpose=True)
                return sv["s_tp"]

            tb_tp = [p3_loads(0), p3_loads(1)]
            for tb in range(5):
                q0, qw = QB0[tb], QBS[tb]
                bb = tb % 2
                tp_tb = tb_tp[tb]
                w("tensor", "s_tp", tp_tb)
                if tb == 0:
                    w("tensor", "s_ld", ld_wo)
                oev = []
                for mt in range(8):
                    hb = mt % 2
                    if mt >= 2:
                        w("tensor", "s_ve", oev[mt - 2])
                    for cc in range(8):
                        op("tensor", lambda e, hb=hb, cc=cc, mt=mt, qw=qw,
                           bb=bb: e.matmul(
                               psA[:, 1024 + 512 * hb:1024 + 512 * hb + qw],
                               wo_sb[:, C * cc + 128 * mt:
                                     C * cc + 128 * (mt + 1)],
                               oT[bb][:, 512 * cc:512 * cc + qw],
                               start=(cc == 0), stop=(cc == 7)),
                           *([("s_pe", 1)] if cc == 7 else []))
                    w("vector", "s_pe", sv["s_pe"])
                    op("vector", lambda e, hb=hb, mt=mt, qw=qw:
                       e.tensor_scalar(
                           r32[:, 512 * mt:512 * mt + qw],
                           psA[:, 1024 + 512 * hb:1024 + 512 * hb + qw],
                           bo8[:, mt:mt + 1], None, ALU.add), ("s_ve", 1))
                    w("vector", "s_ve")
                    op("vector", lambda e, mt=mt, qw=qw, bb=bb: e.tensor_add(
                        r32[:, 512 * mt:512 * mt + qw],
                        r32[:, 512 * mt:512 * mt + qw],
                        xhiT[bb][:, 512 * mt:512 * mt + qw]), ("s_ve", 1))
                    w("vector", "s_ve")
                    op("vector", lambda e, mt=mt, qw=qw, bb=bb: e.tensor_add(
                        r32[:, 512 * mt:512 * mt + qw],
                        r32[:, 512 * mt:512 * mt + qw],
                        xloT[bb][:, 512 * mt:512 * mt + qw]), ("s_ve", 1))
                    oev.append(sv["s_ve"])
                def sink_x1(cc, q0=q0, qw=qw):
                    op("vector", lambda e, cc=cc, qw=qw, q0=q0: e.tensor_copy(
                        x1hi[:, QR * cc + q0:QR * cc + q0 + qw],
                        x1f[:, 0:qw]), ("s_ve", 1))
                    w("vector", "s_ve")
                    op("vector", lambda e, cc=cc, qw=qw, q0=q0: e.tensor_sub(
                        sq32[:, 0:qw], x1f[:, 0:qw],
                        x1hi[:, QR * cc + q0:QR * cc + q0 + qw]), ("s_ve", 1))
                    w("vector", "s_ve")
                    op("vector", lambda e, cc=cc, qw=qw, q0=q0: e.tensor_copy(
                        x1lo[:, QR * cc + q0:QR * cc + q0 + qw],
                        sq32[:, 0:qw]), ("s_ve", 1))
                emit_ln([(r32, 512 * cc) for cc in range(8)], qw, sink_x1)
                tb_pe.append(sv["s_pe"])
                tb_vedone[tb] = sv["s_ve"]
                if tb + 2 < 5:
                    tb_tp.append(p3_loads(tb + 2))
            for cc in range(8):
                for t_ in (x1hi, x1lo):
                    op("vector", lambda e, t_=t_, cc=cc: e.tensor_scalar(
                        t_[:, QR * cc + 127:QR * cc + 128],
                        t_[:, QR * cc + 127:QR * cc + 128],
                        mask_sb[:, 0:1], None, ALU.mult), ("s_ve", 1))
            ve_x1m = sv["s_ve"]
            p3_pe = sv["s_pe"]

        with ExitStack() as ctx4:
            E4 = ctx4.enter_context
            wrT = E4(nc.sbuf_tensor("wrT", [128, 8 * C], BF16))
            wkT = E4(nc.sbuf_tensor("wkT", [128, 8 * C], BF16))
            wvT = E4(nc.sbuf_tensor("wvT", [128, 8 * C], BF16))
            xr_sb = E4(nc.sbuf_tensor("xr_sb", [128, 8 * 512], BF16))
            xk_sb = E4(nc.sbuf_tensor("xk_sb", [128, 8 * 512], BF16))
            rb_sb = E4(nc.sbuf_tensor("rb_sb", [128, 8 * 512], BF16))
            ka_sb = E4(nc.sbuf_tensor("ka_sb", [128, 8 * 512], BF16))
            km32 = E4(nc.sbuf_tensor("km32", [128, 512], F32))
            ff32 = E4(nc.sbuf_tensor("ff32", [128, 8 * 512], F32))
            x2hi = [E4(nc.sbuf_tensor(f"x2hi{i}", [128, 512], BF16))
                    for i in range(2)]
            x2lo = [E4(nc.sbuf_tensor(f"x2lo{i}", [128, 512], BF16))
                    for i in range(2)]

            w("sync", "s_pe", p3_pe)
            w("sync", "s_ve", sv["s_ve"])
            w("sync", "s_sc", ffn_bounce_done)
            for wscr, wsb in ((wT_scr[0], wrT), (wT_scr[1], wkT),
                              (wT_scr[2], wvT)):
                for cc in range(8):
                    dma("sync", "s_tp", wsb[:, C * cc:C * (cc + 1)],
                        wscr.ap()[:, 128 * cc:128 * (cc + 1)], transpose=True)
            tp_w = sv["s_tp"]

            x2_done = []
            x2i = 0
            for fb in range(4):
                f0 = QEXT + 512 * fb
                fw = 512
                if fb == 0:
                    w("vector", "s_ve", ve_x1m)
                for cc in range(8):
                    for (dst, mu, mui) in ((xr_sb, mur8, mur8i),
                                           (xk_sb, muk8, muk8i)):
                        op("vector", lambda e, dst=dst, mu=mu, cc=cc, f0=f0,
                           fw=fw: e.tensor_scalar(
                               dst[:, 512 * cc:512 * cc + fw],
                               x1hi[:, QR * cc + f0:QR * cc + f0 + fw],
                               mu[:, cc:cc + 1], None, ALU.mult), ("s_ve", 1))
                        w("vector", "s_ve")
                        op("vector", lambda e, dst=dst, mui=mui, cc=cc, f0=f0,
                           fw=fw: e.scalar_tensor_tensor(
                               dst[:, 512 * cc:512 * cc + fw],
                               x1hi[:, QR * cc + f0 - 1:
                                    QR * cc + f0 - 1 + fw],
                               mui[:, cc:cc + 1],
                               dst[:, 512 * cc:512 * cc + fw],
                               ALU.mult, ALU.add), ("s_ve", 1))
                ve_bl = sv["s_ve"]
                w("tensor", "s_ve", ve_bl)
                if fb == 0:
                    w("tensor", "s_tp", tp_w)
                sig_ae = []
                for mt in range(8):
                    hb = mt % 2
                    if mt >= 2:
                        w("tensor", "s_ae", sig_ae[mt - 2])
                    for cc in range(8):
                        op("tensor", lambda e, hb=hb, cc=cc, mt=mt, fw=fw:
                           e.matmul(
                               psA[:, 1024 + 512 * hb:1024 + 512 * hb + fw],
                               wrT[:, C * cc + 128 * mt:
                                   C * cc + 128 * (mt + 1)],
                               xr_sb[:, 512 * cc:512 * cc + fw],
                               start=(cc == 0), stop=(cc == 7)),
                           *([("s_pe", 1)] if cc == 7 else []))
                    w("scalar", "s_pe", sv["s_pe"])
                    op("scalar", lambda e, hb=hb, mt=mt, fw=fw: e.activation(
                        rb_sb[:, 512 * mt:512 * mt + fw],
                        psA[:, 1024 + 512 * hb:1024 + 512 * hb + fw],
                        AF.Sigmoid), ("s_ae", 1))
                    sig_ae.append(sv["s_ae"])
                kp_ve = []
                for mt in range(8):
                    hb = mt % 2
                    if mt == 0:
                        w("tensor", "s_ae", sig_ae[-1])
                    if mt >= 2:
                        w("tensor", "s_ve", kp_ve[mt - 2])
                    for cc in range(8):
                        op("tensor", lambda e, hb=hb, cc=cc, mt=mt, fw=fw:
                           e.matmul(
                               psA[:, 1024 + 512 * hb:1024 + 512 * hb + fw],
                               wkT[:, C * cc + 128 * mt:
                                   C * cc + 128 * (mt + 1)],
                               xk_sb[:, 512 * cc:512 * cc + fw],
                               start=(cc == 0), stop=(cc == 7)),
                           *([("s_pe", 1)] if cc == 7 else []))
                    w("vector", "s_pe", sv["s_pe"])
                    op("vector", lambda e, hb=hb, fw=fw: e.tensor_scalar(
                        km32[:, 0:fw],
                        psA[:, 1024 + 512 * hb:1024 + 512 * hb + fw],
                        0.0, None, ALU.max), ("s_ve", 1))
                    w("vector", "s_ve")
                    op("vector", lambda e, mt=mt, fw=fw: e.tensor_mul(
                        ka_sb[:, 512 * mt:512 * mt + fw], km32[:, 0:fw],
                        km32[:, 0:fw]), ("s_ve", 1))
                    kp_ve.append(sv["s_ve"])
                fv_ve = []
                for mt in range(8):
                    hb = mt % 2
                    if mt == 0:
                        w("tensor", "s_ve", kp_ve[-1])
                    if mt >= 2:
                        w("tensor", "s_ve", fv_ve[mt - 2])
                    for cc in range(8):
                        op("tensor", lambda e, hb=hb, cc=cc, mt=mt, fw=fw:
                           e.matmul(
                               psA[:, 1024 + 512 * hb:1024 + 512 * hb + fw],
                               wvT[:, C * cc + 128 * mt:
                                   C * cc + 128 * (mt + 1)],
                               ka_sb[:, 512 * cc:512 * cc + fw],
                               start=(cc == 0), stop=(cc == 7)),
                           *([("s_pe", 1)] if cc == 7 else []))
                    w("vector", "s_pe", sv["s_pe"])
                    op("vector", lambda e, hb=hb, mt=mt, fw=fw: e.tensor_mul(
                        ff32[:, 512 * mt:512 * mt + fw],
                        rb_sb[:, 512 * mt:512 * mt + fw],
                        psA[:, 1024 + 512 * hb:1024 + 512 * hb + fw]),
                        ("s_ve", 1))
                    w("vector", "s_ve")
                    op("vector", lambda e, mt=mt, f0=f0, fw=fw: e.tensor_add(
                        ff32[:, 512 * mt:512 * mt + fw],
                        ff32[:, 512 * mt:512 * mt + fw],
                        x1hi[:, QR * mt + f0:QR * mt + f0 + fw]), ("s_ve", 1))
                    w("vector", "s_ve")
                    op("vector", lambda e, mt=mt, f0=f0, fw=fw: e.tensor_add(
                        ff32[:, 512 * mt:512 * mt + fw],
                        ff32[:, 512 * mt:512 * mt + fw],
                        x1lo[:, QR * mt + f0:QR * mt + f0 + fw]), ("s_ve", 1))
                    fv_ve.append(sv["s_ve"])
                def sink_x2(cc, fb=fb, fw=fw):
                    bi = (fb * 8 + cc) % 2
                    if fb * 8 + cc >= 2:
                        w("vector", "s_sc", x2_done[fb * 8 + cc - 2])
                    op("vector", lambda e, bi=bi, fw=fw: e.tensor_copy(
                        x2hi[bi][:, 0:fw], x1f[:, 0:fw]), ("s_ve", 1))
                    w("vector", "s_ve")
                    op("vector", lambda e, bi=bi, fw=fw: e.tensor_sub(
                        sq32[:, 0:fw], x1f[:, 0:fw], x2hi[bi][:, 0:fw]),
                        ("s_ve", 1))
                    w("vector", "s_ve")
                    op("vector", lambda e, bi=bi, fw=fw: e.tensor_copy(
                        x2lo[bi][:, 0:fw], sq32[:, 0:fw]), ("s_ve", 1))
                    B.vfence()
                    w("gpsimd", "s_ve", sv["s_ve"])
                    dma("gpsimd", "s_sc",
                        x2h_scr.ap()[128 * cc:128 * (cc + 1),
                                     512 * fb:512 * (fb + 1)],
                        x2hi[bi][:, 0:fw])
                    dma("gpsimd", "s_sc",
                        x2l_scr.ap()[128 * cc:128 * (cc + 1),
                                     512 * fb:512 * (fb + 1)],
                        x2lo[bi][:, 0:fw])
                    x2_done.append(sv["s_sc"])
                emit_ln([(ff32, 512 * cc) for cc in range(8)], fw, sink_x2)
            sc_x2 = sv["s_sc"]

        yh = [E(nc.sbuf_tensor(f"yh{i}", [128, C], BF16)) for i in range(2)]
        yl = [E(nc.sbuf_tensor(f"yl{i}", [128, C], BF16)) for i in range(2)]
        y32 = [E(nc.sbuf_tensor(f"y32{i}", [128, C], F32)) for i in range(2)]
        w("sync", "s_sc", sc_x2)
        ydma = []
        for ob in range(16):
            b_ = ob % 2
            if ob >= 2:
                w("sync", "s_y", ydma[ob - 2])
            dma("sync", "s_tp", yh[b_][:, :],
                x2h_scr.ap()[:, 128 * ob:128 * (ob + 1)], transpose=True)
            dma("sync", "s_tp", yl[b_][:, :],
                x2l_scr.ap()[:, 128 * ob:128 * (ob + 1)], transpose=True)
            tp_y = sv["s_tp"]
            w("vector", "s_tp", tp_y)
            if ob >= 2:
                w("vector", "s_y", ydma[ob - 2])
            op("vector", lambda e, b_=b_: e.tensor_add(
                y32[b_][:, :], yh[b_][:, :], yl[b_][:, :]), ("s_ve", 1))
            B.vfence()
            w("sync", "s_ve", sv["s_ve"])
            dma("sync", "s_y", y[128 * ob:128 * (ob + 1), :], y32[b_][:, :])
            ydma.append(sv["s_y"])
        if DEBUG_TAPS >= 1:
            tap = {}
            for nm, shape, dt in (
                    ("mbg", [128, 8], F32), ("corrE", [128, 16], F32),
                    ("stats", [1, 64], F32), ("etab", [16, ETW], BF16),
                    ("q", [128, QR], BF16), ("k", [128, 2048], BF16),
                    ("v", [128, 1024], BF16), ("o", [128, QR], BF16),
                    ("x1", [128, QR], BF16), ("xhi", [128, 1024], BF16),
                    ("rep", [128, 2048], BF16), ("x2h", [128, 2048], BF16),
                    ("x2l", [128, 2048], BF16), ("x1b", [128, QR], BF16),
                    ("x1lo", [128, QR], BF16)):
                tap[nm] = nc.dram_tensor("tap_" + nm, shape, dt,
                                         kind="ExternalOutput").ap()
        if DEBUG_TAPS in (3, 4):
            w("sync", "s_y", sv["s_y"])
            dma("sync", "s_y", tap["mbg"], mbg[:, :])
            dma("sync", "s_y", tap["corrE"], corrE[:, :])
            dma("sync", "s_y", tap["stats"][0:1, 0:16], qmax2[:, :])
            dma("sync", "s_y", tap["stats"][0:1, 16:17], kmax2[:, :])
            dma("sync", "s_y", tap["stats"][0:1, 17:33], mh_row[:, :])
            dma("sync", "s_y", tap["stats"][0:1, 33:41], mhg_row[:, :])
            dma("sync", "s_y", tap["stats"][0:1, 41:57], corr_row[:, :])
            dma("sync", "s_y", tap["etab"], etab_scr.ap())
            dma("sync", "s_y", tap["q"], q_scr.ap()[0:128, :])
            dma("sync", "s_y", tap["k"], k_scr.ap()[0:128, 0:2048])
            dma("sync", "s_y", tap["v"], v_scr.ap()[0:128, :])
            dma("sync", "s_y", tap["o"], o_scr.ap()[0:128, :])
            dma("sync", "s_y", tap["x1"], x1hi[:, 0:QR])
            dma("sync", "s_y", tap["xhi"], xhi_scr.ap()[0:128, :])
            dma("sync", "s_y", tap["rep"], rep_scr.ap()[0:128, 0:2048])
            dma("sync", "s_y", tap["x2h"], x2h_scr.ap()[0:128, :])
            dma("sync", "s_y", tap["x2l"], x2l_scr.ap()[0:128, :])
            dma("sync", "s_y", tap["x1b"], x1hi[:, 3 * QR:4 * QR])
            dma("sync", "s_y", tap["x1lo"], x1lo[:, 0:QR])
        w("sync", "s_y", sv["s_y"])
        w("gpsimd", "s_sc", sv["s_sc"])
        w("gpsimd", "s_g96", sv["s_g96"])
        w("scalar", "s_rep", sv["s_rep"])

        with nc.Block() as block:
            @block.sync
            def _(e):
                for fn in B.prog["sync"]:
                    fn(e)

            @block.tensor
            def _(e):
                for fn in B.prog["tensor"]:
                    fn(e)

            @block.scalar
            def _(e):
                for fn in B.prog["scalar"]:
                    fn(e)

            @block.vector
            def _(e):
                for fn in B.prog["vector"]:
                    fn(e)

            @block.gpsimd
            def _(e):
                for fn in B.prog["gpsimd"]:
                    fn(e)

    return nc


_NC_CACHE = None


def kernel(**inputs):
    global _NC_CACHE
    f32 = np.float32
    L = {k: np.asarray(v) for k, v in inputs.items()}
    x = L["L0_attn_x"]

    def g_of(delta):
        d = delta.astype(np.float64)
        return (np.sign(d) * np.log1p(np.abs(d))).astype(f32)

    in_maps = []
    for c in range(8):
        b, ts = c // 4, c % 4
        t0 = 2048 * ts
        xb = np.ascontiguousarray(x[b]).astype(f32)
        idx = (np.arange(t0 - QEXT, t0 + QOWN) % T)
        xo = np.ascontiguousarray(xb[idx])
        delta0 = t0 - QEXT - (T - 1)
        g = g_of(np.arange(delta0, delta0 + ETW))
        gsl = np.ascontiguousarray(np.broadcast_to(g, (32, ETW)))
        gt4 = np.zeros((8, H), f32)
        for gg in range(8):
            gt4[gg, 2 * gg:2 * gg + 2] = 1.0
        m0 = np.full((128, 1), 0.0 if ts == 0 else 1.0, f32)
        in_maps.append(dict(
            xf=xb, xo=xo,
            wqkv=np.ascontiguousarray(L["L0_attn_w_qkv"], f32),
            bqkv=np.ascontiguousarray(L["L0_attn_b_qkv"], f32).reshape(1, -1),
            wout=np.ascontiguousarray(L["L0_attn_w_out"], f32),
            bout=np.ascontiguousarray(L["L0_attn_b_out"], f32).reshape(1, -1),
            wr=np.ascontiguousarray(L["L0_ffn_W_r"], f32),
            wk=np.ascontiguousarray(L["L0_ffn_W_k"], f32),
            wv=np.ascontiguousarray(L["L0_ffn_W_v"], f32),
            mur=np.ascontiguousarray(L["L0_ffn_mu_r"], f32).reshape(1, -1),
            muk=np.ascontiguousarray(L["L0_ffn_mu_k"], f32).reshape(1, -1),
            w1r=np.ascontiguousarray(L["L0_attn_w1_rpe"], f32),
            b1r=np.ascontiguousarray(L["L0_attn_b1_rpe"], f32).reshape(1, -1),
            w2r=np.ascontiguousarray(L["L0_attn_w2_rpe"], f32),
            b2r=np.ascontiguousarray(L["L0_attn_b2_rpe"], f32).reshape(1, -1),
            gsl=gsl, gt4=gt4, mask0=m0,
        ))

    if _NC_CACHE is None:
        _NC_CACHE = build_nc()
    res = run_bass_kernel_spmd(_NC_CACHE, in_maps, core_ids=list(range(8)))
    out = np.zeros((2, T, C), f32)
    for c in range(8):
        b, ts = c // 4, c % 4
        out[b, 2048 * ts:2048 * (ts + 1), :] = res.results[c]["y"]
    return out


# version 28
# speedup vs baseline: 1.0545x; 1.0108x over previous
import numpy as np
from contextlib import ExitStack

import concourse.bass as bass
import concourse.mybir as mybir
from concourse.ap import AP
from concourse.bass_utils import run_bass_kernel_spmd

F32 = mybir.dt.float32
BF16 = mybir.dt.bfloat16
AF = mybir.ActivationFunctionType
ALU = mybir.AluOpType
AX = mybir.AxisListType

T = 8192
C = 1024
H = 16
QEXT = 128
QOWN = 2048
QR = QEXT + QOWN
NSC = 64
QBS = [512, 512, 512, 512, 128]
QB0 = [0, 512, 1024, 1536, 2048]
ETW = 10368
BUFW = 10240
EPS = 1e-5
DEBUG_TAPS = 0


class Builder:
    def __init__(self, nc, ctx):
        self.nc = nc
        self.ctx = ctx
        self.prog = {k: [] for k in ("sync", "tensor", "scalar", "vector",
                                     "gpsimd")}
        self.sems = {}
        self.sv = {}

    def sem(self, *names):
        for n in names:
            self.sems[n] = self.ctx.enter_context(self.nc.semaphore(n))
            self.sv[n] = 0

    def w(self, eng, sname, val=None):
        v = self.sv[sname] if val is None else val
        if v <= 0:
            return
        self.prog[eng].append(
            lambda e, s=self.sems[sname], v=v: e.wait_ge(s, v))

    def op(self, eng, fn, *incs):
        hs = []
        for (s, n) in incs:
            self.sv[s] += n
            hs.append((self.sems[s], n))

        def run(e, fn=fn, hs=hs):
            ins = fn(e)
            for h, n in hs:
                ins = ins.then_inc(h, n)
        self.prog[eng].append(run)

    def vfence(self):
        f = self.fence_sb
        self.op("vector", lambda e, f=f: e.tensor_copy(
            f[0:1, 0:1], f[0:1, 1:2]), ("s_ve", 1))

    def sfence(self):
        f = self.fence_sb
        self.op("scalar", lambda e, f=f: e.activation(
            f[0:1, 2:3], f[0:1, 3:4], mybir.ActivationFunctionType.Copy),
            ("s_ae", 1))

    def dma(self, eng, sname, out, in_, transpose=False, slow=False):
        if slow:
            self.op(eng, lambda e, o=out, i=in_:
                    e.dma_start(out=o, in_=i,
                                allow_slow_non_contiguous=True), (sname, 16))
        else:
            self.op(eng, lambda e, o=out, i=in_, t=transpose:
                    e.dma_start(out=o, in_=i, transpose=t), (sname, 16))


def build_nc():
    nc = bass.Bass("TRN2", target_bir_lowering=False)
    din = {}

    def mk_in(name, shape, dt=F32):
        h = nc.dram_tensor(name, list(shape), dt, kind="ExternalInput")
        din[name] = h
        return h

    for name, shape in (
            ("xf", [T, C]), ("xo", [QR, C]), ("wqkv", [C, 3 * C]),
            ("bqkv", [1, 3 * C]), ("wout", [C, C]), ("bout", [1, C]),
            ("wr", [C, C]), ("wk", [C, C]), ("wv", [C, C]),
            ("mur", [1, C]), ("muk", [1, C]), ("w1r", [1, 32]),
            ("b1r", [1, 32]), ("w2r", [32, H]), ("b2r", [1, H]),
            ("gsl", [32, ETW]), ("gt4", [8, H]), ("mask0", [128, 1])):
        mk_in(name, shape)

    y = nc.dram_tensor("y", [QOWN, C], F32, kind="ExternalOutput").ap()

    xbf_scr = nc.dram_tensor("xbf_scr", [T, C], BF16)
    xhi_scr = nc.dram_tensor("xhi_scr", [QR, C], BF16)
    xlo_scr = nc.dram_tensor("xlo_scr", [QR, C], BF16)
    k_scr = nc.dram_tensor("k_scr", [C, T], BF16)
    v_scr = nc.dram_tensor("v_scr", [T, C], BF16)
    etab_scr = nc.dram_tensor("etab_scr", [H, ETW], BF16)
    rep_scr = nc.dram_tensor("rep_scr", [128, H * BUFW], BF16)
    o_scr = nc.dram_tensor("o_scr", [C, QR], BF16)
    q_scr = nc.dram_tensor("q_scr", [C, QR], BF16)
    wT_scr = [nc.dram_tensor(f"w{n}T_scr", [C, C], BF16) for n in "rkv"]
    x2h_scr = nc.dram_tensor("x2h_scr", [C, QOWN], BF16)
    x2l_scr = nc.dram_tensor("x2l_scr", [C, QOWN], BF16)
    vec_scr = nc.dram_tensor("vec_scr", [4, 16], F32)

    with ExitStack() as ctx:
        E = ctx.enter_context
        B = Builder(nc, ctx)
        B.sem("s_ld", "s_tp", "s_sc", "s_pe", "s_ve", "s_ae", "s_g",
              "s_smm", "s_exp", "s_mul", "s_pv", "s_nrm", "s_bc",
              "s_g96", "s_ost", "s_y", "s_rep", "s_xa", "s_xb", "s_xo")
        w, op, dma = B.w, B.op, B.dma
        sv = B.sv

        onesl = E(nc.sbuf_tensor("onesl", [128, 1], BF16))
        B.fence_sb = E(nc.sbuf_tensor("fence_sb", [1, 4], F32))
        B.op("vector", lambda e: e.memset(B.fence_sb[:, :], 0.0), ("s_ve", 1))
        onesP = E(nc.sbuf_tensor("onesP", [128, 128], F32))
        onesPb = E(nc.sbuf_tensor("onesPb", [1, 128], BF16))
        onesc = E(nc.sbuf_tensor("onesc", [128, 1], F32))
        bias24 = E(nc.sbuf_tensor("bias24", [128, 24], F32))
        bo8 = E(nc.sbuf_tensor("bo8", [128, 8], F32))
        mur8 = E(nc.sbuf_tensor("mur8", [128, 8], F32))
        muk8 = E(nc.sbuf_tensor("muk8", [128, 8], F32))
        mur8i = E(nc.sbuf_tensor("mur8i", [128, 8], F32))
        muk8i = E(nc.sbuf_tensor("muk8i", [128, 8], F32))
        mask_sb = E(nc.sbuf_tensor("mask_sb", [128, 1], F32))
        bvrow = E(nc.sbuf_tensor("bvrow", [1, C], BF16))
        w1c = E(nc.sbuf_tensor("w1c", [32, 1], F32))
        b1c = E(nc.sbuf_tensor("b1c", [32, 1], F32))
        w2_sb = E(nc.sbuf_tensor("w2_sb", [32, H], F32))
        b2c = E(nc.sbuf_tensor("b2c", [16, 1], F32))
        gt4_sb = E(nc.sbuf_tensor("gt4_sb", [8, H], F32))
        qmax2 = E(nc.sbuf_tensor("qmax2", [1, H], F32))
        kmax2 = E(nc.sbuf_tensor("kmax2", [1, 1], F32))
        t11 = E(nc.sbuf_tensor("t11", [1, 16], F32))
        mh_row = E(nc.sbuf_tensor("mh_row", [1, H], F32))
        mhg_row = E(nc.sbuf_tensor("mhg_row", [1, 8], F32))
        nmhg_row = E(nc.sbuf_tensor("nmhg_row", [1, 8], F32))
        mh_col = E(nc.sbuf_tensor("mh_col", [16, 1], F32))
        mhg_col4 = E(nc.sbuf_tensor("mhg_col4", [8, 1], F32))
        dcol = E(nc.sbuf_tensor("dcol", [16, 1], F32))
        corr_col = E(nc.sbuf_tensor("corr_col", [16, 1], F32))
        corr_row = E(nc.sbuf_tensor("corr_row", [1, 16], F32))
        corrE = E(nc.sbuf_tensor("corrE", [128, 16], F32))
        mbg = E(nc.sbuf_tensor("mbg", [128, 8], F32))
        rl96 = E(nc.sbuf_tensor("rl96", [1, 512], F32))
        sqt = E(nc.sbuf_tensor("sqt", [128, 512], BF16))
        wtmp = E(nc.sbuf_tensor("wtmp", [128, 1024], BF16))

        psA = E(nc.psum_tensor("psA", [128, 2048], F32))
        psB = E(nc.psum_tensor("psB", [128, 512], F32))
        psC = E(nc.psum_tensor("psC", [128, 512], F32))
        psD = E(nc.psum_tensor("psD", [128, 512], F32))
        psE = E(nc.psum_tensor("psE", [128, 512], F32))

        for t_, v_ in ((onesl, 1.0), (onesP, 1.0), (onesPb, 1.0),
                       (onesc, 1.0), (qmax2, 0.0), (kmax2, 0.0)):
            op("vector", lambda e, t=t_, v=v_: e.memset(t[:, :], v), ("s_ve", 1))

        bv32 = E(nc.sbuf_tensor("bv32", [1, C], F32))
        dma("sync", "s_tp", bias24[:, :], AP(din["bqkv"], 0, [[1, 128], [128, 24]]), slow=True)
        dma("sync", "s_tp", bo8[:, :], AP(din["bout"], 0, [[1, 128], [128, 8]]), slow=True)
        dma("sync", "s_tp", mur8[:, :], AP(din["mur"], 0, [[1, 128], [128, 8]]), slow=True)
        dma("sync", "s_tp", muk8[:, :], AP(din["muk"], 0, [[1, 128], [128, 8]]), slow=True)
        dma("sync", "s_tp", mask_sb[:, :], din["mask0"].ap())
        dma("sync", "s_tp", bv32[:, :], din["bqkv"].ap()[0:1, 2048:3072])
        dma("sync", "s_tp", w1c[:, :], AP(din["w1r"], 0, [[1, 32], [1, 1]]))
        dma("sync", "s_tp", b1c[:, :], AP(din["b1r"], 0, [[1, 32], [1, 1]]))
        dma("sync", "s_tp", w2_sb[:, :], din["w2r"].ap())
        dma("sync", "s_tp", b2c[:, :], AP(din["b2r"], 0, [[1, 16], [1, 1]]))
        dma("sync", "s_tp", gt4_sb[:, :], din["gt4"].ap())
        ld_small = sv["s_tp"]

        w("vector", "s_tp", ld_small)
        op("vector", lambda e: e.tensor_scalar(bias24[:, 0:8], bias24[:, 0:8],
           0.125, None, ALU.mult), ("s_ve", 1))
        op("vector", lambda e: e.tensor_scalar(mur8i[:, :], mur8[:, :],
           -1.0, 1.0, ALU.mult, ALU.add), ("s_ve", 1))
        op("vector", lambda e: e.tensor_scalar(muk8i[:, :], muk8[:, :],
           -1.0, 1.0, ALU.mult, ALU.add), ("s_ve", 1))
        op("vector", lambda e: e.tensor_copy(bvrow[:, :], bv32[:, :]),
           ("s_ve", 1))

        with ExitStack() as ctx1:
            E1 = ctx1.enter_context
            wq_sb = E1(nc.sbuf_tensor("wq_sb", [128, 8 * 3072], BF16))

            for cc in range(8):
                dma("gpsimd", "s_ld", wq_sb[:, 3072 * cc:3072 * (cc + 1)],
                    din["wqkv"].ap()[128 * cc:128 * (cc + 1), :])
            ld_wq = sv["s_ld"]
            w("vector", "s_ld", ld_wq)
            for cc in range(8):
                op("vector", lambda e, cc=cc: e.tensor_scalar(
                    wq_sb[:, 3072 * cc:3072 * cc + 1024],
                    wq_sb[:, 3072 * cc:3072 * cc + 1024], 0.125, None,
                    ALU.mult), ("s_ve", 1))
            ve_wq = sv["s_ve"]

            xstg = [E1(nc.sbuf_tensor(f"xstg{i}", [128, 1024], BF16))
                    for i in range(4)]

            with ExitStack() as ctx1a:
                E1a = ctx1a.enter_context
                g_sb = E1a(nc.sbuf_tensor("g_sb", [32, ETW], F32))
                h_sb = E1a(nc.sbuf_tensor("h_sb", [32, ETW], F32))
                etab_sb = E1a(nc.sbuf_tensor("etab_sb", [16, ETW], BF16))
                dma("sync", "s_tp", g_sb[:, :], din["gsl"].ap())
                ld_g = sv["s_tp"]
                w("vector", "s_tp", ld_g)
                op("vector", lambda e, g_sb=g_sb, h_sb=h_sb: e.tensor_scalar(
                    h_sb[:, :], g_sb[:, :], w1c[:, 0:1], None, ALU.mult),
                    ("s_ve", 1))
                w("scalar", "s_ve", sv["s_ve"])
                op("scalar", lambda e, h_sb=h_sb: e.activation(
                    h_sb[:, :], h_sb[:, :], AF.Relu, bias=b1c[:, 0:1],
                    scale=1.0), ("s_ae", 1))
                w("tensor", "s_ae", sv["s_ae"])
                w("tensor", "s_tp", ld_small)
                nch = (ETW + 511) // 512
                for chi in range(nch):
                    c0 = 512 * chi
                    cw = min(512, ETW - c0)
                    bank = psD if chi % 2 == 0 else psE
                    if chi >= 2:
                        w("tensor", "s_ae", sv["s_ae"] - 1)
                    op("tensor", lambda e, c0=c0, cw=cw, h_sb=h_sb, bank=bank:
                       e.matmul(bank[0:16, 0:cw], w2_sb[:, :],
                                h_sb[:, c0:c0 + cw], start=True, stop=True),
                       ("s_pe", 1))
                    w("scalar", "s_pe", sv["s_pe"])
                    op("scalar", lambda e, c0=c0, cw=cw, g_sb=g_sb, bank=bank:
                       e.activation(g_sb[0:16, c0:c0 + cw], bank[0:16, 0:cw],
                                    AF.Exp, bias=b2c[:, 0:1], scale=1.0),
                       ("s_ae", 1))
                w("vector", "s_ae", sv["s_ae"])
                op("vector", lambda e, g_sb=g_sb, etab_sb=etab_sb:
                   e.tensor_copy(etab_sb[:, :], g_sb[0:16, :]), ("s_ve", 1))
                w("vector", "s_ve")
                op("vector", lambda e, etab_sb=etab_sb: e.tensor_copy(
                    etab_sb[0:1, 0:1], etab_sb[0:1, 0:1]), ("s_ve", 1))
                ve_etab = sv["s_ve"]
                if DEBUG_TAPS >= 1:
                    tap_h = nc.dram_tensor("tap_h", [32, ETW], F32,
                                           kind="ExternalOutput").ap()
                if DEBUG_TAPS in (2, 3):
                    w("sync", "s_ae", sv["s_ae"])
                    dma("sync", "s_y", tap_h, h_sb[:, :])

            xo32 = E1(nc.sbuf_tensor("xo32", [128, 1024], F32))
            xlo32 = E1(nc.sbuf_tensor("xlo32", [128, 1024], F32))
            xhib = [E1(nc.sbuf_tensor(f"xhib{i}", [128, 1024], BF16))
                    for i in range(2)]
            xlob = [E1(nc.sbuf_tensor(f"xlob{i}", [128, 1024], BF16))
                    for i in range(2)]
            xT = [[E1(nc.sbuf_tensor(f"xT{b_}_{cc}", [128, 2048], BF16))
                   for cc in range(8)] for b_ in range(2)]
            ktile = [E1(nc.sbuf_tensor(f"ktile{i}", [128, 512], BF16))
                     for i in range(2)]
            vtile = [E1(nc.sbuf_tensor(f"vtile{i}", [128, 1024], BF16))
                     for i in range(2)]
            qstg = [E1(nc.sbuf_tensor(f"qstg{i}", [128, 512], BF16))
                    for i in range(2)]

            xb_st = []
            xb_ld = []
            for t_ in range(64):
                if t_ >= 4:
                    w("gpsimd", "s_xb", xb_st[t_ - 4])
                dma("gpsimd", "s_xa", xstg[t_ % 4][:, :],
                    din["xf"].ap()[128 * t_:128 * (t_ + 1), :])
                xb_ld.append(sv["s_xa"])
                if t_ >= 3:
                    tt = t_ - 3
                    w("sync", "s_xa", xb_ld[tt])
                    dma("sync", "s_xb",
                        xbf_scr.ap()[128 * tt:128 * (tt + 1), :],
                        xstg[tt % 4][:, :])
                    xb_st.append(sv["s_xb"])
            for tt in range(61, 64):
                w("sync", "s_xa", xb_ld[tt])
                dma("sync", "s_xb",
                    xbf_scr.ap()[128 * tt:128 * (tt + 1), :],
                    xstg[tt % 4][:, :])
                xb_st.append(sv["s_xb"])
            xb_mark = {tg: ("s_xb", xb_st[16 * (tg + 1) - 1])
                       for tg in range(4)}

            w("scalar", "s_ve", ve_etab)
            dma("scalar", "s_sc", etab_scr.ap(), etab_sb[:, :])
            sc_etab = sv["s_sc"]
            w("scalar", "s_sc", sc_etab)
            for p_ in range(128):
                dma("scalar", "s_rep", rep_scr.ap()[p_:p_ + 1, :],
                    AP(etab_scr, 127 - p_, [[ETW, H], [1, BUFW]]))
            rep_done = sv["s_rep"]

            kevK = []
            kdma = []
            vevV = []
            vdma = []
            ki = 0
            vi = 0
            pe_end_tg = []

            def emit_xt(tg):
                tb = tg % 2
                w("sync", xb_mark[tg][0], xb_mark[tg][1])
                if tg < 2:
                    w("sync", "s_sc", sc_etab)
                if tg >= 2:
                    w("sync", "s_pe", pe_end_tg[tg - 2])
                for cc in range(8):
                    dma("sync", "s_tp", xT[tb][cc][:, :],
                        xbf_scr.ap()[2048 * tg:2048 * (tg + 1),
                                     128 * cc:128 * (cc + 1)], transpose=True)
                return sv["s_tp"]

            tp_list = [emit_xt(0), emit_xt(1)]

            xo_done = []
            w("sync", "s_sc", sc_etab)
            for t_ in range(17):
                b_ = t_ % 2
                if t_ >= 1:
                    w("sync", "s_ve", xo_done[t_ - 1][0])
                dma("sync", "s_tp", xo32[:, :],
                    din["xo"].ap()[128 * t_:128 * (t_ + 1), :])
                tp_t = sv["s_tp"]
                if t_ == 0:
                    w("gpsimd", "s_sc", sc_etab)
                if t_ >= 1:
                    w("gpsimd", "s_ve", xo_done[t_ - 1][0])
                if t_ >= 2:
                    w("gpsimd", "s_xo", xo_done[t_ - 2][1])
                dma("gpsimd", "s_xa", xhib[b_][:, :],
                    din["xo"].ap()[128 * t_:128 * (t_ + 1), :])
                ld_t = sv["s_xa"]
                w("vector", "s_tp", tp_t)
                w("vector", "s_xa", ld_t)
                if t_ >= 2:
                    w("vector", "s_xo", xo_done[t_ - 2][1])
                op("vector", lambda e, b_=b_: e.tensor_sub(
                    xlo32[:, :], xo32[:, :], xhib[b_][:, :]), ("s_ve", 1))
                w("vector", "s_ve")
                op("vector", lambda e, b_=b_: e.tensor_copy(
                    xlob[b_][:, :], xlo32[:, :]), ("s_ve", 1))
                B.vfence()
                ve_t = sv["s_ve"]
                w("sync", "s_xa", ld_t)
                dma("sync", "s_xo",
                    xhi_scr.ap()[128 * t_:128 * (t_ + 1), :], xhib[b_][:, :])
                w("sync", "s_ve", ve_t)
                dma("sync", "s_xo",
                    xlo_scr.ap()[128 * t_:128 * (t_ + 1), :], xlob[b_][:, :])
                xo_done.append((ve_t, sv["s_xo"]))
            xo_all = sv["s_xo"]

            for tg in range(4):
                tb = tg % 2
                tp_xt = tp_list[tg]
                w("tensor", "s_tp", tp_xt)
                if tg == 0:
                    w("tensor", "s_ve", ve_wq)
                    w("tensor", "s_ld", ld_wq)
                if vevV:
                    w("tensor", "s_ve", vevV[-1])
                for nb in range(4):
                    for mt in range(8):
                        hb = (ki % 2)
                        bank = psA[:, 1024 + 512 * hb:1536 + 512 * hb]
                        if ki >= 2:
                            w("tensor", "s_ve", kevK[ki - 2])
                        for cc in range(8):
                            op("tensor", lambda e, bank=bank, cc=cc, mt=mt,
                               nb=nb, tb=tb: e.matmul(
                                   bank,
                                   wq_sb[:, 3072 * cc + 1024 + 128 * mt:
                                         3072 * cc + 1024 + 128 * (mt + 1)],
                                   xT[tb][cc][:, 512 * nb:512 * (nb + 1)],
                                   start=(cc == 0), stop=(cc == 7)),
                               *([("s_pe", 1)] if cc == 7 else []))
                        pe_k = sv["s_pe"]
                        kb = ktile[ki % 2]
                        w("vector", "s_pe", pe_k)
                        if ki >= 2:
                            w("vector", "s_sc", kdma[ki - 2])
                        op("vector", lambda e, kb=kb, bank=bank, mt=mt:
                           e.tensor_scalar(kb[:, :], bank,
                                           bias24[:, 8 + mt:9 + mt],
                                           None, ALU.add), ("s_ve", 1))
                        if ki >= 1:
                            w("vector", "s_pe", sv["s_pe"])
                        w("vector", "s_ve")
                        op("vector", lambda e, kb=kb: e.tensor_mul(
                            sqt[:, :], kb[:, :], kb[:, :]), ("s_ve", 1))
                        kevK.append(sv["s_ve"])
                        B.vfence()
                        w("tensor", "s_ve", sv["s_ve"])
                        op("tensor", lambda e, mt=mt: e.matmul(
                            psD[0:1, 0:512], onesl[:, 0:1], sqt[:, :],
                            start=(mt == 0), stop=(mt == 7)), ("s_pe", 1))
                        w("gpsimd", "s_ve")
                        dma("gpsimd", "s_sc",
                            k_scr.ap()[128 * mt:128 * (mt + 1),
                                       2048 * tg + 512 * nb:
                                       2048 * tg + 512 * (nb + 1)], kb[:, :])
                        kdma.append(sv["s_sc"])
                        ki += 1
                    pe_kn = sv["s_pe"]
                    w("vector", "s_pe", pe_kn)
                    op("vector", lambda e: e.reduce_max(
                        t11[0:1, 0:1], psD[0:1, 0:512], axis=AX.X), ("s_ve", 1))
                    w("vector", "s_ve")
                    op("vector", lambda e: e.tensor_max(
                        kmax2[:, :], kmax2[:, :], t11[0:1, 0:1]), ("s_ve", 1))
                    w("tensor", "s_ve", sv["s_ve"])
                for st in range(16):
                    vb = vtile[vi % 2]
                    vbase = 1024 * (vi % 2)
                    if vi == 1:
                        w("tensor", "s_ve", kevK[-1])
                    if vi >= 2:
                        w("tensor", "s_ve", vevV[vi - 2])
                    for half in range(2):
                        op("tensor", lambda e, half=half, vbase=vbase:
                           e.matmul(
                               psA[:, vbase + 512 * half:
                                   vbase + 512 * (half + 1)],
                               onesPb[0:1, 0:128],
                               bvrow[0:1, 512 * half:512 * (half + 1)],
                               start=True, stop=True))
                    for cc in range(8):
                        for half in range(2):
                            op("tensor", lambda e, cc=cc, st=st, half=half,
                               tb=tb, vbase=vbase: e.matmul(
                                   psA[:, vbase + 512 * half:
                                       vbase + 512 * (half + 1)],
                                   xT[tb][cc][:, 128 * st:128 * (st + 1)],
                                   wq_sb[:, 3072 * cc + 2048 + 512 * half:
                                         3072 * cc + 2048 + 512 * (half + 1)],
                                   start=False, stop=(cc == 7)),
                               *([("s_pe", 1)] if cc == 7 and half == 1
                                 else []))
                    pe_v = sv["s_pe"]
                    w("vector", "s_pe", pe_v)
                    if vi >= 2:
                        w("vector", "s_sc", vdma[vi - 2])
                    op("vector", lambda e, vb=vb, vbase=vbase: e.tensor_copy(
                        vb[:, :], psA[:, vbase:vbase + 1024]), ("s_ve", 1))
                    vevV.append(sv["s_ve"])
                    B.vfence()
                    w("gpsimd", "s_ve")
                    dma("gpsimd", "s_sc",
                        v_scr.ap()[2048 * tg + 128 * st:
                                   2048 * tg + 128 * (st + 1), :], vb[:, :])
                    vdma.append(sv["s_sc"])
                    vi += 1
                pe_end_tg.append(sv["s_pe"])
                if tg + 2 < 4:
                    tp_list.append(emit_xt(tg + 2))
            sc_kv = sv["s_sc"]

            xoT = [E1(nc.sbuf_tensor(f"xoT{cc}", [128, QR], BF16))
                   for cc in range(8)]
            w("sync", "s_xo", xo_all)
            w("sync", "s_pe", sv["s_pe"])
            for cc in range(8):
                dma("sync", "s_tp", xoT[cc][:, :],
                    xhi_scr.ap()[:, 128 * cc:128 * (cc + 1)], transpose=True)
            tp_xo = sv["s_tp"]
            w("tensor", "s_tp", tp_xo)
            qev = []
            qdma = []
            qi = 0
            for mt in range(8):
                for nb in range(5):
                    q0, qw = QB0[nb], QBS[nb]
                    hb = qi % 2
                    qs = qstg[qi % 2]
                    if qi >= 2:
                        w("tensor", "s_ve", qev[qi - 2])
                    for cc in range(8):
                        op("tensor", lambda e, hb=hb, cc=cc, mt=mt, q0=q0,
                           qw=qw: e.matmul(
                               psA[:, 1024 + 512 * hb:1024 + 512 * hb + qw],
                               wq_sb[:, 3072 * cc + 128 * mt:
                                     3072 * cc + 128 * (mt + 1)],
                               xoT[cc][:, q0:q0 + qw],
                               start=(cc == 0), stop=(cc == 7)),
                           *([("s_pe", 1)] if cc == 7 else []))
                    pe_q = sv["s_pe"]
                    w("vector", "s_pe", pe_q)
                    if qi >= 2:
                        w("vector", "s_sc", qdma[qi - 2])
                    op("vector", lambda e, hb=hb, qs=qs, mt=mt, qw=qw:
                       e.tensor_scalar(
                           qs[:, 0:qw],
                           psA[:, 1024 + 512 * hb:1024 + 512 * hb + qw],
                           bias24[:, mt:mt + 1], None, ALU.add), ("s_ve", 1))
                    B.vfence()
                    ve_qe = sv["s_ve"]
                    w("gpsimd", "s_ve", ve_qe)
                    dma("gpsimd", "s_sc",
                        q_scr.ap()[128 * mt:128 * (mt + 1), q0:q0 + qw],
                        qs[:, 0:qw])
                    qdma.append(sv["s_sc"])
                    w("vector", "s_pe", sv["s_pe"])
                    w("vector", "s_ve")
                    op("vector", lambda e, qs=qs, qw=qw: e.tensor_mul(
                        sqt[:, 0:qw], qs[:, 0:qw], qs[:, 0:qw]), ("s_ve", 1))
                    qev.append(sv["s_ve"])
                    w("tensor", "s_ve", sv["s_ve"])
                    op("tensor", lambda e, qw=qw: e.matmul(
                        psD[0:1, 0:qw], onesl[0:64, 0:1], sqt[0:64, 0:qw],
                        start=True, stop=True), ("s_pe", 1))
                    op("tensor", lambda e, qw=qw: e.matmul(
                        psE[0:1, 0:qw], onesl[64:128, 0:1], sqt[64:128, 0:qw],
                        start=True, stop=True), ("s_pe", 1))
                    pe_qn = sv["s_pe"]
                    w("vector", "s_pe", pe_qn)
                    op("vector", lambda e, qw=qw: e.reduce_max(
                        t11[0:1, 0:1], psD[0:1, 0:qw], axis=AX.X), ("s_ve", 1))
                    op("vector", lambda e, qw=qw: e.reduce_max(
                        t11[0:1, 1:2], psE[0:1, 0:qw], axis=AX.X), ("s_ve", 1))
                    w("vector", "s_ve")
                    op("vector", lambda e, mt=mt: e.tensor_max(
                        qmax2[0:1, 2 * mt:2 * mt + 1],
                        qmax2[0:1, 2 * mt:2 * mt + 1], t11[0:1, 0:1]),
                        ("s_ve", 1))
                    op("vector", lambda e, mt=mt: e.tensor_max(
                        qmax2[0:1, 2 * mt + 1:2 * mt + 2],
                        qmax2[0:1, 2 * mt + 1:2 * mt + 2], t11[0:1, 1:2]),
                        ("s_ve", 1))
                    qi += 1

            w("scalar", "s_ve", sv["s_ve"])
            op("scalar", lambda e: e.activation(mh_row[:, :], qmax2[:, :],
               AF.Sqrt), ("s_ae", 1))
            op("scalar", lambda e: e.activation(t11[0:1, 0:1], kmax2[:, :],
               AF.Sqrt), ("s_ae", 1))
            w("vector", "s_ae", sv["s_ae"])
            op("vector", lambda e: e.tensor_scalar(mh_row[:, :], mh_row[:, :],
               t11[0:1, 0:1], 0.5, ALU.mult, ALU.add), ("s_ve", 1))
            w("vector", "s_ve")
            op("vector", lambda e: e.reduce_max(
                mhg_row[:, :],
                mh_row[0:1, 0:16].rearrange("p (g h) -> p g h", g=8),
                axis=AX.X), ("s_ve", 1))
            w("vector", "s_ve")
            op("vector", lambda e: e.tensor_scalar(nmhg_row[:, :],
               mhg_row[:, :], -1.0, None, ALU.mult), ("s_ve", 1))
            B.vfence()
            ve_m = sv["s_ve"]
            w("gpsimd", "s_ve", ve_m)
            dma("gpsimd", "s_g", vec_scr.ap()[0:1, 0:16], mh_row[0:1, :])
            dma("gpsimd", "s_g", vec_scr.ap()[1:2, 0:8], mhg_row[0:1, :])
            w("gpsimd", "s_g", sv["s_g"])
            dma("gpsimd", "s_g", mh_col[:, :],
                AP(vec_scr, 0, [[1, 16], [1, 1]]))
            dma("gpsimd", "s_g", mhg_col4[:, :],
                AP(vec_scr, 16, [[1, 8], [1, 1]]))
            w("tensor", "s_g", sv["s_g"])
            op("tensor", lambda e: e.matmul(psD[0:16, 0:1], gt4_sb[:, :],
               mhg_col4[:, :], start=True, stop=True), ("s_pe", 1))
            w("vector", "s_pe", sv["s_pe"])
            op("vector", lambda e: e.tensor_sub(dcol[:, :], psD[0:16, 0:1],
               mh_col[:, :]), ("s_ve", 1))
            w("scalar", "s_ve", sv["s_ve"])
            op("scalar", lambda e: e.activation(corr_col[:, :], dcol[:, :],
               AF.Exp), ("s_ae", 1))
            B.sfence()
            w("gpsimd", "s_ae", sv["s_ae"])
            dma("gpsimd", "s_g", vec_scr.ap()[2:3, 0:16],
                corr_col[:, 0:1])
            w("gpsimd", "s_g", sv["s_g"])
            dma("gpsimd", "s_g", corr_row[0:1, :], vec_scr.ap()[2:3, 0:16])
            g_c = sv["s_g"]
            w("tensor", "s_ve", ve_m)
            for g_ in range(8):
                op("tensor", lambda e, g_=g_: e.matmul(
                    psE[:, g_:g_ + 1], onesP[0:1, 0:128],
                    nmhg_row[0:1, g_:g_ + 1], start=True, stop=True),
                    *([("s_pe", 1)] if g_ == 7 else []))
            w("tensor", "s_g", g_c)
            for hh in range(16):
                op("tensor", lambda e, hh=hh: e.matmul(
                    psD[:, hh:hh + 1], onesP[0:1, 0:128],
                    corr_row[0:1, hh:hh + 1], start=True, stop=True),
                    *([("s_pe", 1)] if hh == 15 else []))
            w("vector", "s_pe", sv["s_pe"])
            op("vector", lambda e: e.tensor_copy(mbg[:, :], psE[:, 0:8]),
               ("s_ve", 1))
            op("vector", lambda e: e.tensor_copy(corrE[:, :], psD[:, 0:16]),
               ("s_ve", 1))
            ve_mset = sv["s_ve"]

            p1_sc_done = sv["s_sc"]
            for wsrc, wscr in ((din["wr"], wT_scr[0]), (din["wk"], wT_scr[1]),
                               (din["wv"], wT_scr[2])):
                for rr in range(8):
                    dma("gpsimd", "s_ld", wtmp[:, :],
                        wsrc.ap()[128 * rr:128 * (rr + 1), :])
                    w("gpsimd", "s_ld", sv["s_ld"])
                    dma("gpsimd", "s_sc",
                        wscr.ap()[128 * rr:128 * (rr + 1), :], wtmp[:, :])
                    w("gpsimd", "s_sc", sv["s_sc"])
            ffn_bounce_done = sv["s_sc"]

        with ExitStack() as ctx2:
            E2 = ctx2.enter_context
            kTp = E2(nc.sbuf_tensor("kTp", [128, 2 * T], BF16))
            qTh = E2(nc.sbuf_tensor("qTh", [128, 2 * QR], BF16))
            vS = E2(nc.sbuf_tensor("vS", [128, 2 * NSC * 128], BF16))
            bufs = E2(nc.sbuf_tensor("bufs", [128, 4 * BUFW], BF16))
            pT = [E2(nc.sbuf_tensor(f"pT{i}", [128, 1024], BF16))
                  for i in range(3)]
            pm = [E2(nc.sbuf_tensor(f"pm{i}", [128, 1024], BF16))
                  for i in range(3)]
            rlb = E2(nc.sbuf_tensor("rlb", [128, 512], F32))
            opv = E2(nc.sbuf_tensor("opv", [128, 512], F32))
            otst = [E2(nc.sbuf_tensor(f"otst{i}", [128, 512], BF16))
                    for i in range(2)]

            git = 0
            ost_list = []
            pend_pv = []
            pair_end = {}

            def flush_one():
                fn, wv = pend_pv.pop(0)
                w("tensor", "s_mul", wv)
                fn()

            def flush_pv(keep=0):
                while len(pend_pv) > keep:
                    flush_one()

            def stream_pair(pr):
                sl = pr % 2
                if pr >= 2:
                    smm_e, pv_e, mul_e = pair_end[pr - 2]
                    w("sync", "s_smm", smm_e)
                    w("sync", "s_pv", pv_e)
                    w("sync", "s_mul", mul_e)
                else:
                    w("sync", "s_sc", p1_sc_done)
                    w("sync", "s_rep", rep_done)
                    w("sync", "s_ve", ve_mset)
                dma("sync", "s_tp", kTp[:, T * sl:T * (sl + 1)],
                    k_scr.ap()[128 * pr:128 * (pr + 1), :])
                dma("sync", "s_tp", qTh[:, QR * sl:QR * (sl + 1)],
                    q_scr.ap()[128 * pr:128 * (pr + 1), :])
                dma("sync", "s_tp",
                    vS[:, 8192 * sl:8192 * (sl + 1)].rearrange(
                        "p (n m) -> p n m", n=NSC),
                    AP(v_scr, 128 * pr, [[1024, 128], [131072, NSC],
                                         [1, 128]]))
                for h2 in range(2):
                    dma("sync", "s_tp",
                        bufs[:, BUFW * (2 * sl + h2):
                             BUFW * (2 * sl + h2 + 1)],
                        rep_scr.ap()[:, BUFW * (2 * pr + h2):
                                     BUFW * (2 * pr + h2 + 1)])
                return sv["s_tp"]

            tp_pair = {0: stream_pair(0), 1: stream_pair(1)}
            for pr in range(8):
                sl = pr % 2
                tp_hg = tp_pair[pr]
                w("vector", "s_tp", tp_hg)
                if pr == 0:
                    w("vector", "s_ve", ve_mset)
                for h2 in range(2):
                    op("vector", lambda e, h2=h2, sl=sl, pr=pr:
                       e.tensor_scalar(
                           bufs[:, BUFW * (2 * sl + h2):
                                BUFW * (2 * sl + h2 + 1)],
                           bufs[:, BUFW * (2 * sl + h2):
                                BUFW * (2 * sl + h2 + 1)],
                           corrE[:, 2 * pr + h2:2 * pr + h2 + 1], None,
                           ALU.mult))
                for qb in range(5):
                    q0, qw = QB0[qb], QBS[qb]
                    for sc in range(NSC):
                        j = git
                        jb = j % 2
                        if j >= 2:
                            w("tensor", "s_exp", j - 1)
                        if sc == 0:
                            w("tensor", "s_tp", tp_hg)
                            if qb == 0 and pr == 0:
                                w("tensor", "s_ve", ve_mset)
                            if qb > 0 or pr > 0:
                                w("tensor", "s_nrm", sv["s_nrm"])
                        for h2 in range(2):
                            r0 = 64 * h2
                            op("tensor", lambda e, h2=h2, r0=r0, sl=sl,
                               sc=sc, q0=q0, qw=qw, jb=jb: e.matmul(
                                   psA[:, 1024 * jb + 512 * h2:
                                       1024 * jb + 512 * h2 + qw],
                                   kTp[r0:r0 + 64, T * sl + 128 * sc:
                                       T * sl + 128 * (sc + 1)],
                                   qTh[r0:r0 + 64, QR * sl + q0:
                                       QR * sl + q0 + qw],
                                   start=True, stop=True,
                                   tile_position=(r0, 0)),
                               *([("s_smm", 1)] if h2 == 1 else []))
                        flush_pv(keep=1)
                        if j == 0:
                            w("scalar", "s_ve", ve_mset)
                        w("scalar", "s_smm", j + 1)
                        if j >= 3:
                            w("scalar", "s_mul", j - 2)
                        op("scalar", lambda e, j=j, qw=qw, pr=pr, jb=jb:
                           e.activation(
                               pT[j % 3][:, :].rearrange(
                                   "p (h q) -> p h q", h=2)[:, :, 0:qw],
                               psA[:, 1024 * jb:1024 * (jb + 1)].rearrange(
                                   "p (h q) -> p h q", h=2)[:, :, 0:qw],
                               AF.Exp, bias=mbg[:, pr:pr + 1], scale=1.0),
                           ("s_exp", 1))
                        w("vector", "s_exp", j + 1)
                        if j >= 3:
                            w("vector", "s_pv", j - 2)
                        j0 = q0 - 128 * sc + 8064
                        op("vector", lambda e, j=j, j0=j0, qw=qw, sl=sl:
                           e.tensor_mul(
                               pm[j % 3][:, :].rearrange(
                                   "p (h q) -> p h q", h=2)[:, :, 0:qw],
                               bufs[:, :].rearrange(
                                   "p (h q) -> p h q",
                                   h=4)[:, 2 * sl:2 * sl + 2, j0:j0 + qw],
                               pT[j % 3][:, :].rearrange(
                                   "p (h q) -> p h q", h=2)[:, :, 0:qw]),
                           ("s_mul", 1))
                        mul_target = sv["s_mul"]

                        def mk_pv(sc=sc, qw=qw, g=git, sl=sl):
                            def go():
                                for h2 in range(2):
                                    op("tensor", lambda e, h2=h2, sc=sc,
                                       qw=qw, g=g, sl=sl: e.matmul(
                                           psB[64 * h2:64 * (h2 + 1), 0:qw],
                                           vS[:, 8192 * sl + 128 * sc +
                                              64 * h2:8192 * sl + 128 * sc +
                                              64 * (h2 + 1)],
                                           pm[g % 3][:, 512 * h2:
                                                     512 * h2 + qw],
                                           start=(sc == 0),
                                           stop=(sc == NSC - 1),
                                           tile_position=(0, 64 * h2)))
                                for h2 in range(2):
                                    op("tensor", lambda e, h2=h2, sc=sc,
                                       qw=qw, g=g: e.matmul(
                                           psD[32 * h2:32 * h2 + 1, 0:qw],
                                           onesl[:, 0:1],
                                           pm[g % 3][:, 512 * h2:
                                                     512 * h2 + qw],
                                           start=(sc == 0),
                                           stop=(sc == NSC - 1),
                                           tile_position=(0, 32 * h2)),
                                       *([("s_pv", 1)] if h2 == 1 else []))
                            return go
                        pend_pv.append((mk_pv(), mul_target))
                        git += 1
                    flush_pv()
                    w("vector", "s_pv", sv["s_pv"])
                    w("vector", "s_bc", sv["s_bc"])
                    op("vector", lambda e, qw=qw: e.reciprocal(
                        rlb[:, 0:qw], psD[:, 0:qw]), ("s_nrm", 1))
                    op("vector", lambda e: e.tensor_copy(
                        B.fence_sb[0:1, 0:1], B.fence_sb[0:1, 1:2]),
                        ("s_nrm", 1))
                    w("tensor", "s_nrm", sv["s_nrm"])
                    for h2 in range(2):
                        trow = 32 * h2
                        op("tensor", lambda e, trow=trow, h2=h2, qw=qw:
                           e.matmul(
                               psA[64 * h2:64 * (h2 + 1), 0:qw],
                               onesP[trow:trow + 1, 0:64],
                               rlb[trow:trow + 1, 0:qw],
                               start=True, stop=True,
                               tile_position=(trow, 64 * h2)),
                           *([("s_bc", 1)] if h2 == 1 else []))
                    ot = otst[len(ost_list) % 2]
                    w("vector", "s_bc", sv["s_bc"])
                    oi = len(ost_list)
                    if oi >= 2:
                        w("vector", "s_ost", ost_list[oi - 2])
                    op("vector", lambda e, qw=qw: e.tensor_copy(
                        opv[:, 0:qw], psB[:, 0:qw]), ("s_nrm", 1))
                    w("vector", "s_nrm")
                    op("vector", lambda e, ot=ot, qw=qw: e.tensor_mul(
                        ot[:, 0:qw], opv[:, 0:qw], psA[:, 0:qw]),
                        ("s_nrm", 1))
                    op("vector", lambda e: e.tensor_copy(
                        B.fence_sb[0:1, 0:1], B.fence_sb[0:1, 1:2]),
                        ("s_nrm", 1))
                    w("sync", "s_nrm", sv["s_nrm"])
                    dma("sync", "s_ost",
                        o_scr.ap()[128 * pr:128 * (pr + 1), q0:q0 + qw],
                        ot[:, 0:qw])
                    ost_list.append(sv["s_ost"])
                pair_end[pr] = (sv["s_smm"], sv["s_pv"], sv["s_mul"])
                if pr + 2 < 8:
                    tp_pair[pr + 2] = stream_pair(pr + 2)
            last_ost = sv["s_ost"]

        x1hi = E(nc.sbuf_tensor("x1hi", [128, 8 * QR], BF16))
        x1lo = E(nc.sbuf_tensor("x1lo", [128, 8 * QR], BF16))
        stat = E(nc.sbuf_tensor("stat", [1, 2048], F32))
        sq32 = E(nc.sbuf_tensor("sq32", [128, 512], F32))
        x1f = E(nc.sbuf_tensor("x1f", [128, 512], F32))

        def emit_ln(slabs, qw, sink):
            w("tensor", "s_ve")
            w("tensor", "s_nrm")
            for cc in range(8):
                th, c0 = slabs[cc]
                op("tensor", lambda e, th=th, c0=c0, cc=cc, qw=qw: e.matmul(
                    psD[0:1, 0:qw], onesc[:, 0:1], th[:, c0:c0 + qw],
                    start=(cc == 0), stop=(cc == 7)),
                    *([("s_pe", 1)] if cc == 7 else []))
            for cc in range(8):
                th, c0 = slabs[cc]
                if cc >= 1:
                    w("vector", "s_pe", sv["s_pe"])
                op("vector", lambda e, th=th, c0=c0, qw=qw: e.tensor_mul(
                    sq32[:, 0:qw], th[:, c0:c0 + qw], th[:, c0:c0 + qw]),
                    ("s_ve", 1))
                w("tensor", "s_ve", sv["s_ve"])
                op("tensor", lambda e, cc=cc, qw=qw: e.matmul(
                    psE[0:1, 0:qw], onesc[:, 0:1], sq32[:, 0:qw],
                    start=(cc == 0), stop=(cc == 7)), ("s_pe", 1))
            w("vector", "s_pe", sv["s_pe"])
            op("vector", lambda e, qw=qw: e.tensor_scalar(
                stat[0:1, 0:qw], psD[0:1, 0:qw], 1.0 / C, None, ALU.mult),
                ("s_ve", 1))
            op("vector", lambda e, qw=qw: e.tensor_scalar(
                stat[0:1, 512:512 + qw], psE[0:1, 0:qw], 1.0 / C, None,
                ALU.mult), ("s_ve", 1))
            w("vector", "s_ve")
            op("vector", lambda e, qw=qw: e.tensor_mul(
                stat[0:1, 1024:1024 + qw], stat[0:1, 0:qw], stat[0:1, 0:qw]),
                ("s_ve", 1))
            w("vector", "s_ve")
            op("vector", lambda e, qw=qw: e.tensor_sub(
                stat[0:1, 512:512 + qw], stat[0:1, 512:512 + qw],
                stat[0:1, 1024:1024 + qw]), ("s_ve", 1))
            w("vector", "s_ve")
            op("vector", lambda e, qw=qw: e.tensor_scalar(
                stat[0:1, 512:512 + qw], stat[0:1, 512:512 + qw], 1.0, EPS,
                ALU.mult, ALU.add), ("s_ve", 1))
            w("scalar", "s_ve", sv["s_ve"])
            op("scalar", lambda e, qw=qw: e.activation(
                stat[0:1, 1024:1024 + qw], stat[0:1, 512:512 + qw], AF.Ln),
                ("s_ae", 1))
            w("scalar", "s_ae")
            op("scalar", lambda e, qw=qw: e.activation(
                stat[0:1, 1536:1536 + qw], stat[0:1, 1024:1024 + qw], AF.Exp,
                bias=0.0, scale=-0.5), ("s_ae", 1))
            w("tensor", "s_ve", sv["s_ve"])
            w("tensor", "s_ae", sv["s_ae"])
            op("tensor", lambda e, qw=qw: e.matmul(
                psD[:, 0:qw], onesP[0:1, 0:128], stat[0:1, 0:qw],
                start=True, stop=True), ("s_pe", 1))
            op("tensor", lambda e, qw=qw: e.matmul(
                psE[:, 0:qw], onesP[0:1, 0:128], stat[0:1, 1536:1536 + qw],
                start=True, stop=True), ("s_pe", 1))
            w("vector", "s_pe", sv["s_pe"])
            for cc in range(8):
                th, c0 = slabs[cc]
                op("vector", lambda e, th=th, c0=c0, qw=qw: e.tensor_sub(
                    x1f[:, 0:qw], th[:, c0:c0 + qw], psD[:, 0:qw]),
                    ("s_ve", 1))
                w("vector", "s_ve")
                op("vector", lambda e, qw=qw: e.tensor_mul(
                    x1f[:, 0:qw], x1f[:, 0:qw], psE[:, 0:qw]), ("s_ve", 1))
                w("vector", "s_ve")
                sink(cc)

        with ExitStack() as ctx3:
            E3 = ctx3.enter_context
            wo_sb = E3(nc.sbuf_tensor("wo_sb", [128, 8 * C], BF16))
            oT = [E3(nc.sbuf_tensor(f"oT{i}", [128, 8 * 512], BF16))
                  for i in range(2)]
            xhiT = [E3(nc.sbuf_tensor(f"xhiT{i}", [128, 8 * 512], BF16))
                    for i in range(2)]
            xloT = [E3(nc.sbuf_tensor(f"xloT{i}", [128, 8 * 512], BF16))
                    for i in range(2)]
            r32 = E3(nc.sbuf_tensor("r32", [128, 8 * 512], F32))

            w("gpsimd", "s_mul")
            w("gpsimd", "s_pv")
            w("gpsimd", "s_nrm")
            w("gpsimd", "s_smm")
            w("gpsimd", "s_exp")
            for cc in range(8):
                dma("gpsimd", "s_ld", wo_sb[:, C * cc:C * (cc + 1)],
                    din["wout"].ap()[128 * cc:128 * (cc + 1), :])
            ld_wo = sv["s_ld"]

            tb_pe = []
            tb_tp = []
            tb_vedone = [0] * 5

            def p3_loads(tb):
                q0, qw = QB0[tb], QBS[tb]
                bb = tb % 2
                if tb == 0:
                    w("sync", "s_ost", last_ost)
                if tb >= 2:
                    w("sync", "s_pe", tb_pe[tb - 2])
                    w("sync", "s_ve", tb_vedone[tb - 2])
                for cc in range(8):
                    dma("sync", "s_tp", oT[bb][:, 512 * cc:512 * cc + qw],
                        o_scr.ap()[128 * cc:128 * (cc + 1), q0:q0 + qw])
                    dma("sync", "s_tp", xhiT[bb][:, 512 * cc:512 * cc + qw],
                        xhi_scr.ap()[q0:q0 + qw, 128 * cc:128 * (cc + 1)],
                        transpose=True)
                    dma("sync", "s_tp", xloT[bb][:, 512 * cc:512 * cc + qw],
                        xlo_scr.ap()[q0:q0 + qw, 128 * cc:128 * (cc + 1)],
                        transpose=True)
                return sv["s_tp"]

            tb_tp = [p3_loads(0), p3_loads(1)]
            for tb in range(5):
                q0, qw = QB0[tb], QBS[tb]
                bb = tb % 2
                tp_tb = tb_tp[tb]
                w("tensor", "s_tp", tp_tb)
                if tb == 0:
                    w("tensor", "s_ld", ld_wo)
                oev = []
                for mt in range(8):
                    hb = mt % 2
                    if mt >= 2:
                        w("tensor", "s_ve", oev[mt - 2])
                    for cc in range(8):
                        op("tensor", lambda e, hb=hb, cc=cc, mt=mt, qw=qw,
                           bb=bb: e.matmul(
                               psA[:, 1024 + 512 * hb:1024 + 512 * hb + qw],
                               wo_sb[:, C * cc + 128 * mt:
                                     C * cc + 128 * (mt + 1)],
                               oT[bb][:, 512 * cc:512 * cc + qw],
                               start=(cc == 0), stop=(cc == 7)),
                           *([("s_pe", 1)] if cc == 7 else []))
                    w("vector", "s_pe", sv["s_pe"])
                    op("vector", lambda e, hb=hb, mt=mt, qw=qw:
                       e.tensor_scalar(
                           r32[:, 512 * mt:512 * mt + qw],
                           psA[:, 1024 + 512 * hb:1024 + 512 * hb + qw],
                           bo8[:, mt:mt + 1], None, ALU.add), ("s_ve", 1))
                    w("vector", "s_ve")
                    op("vector", lambda e, mt=mt, qw=qw, bb=bb: e.tensor_add(
                        r32[:, 512 * mt:512 * mt + qw],
                        r32[:, 512 * mt:512 * mt + qw],
                        xhiT[bb][:, 512 * mt:512 * mt + qw]), ("s_ve", 1))
                    w("vector", "s_ve")
                    op("vector", lambda e, mt=mt, qw=qw, bb=bb: e.tensor_add(
                        r32[:, 512 * mt:512 * mt + qw],
                        r32[:, 512 * mt:512 * mt + qw],
                        xloT[bb][:, 512 * mt:512 * mt + qw]), ("s_ve", 1))
                    oev.append(sv["s_ve"])
                def sink_x1(cc, q0=q0, qw=qw):
                    op("vector", lambda e, cc=cc, qw=qw, q0=q0: e.tensor_copy(
                        x1hi[:, QR * cc + q0:QR * cc + q0 + qw],
                        x1f[:, 0:qw]), ("s_ve", 1))
                    w("vector", "s_ve")
                    op("vector", lambda e, cc=cc, qw=qw, q0=q0: e.tensor_sub(
                        sq32[:, 0:qw], x1f[:, 0:qw],
                        x1hi[:, QR * cc + q0:QR * cc + q0 + qw]), ("s_ve", 1))
                    w("vector", "s_ve")
                    op("vector", lambda e, cc=cc, qw=qw, q0=q0: e.tensor_copy(
                        x1lo[:, QR * cc + q0:QR * cc + q0 + qw],
                        sq32[:, 0:qw]), ("s_ve", 1))
                emit_ln([(r32, 512 * cc) for cc in range(8)], qw, sink_x1)
                tb_pe.append(sv["s_pe"])
                tb_vedone[tb] = sv["s_ve"]
                if tb + 2 < 5:
                    tb_tp.append(p3_loads(tb + 2))
            for cc in range(8):
                for t_ in (x1hi, x1lo):
                    op("vector", lambda e, t_=t_, cc=cc: e.tensor_scalar(
                        t_[:, QR * cc + 127:QR * cc + 128],
                        t_[:, QR * cc + 127:QR * cc + 128],
                        mask_sb[:, 0:1], None, ALU.mult), ("s_ve", 1))
            ve_x1m = sv["s_ve"]
            p3_pe = sv["s_pe"]

        with ExitStack() as ctx4:
            E4 = ctx4.enter_context
            wrT = E4(nc.sbuf_tensor("wrT", [128, 8 * C], BF16))
            wkT = E4(nc.sbuf_tensor("wkT", [128, 8 * C], BF16))
            wvT = E4(nc.sbuf_tensor("wvT", [128, 8 * C], BF16))
            xr_sb = E4(nc.sbuf_tensor("xr_sb", [128, 8 * 512], BF16))
            xk_sb = E4(nc.sbuf_tensor("xk_sb", [128, 8 * 512], BF16))
            rb_sb = E4(nc.sbuf_tensor("rb_sb", [128, 8 * 512], BF16))
            ka_sb = E4(nc.sbuf_tensor("ka_sb", [128, 8 * 512], BF16))
            km32 = E4(nc.sbuf_tensor("km32", [128, 512], F32))
            ff32 = E4(nc.sbuf_tensor("ff32", [128, 8 * 512], F32))
            x2hi = [E4(nc.sbuf_tensor(f"x2hi{i}", [128, 512], BF16))
                    for i in range(2)]
            x2lo = [E4(nc.sbuf_tensor(f"x2lo{i}", [128, 512], BF16))
                    for i in range(2)]

            w("sync", "s_pe", p3_pe)
            w("sync", "s_ve", sv["s_ve"])
            w("sync", "s_sc", ffn_bounce_done)
            for wscr, wsb in ((wT_scr[0], wrT), (wT_scr[1], wkT),
                              (wT_scr[2], wvT)):
                for cc in range(8):
                    dma("sync", "s_tp", wsb[:, C * cc:C * (cc + 1)],
                        wscr.ap()[:, 128 * cc:128 * (cc + 1)], transpose=True)
            tp_w = sv["s_tp"]

            x2_done = []
            x2i = 0
            for fb in range(4):
                f0 = QEXT + 512 * fb
                fw = 512
                if fb == 0:
                    w("vector", "s_ve", ve_x1m)
                for cc in range(8):
                    for (dst, mu, mui) in ((xr_sb, mur8, mur8i),
                                           (xk_sb, muk8, muk8i)):
                        op("vector", lambda e, dst=dst, mu=mu, cc=cc, f0=f0,
                           fw=fw: e.tensor_scalar(
                               dst[:, 512 * cc:512 * cc + fw],
                               x1hi[:, QR * cc + f0:QR * cc + f0 + fw],
                               mu[:, cc:cc + 1], None, ALU.mult), ("s_ve", 1))
                        w("vector", "s_ve")
                        op("vector", lambda e, dst=dst, mui=mui, cc=cc, f0=f0,
                           fw=fw: e.scalar_tensor_tensor(
                               dst[:, 512 * cc:512 * cc + fw],
                               x1hi[:, QR * cc + f0 - 1:
                                    QR * cc + f0 - 1 + fw],
                               mui[:, cc:cc + 1],
                               dst[:, 512 * cc:512 * cc + fw],
                               ALU.mult, ALU.add), ("s_ve", 1))
                ve_bl = sv["s_ve"]
                w("tensor", "s_ve", ve_bl)
                if fb == 0:
                    w("tensor", "s_tp", tp_w)
                sig_ae = []
                for mt in range(8):
                    hb = mt % 2
                    if mt >= 2:
                        w("tensor", "s_ae", sig_ae[mt - 2])
                    for cc in range(8):
                        op("tensor", lambda e, hb=hb, cc=cc, mt=mt, fw=fw:
                           e.matmul(
                               psA[:, 1024 + 512 * hb:1024 + 512 * hb + fw],
                               wrT[:, C * cc + 128 * mt:
                                   C * cc + 128 * (mt + 1)],
                               xr_sb[:, 512 * cc:512 * cc + fw],
                               start=(cc == 0), stop=(cc == 7)),
                           *([("s_pe", 1)] if cc == 7 else []))
                    w("scalar", "s_pe", sv["s_pe"])
                    op("scalar", lambda e, hb=hb, mt=mt, fw=fw: e.activation(
                        rb_sb[:, 512 * mt:512 * mt + fw],
                        psA[:, 1024 + 512 * hb:1024 + 512 * hb + fw],
                        AF.Sigmoid), ("s_ae", 1))
                    sig_ae.append(sv["s_ae"])
                kp_ve = []
                for mt in range(8):
                    hb = mt % 2
                    if mt == 0:
                        w("tensor", "s_ae", sig_ae[-1])
                    if mt >= 2:
                        w("tensor", "s_ve", kp_ve[mt - 2])
                    for cc in range(8):
                        op("tensor", lambda e, hb=hb, cc=cc, mt=mt, fw=fw:
                           e.matmul(
                               psA[:, 1024 + 512 * hb:1024 + 512 * hb + fw],
                               wkT[:, C * cc + 128 * mt:
                                   C * cc + 128 * (mt + 1)],
                               xk_sb[:, 512 * cc:512 * cc + fw],
                               start=(cc == 0), stop=(cc == 7)),
                           *([("s_pe", 1)] if cc == 7 else []))
                    w("vector", "s_pe", sv["s_pe"])
                    op("vector", lambda e, hb=hb, fw=fw: e.tensor_scalar(
                        km32[:, 0:fw],
                        psA[:, 1024 + 512 * hb:1024 + 512 * hb + fw],
                        0.0, None, ALU.max), ("s_ve", 1))
                    w("vector", "s_ve")
                    op("vector", lambda e, mt=mt, fw=fw: e.tensor_mul(
                        ka_sb[:, 512 * mt:512 * mt + fw], km32[:, 0:fw],
                        km32[:, 0:fw]), ("s_ve", 1))
                    kp_ve.append(sv["s_ve"])
                fv_ve = []
                for mt in range(8):
                    hb = mt % 2
                    if mt == 0:
                        w("tensor", "s_ve", kp_ve[-1])
                    if mt >= 2:
                        w("tensor", "s_ve", fv_ve[mt - 2])
                    for cc in range(8):
                        op("tensor", lambda e, hb=hb, cc=cc, mt=mt, fw=fw:
                           e.matmul(
                               psA[:, 1024 + 512 * hb:1024 + 512 * hb + fw],
                               wvT[:, C * cc + 128 * mt:
                                   C * cc + 128 * (mt + 1)],
                               ka_sb[:, 512 * cc:512 * cc + fw],
                               start=(cc == 0), stop=(cc == 7)),
                           *([("s_pe", 1)] if cc == 7 else []))
                    w("vector", "s_pe", sv["s_pe"])
                    op("vector", lambda e, hb=hb, mt=mt, fw=fw: e.tensor_mul(
                        ff32[:, 512 * mt:512 * mt + fw],
                        rb_sb[:, 512 * mt:512 * mt + fw],
                        psA[:, 1024 + 512 * hb:1024 + 512 * hb + fw]),
                        ("s_ve", 1))
                    w("vector", "s_ve")
                    op("vector", lambda e, mt=mt, f0=f0, fw=fw: e.tensor_add(
                        ff32[:, 512 * mt:512 * mt + fw],
                        ff32[:, 512 * mt:512 * mt + fw],
                        x1hi[:, QR * mt + f0:QR * mt + f0 + fw]), ("s_ve", 1))
                    w("vector", "s_ve")
                    op("vector", lambda e, mt=mt, f0=f0, fw=fw: e.tensor_add(
                        ff32[:, 512 * mt:512 * mt + fw],
                        ff32[:, 512 * mt:512 * mt + fw],
                        x1lo[:, QR * mt + f0:QR * mt + f0 + fw]), ("s_ve", 1))
                    fv_ve.append(sv["s_ve"])
                def sink_x2(cc, fb=fb, fw=fw):
                    bi = (fb * 8 + cc) % 2
                    if fb * 8 + cc >= 2:
                        w("vector", "s_sc", x2_done[fb * 8 + cc - 2])
                    op("vector", lambda e, bi=bi, fw=fw: e.tensor_copy(
                        x2hi[bi][:, 0:fw], x1f[:, 0:fw]), ("s_ve", 1))
                    w("vector", "s_ve")
                    op("vector", lambda e, bi=bi, fw=fw: e.tensor_sub(
                        sq32[:, 0:fw], x1f[:, 0:fw], x2hi[bi][:, 0:fw]),
                        ("s_ve", 1))
                    w("vector", "s_ve")
                    op("vector", lambda e, bi=bi, fw=fw: e.tensor_copy(
                        x2lo[bi][:, 0:fw], sq32[:, 0:fw]), ("s_ve", 1))
                    B.vfence()
                    w("gpsimd", "s_ve", sv["s_ve"])
                    dma("gpsimd", "s_sc",
                        x2h_scr.ap()[128 * cc:128 * (cc + 1),
                                     512 * fb:512 * (fb + 1)],
                        x2hi[bi][:, 0:fw])
                    dma("gpsimd", "s_sc",
                        x2l_scr.ap()[128 * cc:128 * (cc + 1),
                                     512 * fb:512 * (fb + 1)],
                        x2lo[bi][:, 0:fw])
                    x2_done.append(sv["s_sc"])
                emit_ln([(ff32, 512 * cc) for cc in range(8)], fw, sink_x2)
            sc_x2 = sv["s_sc"]

        yh = [E(nc.sbuf_tensor(f"yh{i}", [128, C], BF16)) for i in range(2)]
        yl = [E(nc.sbuf_tensor(f"yl{i}", [128, C], BF16)) for i in range(2)]
        y32 = [E(nc.sbuf_tensor(f"y32{i}", [128, C], F32)) for i in range(2)]
        w("sync", "s_sc", sc_x2)
        ydma = []
        for ob in range(16):
            b_ = ob % 2
            if ob >= 2:
                w("sync", "s_y", ydma[ob - 2])
            dma("sync", "s_tp", yh[b_][:, :],
                x2h_scr.ap()[:, 128 * ob:128 * (ob + 1)], transpose=True)
            dma("sync", "s_tp", yl[b_][:, :],
                x2l_scr.ap()[:, 128 * ob:128 * (ob + 1)], transpose=True)
            tp_y = sv["s_tp"]
            w("vector", "s_tp", tp_y)
            if ob >= 2:
                w("vector", "s_y", ydma[ob - 2])
            op("vector", lambda e, b_=b_: e.tensor_add(
                y32[b_][:, :], yh[b_][:, :], yl[b_][:, :]), ("s_ve", 1))
            B.vfence()
            w("sync", "s_ve", sv["s_ve"])
            dma("sync", "s_y", y[128 * ob:128 * (ob + 1), :], y32[b_][:, :])
            ydma.append(sv["s_y"])
        if DEBUG_TAPS >= 1:
            tap = {}
            for nm, shape, dt in (
                    ("mbg", [128, 8], F32), ("corrE", [128, 16], F32),
                    ("stats", [1, 64], F32), ("etab", [16, ETW], BF16),
                    ("q", [128, QR], BF16), ("k", [128, 2048], BF16),
                    ("v", [128, 1024], BF16), ("o", [128, QR], BF16),
                    ("x1", [128, QR], BF16), ("xhi", [128, 1024], BF16),
                    ("rep", [128, 2048], BF16), ("x2h", [128, 2048], BF16),
                    ("x2l", [128, 2048], BF16), ("x1b", [128, QR], BF16),
                    ("x1lo", [128, QR], BF16)):
                tap[nm] = nc.dram_tensor("tap_" + nm, shape, dt,
                                         kind="ExternalOutput").ap()
        if DEBUG_TAPS in (3, 4):
            w("sync", "s_y", sv["s_y"])
            dma("sync", "s_y", tap["mbg"], mbg[:, :])
            dma("sync", "s_y", tap["corrE"], corrE[:, :])
            dma("sync", "s_y", tap["stats"][0:1, 0:16], qmax2[:, :])
            dma("sync", "s_y", tap["stats"][0:1, 16:17], kmax2[:, :])
            dma("sync", "s_y", tap["stats"][0:1, 17:33], mh_row[:, :])
            dma("sync", "s_y", tap["stats"][0:1, 33:41], mhg_row[:, :])
            dma("sync", "s_y", tap["stats"][0:1, 41:57], corr_row[:, :])
            dma("sync", "s_y", tap["etab"], etab_scr.ap())
            dma("sync", "s_y", tap["q"], q_scr.ap()[0:128, :])
            dma("sync", "s_y", tap["k"], k_scr.ap()[0:128, 0:2048])
            dma("sync", "s_y", tap["v"], v_scr.ap()[0:128, :])
            dma("sync", "s_y", tap["o"], o_scr.ap()[0:128, :])
            dma("sync", "s_y", tap["x1"], x1hi[:, 0:QR])
            dma("sync", "s_y", tap["xhi"], xhi_scr.ap()[0:128, :])
            dma("sync", "s_y", tap["rep"], rep_scr.ap()[0:128, 0:2048])
            dma("sync", "s_y", tap["x2h"], x2h_scr.ap()[0:128, :])
            dma("sync", "s_y", tap["x2l"], x2l_scr.ap()[0:128, :])
            dma("sync", "s_y", tap["x1b"], x1hi[:, 3 * QR:4 * QR])
            dma("sync", "s_y", tap["x1lo"], x1lo[:, 0:QR])
        w("sync", "s_y", sv["s_y"])
        w("gpsimd", "s_sc", sv["s_sc"])
        w("gpsimd", "s_g96", sv["s_g96"])
        w("scalar", "s_rep", sv["s_rep"])

        with nc.Block() as block:
            @block.sync
            def _(e):
                for fn in B.prog["sync"]:
                    fn(e)

            @block.tensor
            def _(e):
                for fn in B.prog["tensor"]:
                    fn(e)

            @block.scalar
            def _(e):
                for fn in B.prog["scalar"]:
                    fn(e)

            @block.vector
            def _(e):
                for fn in B.prog["vector"]:
                    fn(e)

            @block.gpsimd
            def _(e):
                for fn in B.prog["gpsimd"]:
                    fn(e)

    return nc


_NC_CACHE = None


def kernel(**inputs):
    global _NC_CACHE
    f32 = np.float32
    L = {k: np.asarray(v) for k, v in inputs.items()}
    x = L["L0_attn_x"]

    def g_of(delta):
        d = delta.astype(np.float64)
        return (np.sign(d) * np.log1p(np.abs(d))).astype(f32)

    in_maps = []
    for c in range(8):
        b, ts = c // 4, c % 4
        t0 = 2048 * ts
        xb = np.ascontiguousarray(x[b]).astype(f32)
        idx = (np.arange(t0 - QEXT, t0 + QOWN) % T)
        xo = np.ascontiguousarray(xb[idx])
        delta0 = t0 - QEXT - (T - 1)
        g = g_of(np.arange(delta0, delta0 + ETW))
        gsl = np.ascontiguousarray(np.broadcast_to(g, (32, ETW)))
        gt4 = np.zeros((8, H), f32)
        for gg in range(8):
            gt4[gg, 2 * gg:2 * gg + 2] = 1.0
        m0 = np.full((128, 1), 0.0 if ts == 0 else 1.0, f32)
        in_maps.append(dict(
            xf=xb, xo=xo,
            wqkv=np.ascontiguousarray(L["L0_attn_w_qkv"], f32),
            bqkv=np.ascontiguousarray(L["L0_attn_b_qkv"], f32).reshape(1, -1),
            wout=np.ascontiguousarray(L["L0_attn_w_out"], f32),
            bout=np.ascontiguousarray(L["L0_attn_b_out"], f32).reshape(1, -1),
            wr=np.ascontiguousarray(L["L0_ffn_W_r"], f32),
            wk=np.ascontiguousarray(L["L0_ffn_W_k"], f32),
            wv=np.ascontiguousarray(L["L0_ffn_W_v"], f32),
            mur=np.ascontiguousarray(L["L0_ffn_mu_r"], f32).reshape(1, -1),
            muk=np.ascontiguousarray(L["L0_ffn_mu_k"], f32).reshape(1, -1),
            w1r=np.ascontiguousarray(L["L0_attn_w1_rpe"], f32),
            b1r=np.ascontiguousarray(L["L0_attn_b1_rpe"], f32).reshape(1, -1),
            w2r=np.ascontiguousarray(L["L0_attn_w2_rpe"], f32),
            b2r=np.ascontiguousarray(L["L0_attn_b2_rpe"], f32).reshape(1, -1),
            gsl=gsl, gt4=gt4, mask0=m0,
        ))

    if _NC_CACHE is None:
        _NC_CACHE = build_nc()
    res = run_bass_kernel_spmd(_NC_CACHE, in_maps, core_ids=list(range(8)))
    out = np.zeros((2, T, C), f32)
    for c in range(8):
        b, ts = c // 4, c % 4
        out[b, 2048 * ts:2048 * (ts + 1), :] = res.results[c]["y"]
    return out
